# version 02
# baseline (speedup 1.0000x reference)
import numpy as np
import concourse.bass as bass
import concourse.mybir as mybir
from concourse.bass_utils import run_bass_kernel_spmd

F32 = mybir.dt.float32
BF16 = mybir.dt.bfloat16
AF = mybir.ActivationFunctionType
ALU = mybir.AluOpType
AX = mybir.AxisListType

D = 2048
T = 2048
NDC = D // 128
NTC = T // 512
EPS = 1e-5


def _prep_host(inputs):
    H = {}
    f32 = np.float32

    H["identity"] = np.eye(128, dtype=f32)
    H["ones128"] = np.ones((128, 128), dtype=f32)
    half = 32
    inv_freq = 1.0 / (10000.0 ** (np.arange(half, dtype=np.float64) * 2.0 / 64.0))
    pos = np.arange(T, dtype=np.float64)
    ang = pos[None, :] * inv_freq[:, None]
    cos = np.cos(ang).astype(f32)
    sin = np.sin(ang).astype(f32)
    H["cos"] = cos
    H["sin"] = sin
    am = np.zeros((128, 4, 512), dtype=f32)
    for p in range(4):
        for b in range(4):
            blk = am[:, p, 128 * b:128 * (b + 1)]
            if b > p:
                blk[:] = -30000.0
            elif b == p:
                r = np.arange(128)
                blk[:] = np.where(r[None, :] <= r[:, None], 0.0, -30000.0)
    H["smask"] = am.reshape(128, 4 * 512)
    pm = np.zeros((128, 4, 512), dtype=f32)
    for p in range(4):
        for b in range(4):
            blk = pm[:, p, 128 * b:128 * (b + 1)]
            if b < p:
                blk[:] = 0.0
            elif b > p:
                blk[:] = 1.0
            else:
                r = np.arange(128)
                blk[:] = np.where(r[:, None] <= r[None, :], 1.0, 0.0)
    H["ptmask"] = pm.reshape(128, 4 * 512)
    cst = np.zeros((128, 8), dtype=f32)
    cst[:, 0] = EPS
    cst[:, 1] = 1.0 / D
    cst[:, 2] = 1.0
    cst[:, 3] = 1.0 / np.sqrt(128.0)
    cst[:, 4] = -1.0
    cst[:, 5] = -0.5
    H["consts"] = cst
    seln = np.zeros((32, 32 * 128), dtype=f32)
    for j in range(32):
        seln[j, j * 128:(j + 1) * 128] = 1.0
    H["seln"] = seln

    W_uq = np.asarray(inputs["L0_attn_W_uq"], f32).reshape(512, 16, 64)
    W_qr = np.asarray(inputs["L0_attn_W_qr"], f32).reshape(512, 16, 64)
    W_uk = np.asarray(inputs["L0_attn_W_uk"], f32).reshape(512, 16, 64)
    W_kr = np.asarray(inputs["L0_attn_W_kr"], f32).reshape(512, 16, 64)
    q_comb = np.concatenate([W_qr[:, :, :32], W_qr[:, :, 32:], W_uq], axis=2)
    k_comb = np.concatenate([W_kr[:, :, :32], W_kr[:, :, 32:], W_uk], axis=2)
    H["Wq_comb"] = np.ascontiguousarray(q_comb.reshape(512, 2048))
    H["Wk_comb"] = np.ascontiguousarray(k_comb.reshape(512, 2048))
    H["W_dkv"] = np.asarray(inputs["L0_attn_W_dkv"], f32)
    H["W_dq"] = np.asarray(inputs["L0_attn_W_dq"], f32)
    H["W_uv"] = np.asarray(inputs["L0_attn_W_uv"], f32)
    H["W_out"] = np.asarray(inputs["L0_attn_W_out"], f32)
    H["b_out"] = np.ascontiguousarray(
        np.asarray(inputs["L0_attn_b_out"], f32).reshape(16, 128).T)

    H["rw_T"] = np.ascontiguousarray(np.asarray(inputs["L0_ffn_router_weight"], f32).T)
    H["sh_gate_T"] = np.ascontiguousarray(np.asarray(inputs["L0_ffn_sh_gate"], f32).T)
    H["sh_up_T"] = np.ascontiguousarray(np.asarray(inputs["L0_ffn_sh_up"], f32).T)
    H["sh_down_T"] = np.ascontiguousarray(np.asarray(inputs["L0_ffn_sh_down"], f32).T)
    H["r_gate_T"] = np.ascontiguousarray(np.asarray(inputs["L0_ffn_r_gate"], f32).T)
    H["r_up_T"] = np.ascontiguousarray(np.asarray(inputs["L0_ffn_r_up"], f32).T)
    H["r_down_T"] = np.ascontiguousarray(np.asarray(inputs["L0_ffn_r_down"], f32).T)

    W = np.asarray(inputs["L1_attn_x_proj_weight"], f32).T
    H["dt_down"] = np.ascontiguousarray(W[:, :128])
    H["WBC"] = np.ascontiguousarray(W[:, 128:160])
    H["dtw_T"] = np.ascontiguousarray(np.asarray(inputs["L1_attn_dt_proj_weight"], f32).T)
    H["dt_bias"] = np.ascontiguousarray(
        np.asarray(inputs["L1_attn_dt_proj_bias"], f32).reshape(16, 128).T)
    H["A_log"] = np.ascontiguousarray(
        np.asarray(inputs["L1_attn_A_log"], f32).reshape(16, 128, 16).transpose(1, 0, 2).reshape(128, 16 * 16))
    H["Dvec"] = np.ascontiguousarray(
        np.asarray(inputs["L1_attn_D"], f32).reshape(16, 128).T)
    H["op_T"] = np.ascontiguousarray(np.asarray(inputs["L1_attn_out_proj_weight"], f32).T)
    H["op_bias"] = np.ascontiguousarray(
        np.asarray(inputs["L1_attn_out_proj_bias"], f32).reshape(16, 128).T)

    H["f_gate_T"] = np.ascontiguousarray(np.asarray(inputs["L1_ffn_gate_w"], f32).T)
    H["f_up_T"] = np.ascontiguousarray(np.asarray(inputs["L1_ffn_up_w"], f32).T)
    H["f_down_T"] = np.ascontiguousarray(np.asarray(inputs["L1_ffn_down_w"], f32).T)
    H["gate_s"] = np.ascontiguousarray(
        np.asarray(inputs["L1_ffn_gate_s"], f32).reshape(44, 128).T)
    H["up_s"] = np.ascontiguousarray(
        np.asarray(inputs["L1_ffn_up_s"], f32).reshape(44, 128).T)
    H["down_s"] = np.ascontiguousarray(
        np.asarray(inputs["L1_ffn_down_s"], f32).reshape(16, 128).T)
    return H



ENGINES = ("sync", "tensor", "scalar", "vector", "gpsimd")


class Prog:

    def __init__(self, nc, block):
        self.nc = nc
        self.block = block
        self.ops = {e: [] for e in ENGINES}
        self.semh = {}
        self.cnt = {}
        self._guards = []

    def sem(self, name):
        if name not in self.semh:
            cm = self.nc.semaphore(name)
            self.semh[name] = cm.__enter__()
            self._guards.append(cm)
            self.cnt[name] = 0
        return self.semh[name]

    def bump(self, name, amt):
        self.sem(name)
        self.cnt[name] += amt
        return self.cnt[name]

    def emit(self, e, fn):
        self.ops[e].append(fn)

    def add(self, e, builder, inc=None):
        if inc is not None:
            name, amt = inc
            tgt = self.bump(name, amt)
            h = self.semh[name]

            def f(eng, builder=builder, h=h, amt=amt):
                builder(eng).then_inc(h, amt)
            self.emit(e, f)
            return tgt
        self.emit(e, lambda eng, builder=builder: builder(eng))
        return None

    def addw(self, e, builder):
        tgt = self.add(e, builder, inc=("p_" + e, 1))
        self.wait(e, "p_" + e, tgt)
        return tgt

    def wait(self, e, name, tgt):
        h = self.sem(name)
        self.emit(e, lambda eng, h=h, tgt=tgt: eng.wait_ge(h, tgt))

    def val(self, name):
        return self.cnt.get(name, 0)

    def flush(self):
        for e in ENGINES:
            fns = self.ops[e]
            if not fns:
                continue
            self.ops[e] = []

            def chain(eng, fns=fns):
                for fn in fns:
                    fn(eng)
            getattr(self.block, e)(chain)


class Scope:

    _n = 0

    def __init__(self, nc):
        self.nc = nc
        self.cms = []
        Scope._n += 1
        self.uid = Scope._n

    def __enter__(self):
        return self

    def sb(self, name, shape, dtype):
        cm = self.nc.sbuf_tensor("%s_u%d" % (name, self.uid), shape, dtype)
        t = cm.__enter__()
        self.cms.append(cm)
        return t

    def ps(self, name, shape, dtype=F32):
        cm = self.nc.psum_tensor("%s_u%d" % (name, self.uid), shape, dtype)
        t = cm.__enter__()
        self.cms.append(cm)
        return t

    def __exit__(self, *a):
        for cm in reversed(self.cms):
            cm.__exit__(None, None, None)
        return False


def _load(P, dst, src, sem="ld"):
    tgt = P.bump(sem, 16)
    h = P.semh[sem]
    P.emit("sync", lambda eng, dst=dst, src=src, h=h: eng.dma_start(out=dst, in_=src).then_inc(h, 16))
    return tgt


def _store(P, dst, src, sem="gd"):
    tgt = P.bump(sem, 16)
    h = P.semh[sem]
    P.emit("gpsimd", lambda eng, dst=dst, src=src, h=h: eng.dma_start(out=dst, in_=src).then_inc(h, 16))
    return tgt


def chunk_dram(ap, tc, width=512):
    return ap[:, tc * width:(tc + 1) * width].rearrange("(c p) t -> p c t", p=128)


def _v3(tile, w=512):
    return tile[:].rearrange("p (c t) -> p c t", t=w)



SC = 1.0 / np.sqrt(128.0)


def emit_ln_in(P, x_in_ap, xln, xf, xb, xsq, ones_bf, stats, cpool, psA, psB):
    nc = P.nc
    mean, msq, var, rstd = stats["mean"], stats["msq"], stats["var"], stats["rstd"]
    v_norm_prev = P.val("p_vector")
    v_var_prev = 0
    g_prev = P.val("gd")
    for tc in range(NTC):
        P.wait("sync", "p_vector", v_norm_prev)
        ld = _load(P, _v3(xf), chunk_dram(x_in_ap, tc))
        P.wait("vector", "ld", ld)
        v_cast = P.add("vector", lambda e, xb=xb, xf=xf: e.tensor_copy(xb[:], xf[:]),
                       inc=("p_vector", 1))
        P.wait("scalar", "p_vector", v_cast)
        s_sq = P.add("scalar", lambda e, xsq=xsq, xb=xb: e.square(xsq[:], xb[:]),
                     inc=("p_scalar", 1))
        P.wait("tensor", "p_vector", v_var_prev)
        P.wait("tensor", "p_scalar", s_sq)
        def mk_stats(e, xb=xb, xsq=xsq, psA=psA, psB=psB, ones_bf=ones_bf):
            xb3 = _v3(xb)
            xq3 = _v3(xsq)
            for dc in range(NDC):
                e.matmul(psA[:], ones_bf[:], xb3[:, dc, :], start=(dc == 0), stop=(dc == NDC - 1))
            for dc in range(NDC):
                ins = e.matmul(psB[:], ones_bf[:], xq3[:, dc, :], start=(dc == 0), stop=(dc == NDC - 1))
            return ins
        t_stats = P.add("tensor", mk_stats, inc=("p_tensor", 1))
        P.wait("vector", "p_tensor", t_stats)
        P.addw("vector", lambda e: e.tensor_scalar_mul(mean[:], psA[:], 1.0 / D))
        P.addw("vector", lambda e: e.tensor_mul(msq[:], mean[:], mean[:]))
        v_var = P.add("vector",
                      lambda e: e.scalar_tensor_tensor(var[:], psB[:], 1.0 / D,
                                                       msq[:], ALU.mult, ALU.subtract),
                      inc=("p_vector", 1))
        v_var_prev = v_var
        P.wait("scalar", "p_vector", v_var)
        P.addw("scalar", lambda e: e.activation(msq[:], var[:], AF.Ln,
                                                bias=cpool[:, 0:1]))
        s_rstd = P.add("scalar",
                       lambda e: e.activation(rstd[:], msq[:], AF.Exp,
                                              scale=cpool[:, 5:6]),
                       inc=("p_scalar", 1))
        P.wait("vector", "p_scalar", s_rstd)
        xf3 = _v3(xf)
        xl3 = xln[:].rearrange("p (c t) -> p c t", t=T)
        for dc in range(NDC):
            P.addw("vector", lambda e, dc=dc: e.tensor_sub(msq[:], xf3[:, dc, :], mean[:]))
            last = P.add("vector",
                         lambda e, dc=dc, tc=tc: e.tensor_mul(
                             xl3[:, dc, tc * 512:(tc + 1) * 512], msq[:], rstd[:]),
                         inc=("p_vector", 1) if dc == NDC - 1 else None)
            if dc < NDC - 1:
                P.wait("vector", "p_vector", P.val("p_vector"))
        v_norm_prev = last
    return v_norm_prev


def emit_out_ln_residual(P, tc, x_in_ap, x_out_ap, y_tc, xf, xsq_dc, ones_bf,
                         stats, cpool, psA, psB, dbg_stats=None):
    mean, msq, var, rstd = stats["mean"], stats["msq"], stats["var"], stats["rstd"]
    y3 = _v3(y_tc)
    t_used = [0, 0]
    sq_done = [0, 0]
    for dc in range(NDC):
        pp = dc % 2
        P.wait("scalar", "p_tensor", t_used[pp])
        def sq(e, dc=dc, pp=pp):
            return e.square(xsq_dc[:, pp * 512:(pp + 1) * 512], y3[:, dc, :])
        sq_done[pp] = P.add("scalar", sq, inc=("p_scalar", 1))
        P.wait("tensor", "p_scalar", sq_done[pp])
        def mm(e, dc=dc, pp=pp):
            e.matmul(psA[:], ones_bf[:], y3[:, dc, :],
                     start=(dc == 0), stop=(dc == NDC - 1))
            return e.matmul(psB[:], ones_bf[:], xsq_dc[:, pp * 512:(pp + 1) * 512],
                            start=(dc == 0), stop=(dc == NDC - 1))
        t_used[pp] = P.add("tensor", mm, inc=("p_tensor", 1))
    t_stats = t_used[(NDC - 1) % 2]
    P.wait("vector", "p_tensor", t_stats)
    P.addw("vector", lambda e: e.tensor_scalar_mul(mean[:], psA[:], 1.0 / D))
    P.addw("vector", lambda e: e.tensor_mul(msq[:], mean[:], mean[:]))
    v_var = P.add("vector", lambda e: e.scalar_tensor_tensor(
        var[:], psB[:], 1.0 / D, msq[:], ALU.mult, ALU.subtract),
        inc=("p_vector", 1))
    P.wait("scalar", "p_vector", v_var)
    P.addw("scalar", lambda e: e.activation(msq[:], var[:], AF.Ln, bias=cpool[:, 0:1]))
    s_rstd = P.add("scalar", lambda e: e.activation(rstd[:], msq[:], AF.Exp,
                                                    scale=cpool[:, 5:6]),
                   inc=("p_scalar", 1))
    if dbg_stats is not None and tc == dbg_stats[0]:
        P.wait("gpsimd", "p_scalar", s_rstd)
        _store(P, dbg_stats[1], mean[:])
        _store(P, dbg_stats[2], rstd[:])
        P.wait("sync", "gd", P.val("gd"))
    sync_fence(P)
    ld = _load(P, _v3(xf), chunk_dram(x_in_ap, tc))
    P.wait("vector", "ld", ld)
    P.wait("vector", "p_scalar", s_rstd)
    xf3 = _v3(xf)
    for dc in range(NDC):
        P.addw("vector", lambda e, dc=dc: e.tensor_sub(msq[:], y3[:, dc, :], mean[:]))
        P.addw("vector", lambda e, dc=dc: e.tensor_mul(msq[:], msq[:], rstd[:]))
        v_res = P.add("vector", lambda e, dc=dc: e.tensor_add(
            xf3[:, dc, :], xf3[:, dc, :], msq[:]),
            inc=("p_vector", 1) if dc == NDC - 1 else None)
        if dc < NDC - 1:
            pass
    v_res = P.val("p_vector")
    P.wait("gpsimd", "p_vector", v_res)
    st = _store(P, chunk_dram(x_out_ap, tc), _v3(xf))
    return st, v_res




def _l0a_phaseC(P, aps, xln, ckvT, cqT, stage, wbuf, pspool):
    sync_fence(P)
    psC = [pspool[:, 0:512], pspool[:, 512:1024]]
    ev_prev = [0, 0]
    for wkey, dstT in (("W_dkv", ckvT), ("W_dq", cqT)):
        P.wait("sync", "p_vector", P.val("p_vector"))
        ldw = _load(P, stage[:].rearrange("p (c i) -> p c i", i=512),
                    aps[wkey].rearrange("(c p) i -> p c i", p=128))
        P.wait("vector", "ld", ldw)
        v_w = P.add("vector", lambda e, wbuf=wbuf, stage=stage:
                    e.tensor_copy(wbuf[:], stage[:]), inc=("p_vector", 1))
        if wkey == "W_dq" and "dbg_wbuf" in aps:
            P.wait("gpsimd", "p_vector", v_w)
            _store(P, aps["dbg_wbuf"], wbuf[:])
            P.wait("tensor", "gd", P.val("gd"))
        P.wait("tensor", "p_vector", v_w)
        for ct in range(4):
            for tc in range(4):
                g = ct * 4 + tc
                pp = g % 2
                P.wait("tensor", "p_scalar", ev_prev[pp])
                def mk(e, pp=pp, ct=ct, tc=tc, wbuf=wbuf, xln=xln):
                    for dc in range(NDC):
                        ins = e.matmul(
                            psC[pp],
                            wbuf[:, dc * 512 + ct * 128: dc * 512 + (ct + 1) * 128],
                            xln[:, dc * T + tc * 512: dc * T + (tc + 1) * 512],
                            start=(dc == 0), stop=(dc == NDC - 1))
                    return ins
                t_g = P.add("tensor", mk, inc=("p_tensor", 1))
                P.wait("scalar", "p_tensor", t_g)
                def ev(e, pp=pp, ct=ct, tc=tc, dstT=dstT):
                    return e.copy(
                        dstT[:, ct * T + tc * 512: ct * T + (tc + 1) * 512],
                        psC[pp])
                ev_prev[pp] = P.add("scalar", ev, inc=("p_scalar", 1))
        P.wait("vector", "p_tensor", P.val("p_tensor"))
    P.wait("tensor", "p_scalar", P.val("p_scalar"))


def _l0a_loadD(P, aps, cosT, sinT, smask, ptmask, wq, wk, wuv, stage2):
    sync_fence(P)
    ld1 = _load(P, stage2[0:32, 0:2048], aps["cos"])
    ld1 = _load(P, stage2[0:32, 2048:4096], aps["sin"])
    P.wait("vector", "ld", ld1)
    P.add("vector", lambda e: e.tensor_copy(cosT[:], stage2[0:32, 0:2048]))
    v_t = P.add("vector", lambda e: e.tensor_copy(sinT[:], stage2[0:32, 2048:4096]),
                inc=("p_vector", 1))
    _load(P, smask[:], aps["smask"])
    P.wait("sync", "p_vector", v_t)
    ld2 = _load(P, stage2[:, 0:2048], aps["ptmask"])
    P.wait("vector", "ld", ld2)
    v_t = P.add("vector", lambda e: e.tensor_copy(ptmask[:], stage2[:, 0:2048]),
                inc=("p_vector", 1))
    for wkey, dst in (("Wq_comb", wq), ("Wk_comb", wk), ("W_uv", wuv)):
        for ch in range(2):
            P.wait("sync", "p_vector", P.val("p_vector"))
            ldw = _load(P, stage2[:].rearrange("p (c i) -> p c i", i=2048),
                        aps[wkey][256 * ch:256 * (ch + 1), :]
                        .rearrange("(c p) i -> p c i", p=128))
            P.wait("vector", "ld", ldw)
            P.add("vector", lambda e, dst=dst, ch=ch, stage2=stage2:
                  e.tensor_copy(dst[:, ch * 2 * T:(ch + 1) * 2 * T], stage2[:]),
                  inc=("p_vector", 1))


def _l0a_phaseE(P, aps, ctxT, stage3, woutp, y_tc, xf2, xsq_dc, ones_bf, bout,
                stats, cpool, pspool, x_in, x_out):
    sync_fence(P)
    psO = [pspool[:, 0:512], pspool[:, 512:1024]]
    ev_prev = [0, 0]
    v_res_prev = 0
    for tc in range(NTC):
        for p in range(4):
            for ch in range(2):
                P.wait("sync", "p_vector", P.val("p_vector"))
                ldw = _load(P, stage3[:].rearrange("p (c i) -> p c i", i=512),
                            aps["W_out"][1024 * ch:1024 * (ch + 1),
                                         p * 512:(p + 1) * 512]
                            .rearrange("(c p2) i -> p2 c i", p2=128))
                P.wait("vector", "ld", ldw)
                P.wait("vector", "p_tensor", P.val("p_tensor"))
                P.add("vector", lambda e, ch=ch, woutp=woutp, stage3=stage3:
                      e.tensor_copy(
                          woutp[:, ch * 8 * 512:(ch + 1) * 8 * 512], stage3[:]),
                      inc=("p_vector", 1))
            v_wp = P.val("p_vector")
            P.wait("tensor", "p_vector", v_wp)
            for ot in range(4):
                oc = 4 * p + ot
                pp = oc % 2
                P.wait("tensor", "p_scalar", ev_prev[pp])
                def mko(e, ot=ot, pp=pp, tc=tc, woutp=woutp, ctxT=ctxT):
                    for dc in range(NDC):
                        ins = e.matmul(
                            psO[pp],
                            woutp[:, dc * 512 + ot * 128:dc * 512 + (ot + 1) * 128],
                            ctxT[:, dc * T + tc * 512:dc * T + (tc + 1) * 512],
                            start=(dc == 0), stop=(dc == NDC - 1))
                    return ins
                t_o = P.add("tensor", mko, inc=("p_tensor", 1))
                P.wait("scalar", "p_tensor", t_o)
                P.wait("scalar", "p_vector", v_res_prev)
                def evo(e, pp=pp, oc=oc, y_tc=y_tc, bout=bout):
                    return e.activation(
                        y_tc[:, oc * 512:(oc + 1) * 512], psO[pp],
                        AF.Identity, bias=bout[:, oc:oc + 1])
                ev_prev[pp] = P.add("scalar", evo, inc=("p_scalar", 1))
        P.wait("sync", "gd", P.val("gd"))
        P.wait("tensor", "p_scalar", P.val("p_scalar"))
        st, v_res_prev = emit_out_ln_residual(
            P, tc, x_in, x_out, y_tc, xf2, xsq_dc, ones_bf,
            stats, cpool, pspool[:, 1024:1536], pspool[:, 1536:2048])


def emit_l0a(P, aps):
    nc = P.nc
    x_in, x_out = aps["x_in"], aps["x_out"]

    with Scope(nc) as S:
        cpool = S.sb("cpool", [128, 8], F32)
        ones_f = S.sb("onesf", [128, 128], F32)
        ones_bf = S.sb("onesbf", [128, 128], BF16)
        ident_f = S.sb("identf", [128, 128], F32)
        ident_bf = S.sb("identbf", [128, 128], BF16)
        ones2_bf = S.sb("ones2bf", [2, 128], BF16)
        bout = S.sb("bout", [128, 16], F32)
        st_mean = S.sb("stmean", [128, 512], F32)
        st_msq = S.sb("stmsq", [128, 512], F32)
        st_var = S.sb("stvar", [128, 512], F32)
        big_bf = S.sb("bigbf", [128, NDC * T], BF16)
        ckvT = S.sb("ckvT", [128, 4 * T], BF16)
        cqT = S.sb("cqT", [128, 4 * T], BF16)
        pspool = S.ps("pspool", [128, 4096], F32)

        stats = {"mean": st_mean, "msq": st_msq, "var": st_var, "rstd": st_var}
        xln = big_bf
        ctxT = big_bf

        _load(P, cpool[:], aps["consts"])
        _load(P, ones_f[:], aps["ones128"])
        _load(P, bout[:], aps["b_out"])
        ldi = _load(P, ident_f[:], aps["identity"])
        P.wait("vector", "ld", ldi)
        P.add("vector", lambda e: e.tensor_copy(ones_bf[:], ones_f[:]))
        P.add("vector", lambda e: e.tensor_copy(ones2_bf[:], ones_f[0:2, :]))
        P.add("vector", lambda e: e.tensor_copy(ident_bf[:], ident_f[:]),
              inc=("p_vector", 1))

        with Scope(nc) as S1:
            xf = S1.sb("xf", [128, NDC * 512], F32)
            xb = S1.sb("xb", [128, NDC * 512], BF16)
            xsq = S1.sb("xsq", [128, NDC * 512], BF16)
            emit_ln_in(P, x_in, xln, xf, xb, xsq, ones_bf, stats, cpool,
                       pspool[:, 0:512], pspool[:, 512:1024])
            P.flush()

        with Scope(nc) as S2:
            stage = S2.sb("stage", [128, 8192], F32)
            wbuf = S2.sb("wbuf", [128, NDC * 512], BF16)
            _l0a_phaseC(P, aps, xln, ckvT, cqT, stage, wbuf, pspool)
            if "dbg_ckvT" in aps:
                P.wait("gpsimd", "p_scalar", P.val("p_scalar"))
                _store(P, aps["dbg_ckvT"], ckvT[:])
                _store(P, aps["dbg_cqT"], cqT[:])
            P.flush()

        with Scope(nc) as S3:
            cosT = S3.sb("cosT", [32, T], BF16)
            sinT = S3.sb("sinT", [32, T], BF16)
            smask = S3.sb("sb_smask", [128, 2048], F32)
            ptmask = S3.sb("sb_ptmask", [128, 2048], BF16)
            wq = S3.sb("wq", [128, 4 * T], BF16)
            wk = S3.sb("wk", [128, 4 * T], BF16)
            wuv = S3.sb("wuv", [128, 4 * T], BF16)
            with Scope(nc) as S3a:
                stage2 = S3a.sb("stage2", [128, 4096], F32)
                _l0a_loadD(P, aps, cosT, sinT, smask, ptmask, wq, wk, wuv, stage2)
                P.flush()
            P.wait("tensor", "p_vector", P.val("p_vector"))

            with Scope(nc) as S3b:
                qh = S3b.sb("qh", [128, T], BF16)
                kh = S3b.sb("kh", [128, T], BF16)
                vh = S3b.sb("vh", [128, 16 * 128], BF16)
                ta = S3b.sb("ta", [32, 512], F32)
                tb = S3b.sb("tb", [32, 512], F32)
                junk = S3b.sb("junk", [128, 512], F32)
                racc = S3b.sb("racc", [128, 4], F32)
                nmx = S3b.sb("nmx", [128, 4], F32)
                scl = S3b.sb("scl", [128, 8], F32)
                rowsrc = S3b.sb("rowsrc", [128, 2], BF16)
                Lrow2 = S3b.sb("Lrow2", [2, T], BF16)
                ptb0 = S3b.sb("ptb0", [128, 512], BF16)
                ptb1 = S3b.sb("ptb1", [128, 512], BF16)
                emit_l0a_heads(P, dict(
                    xln=xln, ckvT=ckvT, cqT=cqT, wq=wq, wk=wk, wuv=wuv,
                    cosT=cosT, sinT=sinT, smask=smask, ptmask=ptmask, ctxT=ctxT,
                    qh=qh, kh=kh, vh=vh, ta=ta, tb=tb, junk=junk, racc=racc,
                    nmx=nmx, scl=scl, rowsrc=rowsrc, Lrow2=Lrow2,
                    ptb=[ptb0, ptb1],
                    psS=pspool[:, 0:2048],
                    psPT=[pspool[:, 2048:2560], pspool[:, 2560:3072]],
                    psCtx=pspool[:, 3072:3584],
                    psRow=pspool[0:2, 3584:3648].bitcast(BF16),
                    ones2_bf=ones2_bf, ident_bf=ident_bf, cpool=cpool,
                    **{k: aps[k] for k in aps if k.startswith("dbg_")}))
                P.flush()

        soft_barrier(P)
        with Scope(nc) as S4:
            stage3 = S4.sb("stage3", [128, 4096], F32)
            woutp = S4.sb("woutp", [128, NDC * 512], BF16)
            y_tc = S4.sb("y_tc", [128, NDC * 512], BF16)
            xf2 = S4.sb("xf2", [128, NDC * 512], F32)
            xsq_dc = S4.sb("xsqdc", [128, 1024], BF16)
            _l0a_phaseE(P, aps, ctxT, stage3, woutp, y_tc, xf2, xsq_dc,
                        ones_bf, bout, stats, cpool, pspool, x_in, x_out)
            P.flush()
    P.wait("gpsimd", "gd", P.val("gd"))
    P.flush()



def sync_fence(P):
    for e in ("tensor", "scalar", "vector"):
        P.wait("sync", "p_" + e, P.val("p_" + e))
    P.wait("sync", "gd", P.val("gd"))


def soft_barrier(P, engines=("tensor", "scalar", "vector")):
    vals = {e: P.val("p_" + e) for e in engines}
    for e in engines:
        for o in engines:
            if o != e:
                P.wait(e, "p_" + o, vals[o])


def emit_l0a_heads(P, B):
    xln, ckvT, cqT = B["xln"], B["ckvT"], B["cqT"]
    wq, wk, wuv = B["wq"], B["wk"], B["wuv"]
    cosT, sinT, smask, ptmask = B["cosT"], B["sinT"], B["smask"], B["ptmask"]
    ctxT, qh, kh, vh = B["ctxT"], B["qh"], B["kh"], B["vh"]
    ta, tb, junk, racc, nmx, scl = B["ta"], B["tb"], B["junk"], B["racc"], B["nmx"], B["scl"]
    rowsrc, Lrow2, ptb, psS, psPT = B["rowsrc"], B["Lrow2"], B["ptb"], B["psS"], B["psPT"]
    psCtx, psRow = B["psCtx"], B["psRow"]
    ones2_bf, ident_bf, cpool = B["ones2_bf"], B["ident_bf"], B["cpool"]
    SQ = float(np.sqrt(128.0))

    for h in range(16):
        soft_barrier(P)
        pp_guard_v = [0, 0]
        pp_guard_s = [0, 0]
        for side in ("q", "k"):
            wqk = wq if side == "q" else wk
            srcT = cqT if side == "q" else ckvT
            dsth = qh if side == "q" else kh
            for tc in range(4):
                pp = tc % 2
                P.wait("tensor", "p_vector", pp_guard_v[pp])
                P.wait("tensor", "p_scalar", pp_guard_s[pp])
                def mkp(e, pp=pp, tc=tc, wqk=wqk, srcT=srcT, h=h):
                    for ct in range(4):
                        ins = e.matmul(
                            psPT[pp][:],
                            wqk[:, ct * T + h * 128:ct * T + (h + 1) * 128],
                            srcT[:, ct * T + tc * 512:ct * T + (tc + 1) * 512],
                            start=(ct == 0), stop=(ct == 3))
                    return ins
                t_p = P.add("tensor", mkp, inc=("p_tensor", 1))
                P.wait("vector", "p_tensor", t_p)
                ps = psPT[pp]
                csl = slice(tc * 512, (tc + 1) * 512)
                P.add("vector", lambda e, ps=ps, csl=csl: e.tensor_mul(
                    ta[:], ps[0:32, :], cosT[:, csl]))
                P.addw("vector", lambda e, ps=ps, csl=csl: e.tensor_mul(
                    tb[:], ps[32:64, :], sinT[:, csl]))
                P.add("vector", lambda e, ps=ps, csl=csl, dsth=dsth: e.tensor_sub(
                    dsth[0:32, csl], ta[:], tb[:]))
                P.add("vector", lambda e, ps=ps, csl=csl: e.tensor_mul(
                    ta[:], ps[32:64, :], cosT[:, csl]))
                P.addw("vector", lambda e, ps=ps, csl=csl: e.tensor_mul(
                    tb[:], ps[0:32, :], sinT[:, csl]))
                pp_guard_v[pp] = P.add(
                    "vector", lambda e, ps=ps, csl=csl, dsth=dsth: e.tensor_add(
                        dsth[32:64, csl], ta[:], tb[:]), inc=("p_vector", 1))
                P.wait("scalar", "p_tensor", t_p)
                def mkbase(e, pp=pp, tc=tc, dsth=dsth):
                    return e.copy(dsth[64:128, tc * 512:(tc + 1) * 512],
                                  psPT[pp][64:128, :])
                pp_guard_s[pp] = P.add("scalar", mkbase, inc=("p_scalar", 1))
        for tt in range(16):
            pp = tt % 2
            P.wait("tensor", "p_vector", pp_guard_v[pp])
            P.wait("tensor", "p_scalar", pp_guard_s[pp])
            def mkv(e, pp=pp, tt=tt, h=h):
                for ct in range(4):
                    ins = e.matmul(
                        psPT[pp][:, 0:128],
                        ckvT[:, ct * T + tt * 128:ct * T + (tt + 1) * 128],
                        wuv[:, ct * T + h * 128:ct * T + (h + 1) * 128],
                        start=(ct == 0), stop=(ct == 3))
                return ins
            t_v = P.add("tensor", mkv, inc=("p_tensor", 1))
            pp_guard_v[pp] = P.val("p_vector")
            P.wait("scalar", "p_tensor", t_v)
            def evv(e, pp=pp, tt=tt):
                return e.copy(vh[:, tt * 128:(tt + 1) * 128], psPT[pp][:, 0:128])
            pp_guard_s[pp] = P.add("scalar", evv, inc=("p_scalar", 1))
        soft_barrier(P)
        if h == 0 and "dbg_qh" in B:
            P.wait("gpsimd", "p_scalar", P.val("p_scalar"))
            P.wait("gpsimd", "p_vector", P.val("p_vector"))
            _store(P, B["dbg_qh"], qh[:])
            _store(P, B["dbg_kh"], kh[:])
            _store(P, B["dbg_vh"], vh[:])
            P.wait("sync", "gd", P.val("gd"))

        s_psS_prev = 0
        s_row_prev = 0
        for qi in range(16):
            nch = qi // 4 + 1
            P.wait("tensor", "p_scalar", s_psS_prev)
            def mks(e, qi=qi, nch=nch):
                for kc in range(nch):
                    ins = e.matmul(
                        psS[:, kc * 512:(kc + 1) * 512],
                        qh[:, qi * 128:(qi + 1) * 128],
                        kh[:, kc * 512:(kc + 1) * 512],
                        start=True, stop=True)
                return ins
            t_s = P.add("tensor", mks, inc=("p_tensor", 1))
            P.wait("vector", "p_tensor", t_s)
            dg = (nch - 1) * 512
            P.addw("vector", lambda e, dg=dg, qi=qi: e.tensor_add(
                psS[:, dg:dg + 512], psS[:, dg:dg + 512],
                smask[:, (qi % 4) * 512:(qi % 4 + 1) * 512]))
            def mkmaxes(e, nch=nch):
                for kc in range(nch):
                    ins = e.reduce_max(nmx[:, kc:kc + 1],
                                       psS[:, kc * 512:(kc + 1) * 512],
                                       axis=AX.X, negate=True)
                return ins
            P.addw("vector", mkmaxes)
            P.addw("vector", lambda e, nch=nch: e.tensor_reduce(
                scl[:, 0:1], nmx[:, 0:nch], axis=AX.X, op=ALU.min))
            v_m = P.add("vector", lambda e: e.tensor_scalar_mul(
                scl[:, 1:2], scl[:, 0:1], SC), inc=("p_vector", 1))
            P.wait("scalar", "p_vector", v_m)
            def mkexp(e, qi=qi, nch=nch):
                for kc in range(nch):
                    ins = e.activation(junk[:], psS[:, kc * 512:(kc + 1) * 512],
                                       AF.Exp, bias=scl[:, 1:2], scale=cpool[:, 3:4],
                                       accum_out=racc[:, kc:kc + 1])
                return ins
            s_psS_prev = P.add("scalar", mkexp, inc=("p_scalar", 1))
            P.wait("vector", "p_scalar", s_psS_prev)
            v_r = P.add("vector", lambda e, nch=nch: e.tensor_reduce(
                scl[:, 2:3], racc[:, 0:nch], axis=AX.X, op=ALU.add),
                inc=("p_vector", 1))
            P.wait("scalar", "p_vector", v_r)
            s_ln = P.add("scalar", lambda e: e.activation(scl[:, 3:4], scl[:, 2:3], AF.Ln),
                         inc=("p_scalar", 1))
            P.wait("vector", "p_scalar", s_ln)
            P.addw("vector", lambda e: e.tensor_sub(scl[:, 4:5], scl[:, 1:2], scl[:, 3:4]))
            P.addw("vector", lambda e: e.tensor_scalar_mul(scl[:, 5:6], scl[:, 4:5], SQ))
            P.addw("vector", lambda e: e.tensor_copy(rowsrc[:, 0:1], scl[:, 5:6]))
            P.addw("vector", lambda e: e.tensor_copy(scl[:, 6:7], rowsrc[:, 0:1]))
            P.addw("vector", lambda e: e.tensor_sub(scl[:, 7:8], scl[:, 5:6], scl[:, 6:7]))
            v_row = P.add("vector", lambda e: e.tensor_copy(rowsrc[:, 1:2], scl[:, 7:8]),
                          inc=("p_vector", 1))
            P.wait("tensor", "p_vector", v_row)
            P.wait("tensor", "p_scalar", s_row_prev)
            t_tr = P.add("tensor", lambda e: e.transpose(psRow[:], rowsrc[:, 0:2], ident_bf[:]),
                         inc=("p_tensor", 1))
            P.wait("scalar", "p_tensor", t_tr)
            def evrow(e, qi=qi):
                return e.copy(Lrow2[0:2, qi * 128:(qi + 1) * 128], psRow[0:2, 0:128])
            s_row_prev = P.add("scalar", evrow, inc=("p_scalar", 1))

        if h == 0 and "dbg_Lrow" in B:
            P.wait("gpsimd", "p_scalar", s_row_prev)
            _store(P, B["dbg_Lrow"], Lrow2[0:2, :])
            P.wait("sync", "gd", P.val("gd"))
        P.wait("tensor", "p_scalar", s_row_prev)
        exp_mark = [0, 0]
        ctx_mark = [0, 0]
        mask_mark = [0, 0]
        s_ctx_prev = 0
        for tqc in range(4):
            ntt = 4 * tqc + 4
            for tt in range(ntt):
                pp = tt % 2
                P.wait("tensor", "p_scalar", exp_mark[pp])
                def mkpt(e, pp=pp, tt=tt, tqc=tqc):
                    e.matmul(psPT[pp][:],
                             kh[:, tt * 128:(tt + 1) * 128],
                             qh[:, tqc * 512:(tqc + 1) * 512],
                             start=True, stop=False)
                    return e.matmul(psPT[pp][:], ones2_bf[:],
                                    Lrow2[0:2, tqc * 512:(tqc + 1) * 512],
                                    start=False, stop=True)
                t_pt = P.add("tensor", mkpt, inc=("p_tensor", 1))
                P.wait("scalar", "p_tensor", t_pt)
                P.wait("scalar", "p_tensor", ctx_mark[pp])
                def mkexp2(e, pp=pp):
                    return e.activation(ptb[pp][:], psPT[pp][:], AF.Exp,
                                        scale=cpool[:, 3:4])
                exp_mark[pp] = P.add("scalar", mkexp2, inc=("p_scalar", 1))
                pat = tt - 4 * tqc
                if pat >= 0:
                    P.wait("vector", "p_scalar", exp_mark[pp])
                    def mkm2(e, pp=pp, pat=pat):
                        return e.tensor_mul(ptb[pp][:], ptb[pp][:],
                                            ptmask[:, pat * 512:(pat + 1) * 512])
                    mask_mark[pp] = P.add("vector", mkm2, inc=("p_vector", 1))
                    P.wait("tensor", "p_vector", mask_mark[pp])
                else:
                    P.wait("tensor", "p_scalar", exp_mark[pp])
                if tt == 0:
                    P.wait("tensor", "p_scalar", s_ctx_prev)
                def mkctx(e, pp=pp, tt=tt, ntt=ntt):
                    return e.matmul(psCtx[:], vh[:, tt * 128:(tt + 1) * 128],
                                    ptb[pp][:], start=(tt == 0), stop=(tt == ntt - 1))
                ctx_mark[pp] = P.add("tensor", mkctx, inc=("p_tensor", 1))
            P.wait("scalar", "p_tensor", ctx_mark[(ntt - 1) % 2])
            def evctx(e, h=h, tqc=tqc):
                return e.copy(ctxT[:, h * T + tqc * 512:h * T + (tqc + 1) * 512],
                              psCtx[:])
            s_ctx_prev = P.add("scalar", evctx, inc=("p_scalar", 1))
        if h == 0 and "dbg_ctx" in B:
            P.wait("gpsimd", "p_scalar", s_ctx_prev)
            _store(P, B["dbg_ctx"], ctxT[:, 0:T])
            P.wait("sync", "gd", P.val("gd"))



L0A_KEYS = ("consts", "ones128", "identity", "cos", "sin", "smask", "ptmask",
            "b_out", "W_dkv", "W_dq", "Wq_comb", "Wk_comb", "W_uv", "W_out")


def build_l0a(H):
    nc = bass.Bass("TRN2", target_bir_lowering=False)
    aps = {}
    for name in L0A_KEYS:
        aps[name] = nc.dram_tensor(name, list(H[name].shape), F32,
                                   kind="ExternalInput").ap()
    aps["x_in"] = nc.dram_tensor("x_in", [D, T], F32, kind="ExternalInput").ap()
    aps["x_out"] = nc.dram_tensor("x_out", [D, T], F32, kind="ExternalOutput").ap()
    with nc.Block() as block:
        P = Prog(nc, block)
        emit_l0a(P, aps)
    return nc


def kernel_unfused(**inputs):
    H = _prep_host(inputs)
    x = np.asarray(inputs["L0_attn_x"], np.float32)
    ncores = 8
    cores = list(range(ncores))
    xT = [np.ascontiguousarray(x[b].T) for b in range(ncores)]

    for build, keys in ((build_l0a, L0A_KEYS), (build_l0f, L0F_KEYS),
                        (build_l1a, L1A_KEYS), (build_l1f, L1F_KEYS)):
        nc = build(H)
        common = {k: H[k] for k in keys}
        in_maps = [dict(common, x_in=xT[b]) for b in range(ncores)]
        res = run_bass_kernel_spmd(nc, in_maps, core_ids=cores)
        xT = [np.asarray(r["x_out"]) for r in res.results]

    out = np.stack([t.T for t in xT]).astype(np.float32)
    return out



def _ffn_load_w(P, stage, dst, src_ap, mode, I):
    for hch in range(2):
        P.wait("sync", "p_vector", P.val("p_vector"))
        if mode == "cols":
            sl = src_ap[hch * 1024:(hch + 1) * 1024, I * 512:(I + 1) * 512]
            nsub = 8
        else:
            sl = src_ap[I * 512 + hch * 256:I * 512 + (hch + 1) * 256, :]
            nsub = 2
        ld = _load(P, stage[:].rearrange("p (c i) -> p c i", c=nsub),
                   sl.rearrange("(c p) i -> p c i", p=128))
        P.wait("vector", "ld", ld)
        P.add("vector", lambda e, hch=hch, dst=dst, stage=stage:
              e.tensor_copy(dst[:, hch * 4096:(hch + 1) * 4096], stage[:]),
              inc=("p_vector", 1))
    return P.val("p_vector")


def emit_ffn_pass(P, B, half, wg_ap, wu_ap, wd_ap, nchunks, first_pass,
                  gs=None, us=None, ds=None, comb=None, sc_base=0):
    nc = P.nc
    xln, acc, stage = B["xln"], B["acc"], B["stage"]
    wbuf, hT, hsil = B["wbuf"], B["hT"], B["hsil"]
    pspool = B["pspool"]
    psG = [pspool[:, 0:512], pspool[:, 512:1024]]
    psU = [pspool[:, 1024:1536], pspool[:, 1536:2048]]
    psD = [pspool[:, 2048:2560], pspool[:, 2560:3072]]
    tok0 = half * 1024

    for I in range(nchunks):
        P.wait("sync", "p_tensor", P.val("p_tensor"))
        _ffn_load_w(P, stage, wbuf, wg_ap, "cols", I)
        P.wait("tensor", "p_vector", P.val("p_vector"))
        sil_done = [0, 0]
        for isub in range(4):
            g_isub = sc_base + I * 4 + isub
            for tch in range(2):
                pp = (isub * 2 + tch) % 2
                P.wait("tensor", "p_scalar", sil_done[pp])
                def mmg(e, isub=isub, tch=tch, pp=pp):
                    for dc in range(NDC):
                        ins = e.matmul(psG[pp],
                                       wbuf[:, dc * 512 + isub * 128:dc * 512 + (isub + 1) * 128],
                                       xln[:, dc * T + tok0 + tch * 512:dc * T + tok0 + (tch + 1) * 512],
                                       start=(dc == 0), stop=(dc == NDC - 1))
                    return ins
                t_g = P.add("tensor", mmg, inc=("p_tensor", 1))
                P.wait("scalar", "p_tensor", t_g)
                hs = slice(isub * 1024 + tch * 512, isub * 1024 + (tch + 1) * 512)
                if gs is not None:
                    def mksil(e, pp=pp, hs=hs, g_isub=g_isub):
                        return e.activation(hsil[:, hs], psG[pp], AF.Silu,
                                            scale=gs[:, g_isub:g_isub + 1])
                else:
                    def mksil(e, pp=pp, hs=hs):
                        return e.activation(hsil[:, hs], psG[pp], AF.Silu)
                sil_done[pp] = P.add("scalar", mksil, inc=("p_scalar", 1))
        P.wait("sync", "p_tensor", P.val("p_tensor"))
        _ffn_load_w(P, stage, wbuf, wu_ap, "cols", I)
        P.wait("tensor", "p_vector", P.val("p_vector"))
        P.wait("tensor", "p_scalar", P.val("p_scalar"))
        h_done = [0, 0]
        for isub in range(4):
            g_isub = sc_base + I * 4 + isub
            for tch in range(2):
                pp = (isub * 2 + tch) % 2
                P.wait("tensor", "p_vector", h_done[pp])
                def mmu(e, isub=isub, tch=tch, pp=pp):
                    for dc in range(NDC):
                        ins = e.matmul(psU[pp],
                                       wbuf[:, dc * 512 + isub * 128:dc * 512 + (isub + 1) * 128],
                                       xln[:, dc * T + tok0 + tch * 512:dc * T + tok0 + (tch + 1) * 512],
                                       start=(dc == 0), stop=(dc == NDC - 1))
                    return ins
                t_u = P.add("tensor", mmu, inc=("p_tensor", 1))
                P.wait("vector", "p_tensor", t_u)
                hs = slice(isub * 1024 + tch * 512, isub * 1024 + (tch + 1) * 512)
                if us is not None:
                    def mkh(e, pp=pp, hs=hs, g_isub=g_isub):
                        return e.scalar_tensor_tensor(
                            hT[:, hs], psU[pp], us[:, g_isub:g_isub + 1],
                            hsil[:, hs], ALU.mult, ALU.mult)
                else:
                    def mkh(e, pp=pp, hs=hs):
                        return e.tensor_mul(hT[:, hs], psU[pp], hsil[:, hs])
                v_h = P.add("vector", mkh, inc=("p_vector", 1))
                if comb is not None:
                    P.wait("vector", "p_vector", v_h)
                    def mkcomb(e, hs=hs, tch=tch):
                        return e.tensor_mul(
                            hT[:, hs], hT[:, hs],
                            comb[:, tok0 + tch * 512:tok0 + (tch + 1) * 512])
                    v_h = P.add("vector", mkcomb, inc=("p_vector", 1))
                h_done[pp] = v_h
        P.wait("sync", "p_tensor", P.val("p_tensor"))
        _ffn_load_w(P, stage, wbuf, wd_ap, "rows", I)
        P.wait("tensor", "p_vector", P.val("p_vector"))
        acc_done = [0, 0]
        for ot in range(16):
            for tch in range(2):
                pp = (ot * 2 + tch) % 2
                P.wait("tensor", "p_vector", acc_done[pp])
                def mmd(e, ot=ot, tch=tch, pp=pp):
                    for isub in range(4):
                        ins = e.matmul(
                            psD[pp],
                            wbuf[:, isub * 2048 + ot * 128:isub * 2048 + (ot + 1) * 128],
                            hT[:, isub * 1024 + tch * 512:isub * 1024 + (tch + 1) * 512],
                            start=(isub == 0), stop=(isub == 3))
                    return ins
                t_d = P.add("tensor", mmd, inc=("p_tensor", 1))
                P.wait("vector", "p_tensor", t_d)
                asl = slice(ot * 1024 + tch * 512, ot * 1024 + (tch + 1) * 512)
                if first_pass and I == 0:
                    if ds is not None:
                        def mkacc(e, pp=pp, asl=asl, ot=ot):
                            return e.tensor_single_scalar(
                                acc[:, asl], psD[pp], ds[:, ot:ot + 1], ALU.mult)
                    else:
                        def mkacc(e, pp=pp, asl=asl):
                            return e.tensor_copy(acc[:, asl], psD[pp])
                else:
                    if ds is not None:
                        def mkacc(e, pp=pp, asl=asl, ot=ot):
                            return e.scalar_tensor_tensor(
                                acc[:, asl], psD[pp], ds[:, ot:ot + 1],
                                acc[:, asl], ALU.mult, ALU.add)
                    else:
                        def mkacc(e, pp=pp, asl=asl):
                            return e.tensor_add(acc[:, asl], acc[:, asl], psD[pp])
                acc_done[pp] = P.add("vector", mkacc, inc=("p_vector", 1))
        P.wait("tensor", "p_vector", P.val("p_vector"))


def emit_router(P, B, aps):
    nc = P.nc
    xln, stage = B["xln"], B["stage"]
    rwb, comb, crow, crow_bf = B["rwb"], B["comb"], B["crow"], B["crow_bf"]
    combN, t8 = B["combN"], B["t8"]
    ident_f, ones_bf, cpool = B["ident_f"], B["ones_bf"], B["cpool"]
    pspool = B["pspool"]
    psL = pspool[:, 0:512]
    psT = pspool[0:1, 512:640]
    psBC = pspool[:, 1024:1536]

    P.wait("sync", "p_vector", P.val("p_vector"))
    ld = _load(P, stage[:, 0:128].rearrange("p (c e) -> p c e", e=8),
               aps["rw_T"].rearrange("(c p) e -> p c e", p=128))
    P.wait("vector", "ld", ld)
    v_rw = P.add("vector", lambda e: e.tensor_copy(rwb[:], stage[:, 0:128]),
                 inc=("p_vector", 1))
    P.wait("tensor", "p_vector", v_rw)
    v_done = 0
    for tt in range(16):
        P.wait("tensor", "p_vector", v_done)
        def mml(e, tt=tt):
            for dc in range(NDC):
                ins = e.matmul(psL[:, 0:8],
                               xln[:, dc * T + tt * 128:dc * T + tt * 128 + 128],
                               rwb[:, dc * 8:(dc + 1) * 8],
                               start=(dc == 0), stop=(dc == NDC - 1))
            return ins
        t_l = P.add("tensor", mml, inc=("p_tensor", 1))
        P.wait("vector", "p_tensor", t_l)
        P.addw("vector", lambda e: e.reduce_max(t8[:, 0:1], psL[:, 0:8],
                                                axis=AX.X, negate=True))
        P.addw("vector", lambda e: e.tensor_scalar_mul(t8[:, 1:2], t8[:, 0:1], -1.0))
        P.addw("vector", lambda e: e.tensor_single_scalar(
            t8[:, 8:16], psL[:, 0:8], t8[:, 1:2], ALU.is_ge))
        P.addw("vector", lambda e: e.scalar_tensor_tensor(
            t8[:, 16:24], t8[:, 8:16], -30000.0, psL[:, 0:8], ALU.mult, ALU.add))
        v_m2 = P.addw("vector", lambda e: e.reduce_max(
            t8[:, 2:3], t8[:, 16:24], axis=AX.X, negate=True))
        P.wait("scalar", "p_vector", v_m2)
        s_e = P.add("scalar", lambda e: e.activation(
            t8[:, 3:4], t8[:, 2:3], AF.Exp, scale=cpool[:, 4:5], bias=t8[:, 0:1]),
            inc=("p_scalar", 1))
        P.wait("vector", "p_scalar", s_e)
        P.addw("vector", lambda e: e.tensor_scalar_add(t8[:, 4:5], t8[:, 3:4], 1.0))
        P.addw("vector", lambda e: e.reciprocal(t8[:, 5:6], t8[:, 4:5]))
        v_done = P.add("vector", lambda e, tt=tt: e.tensor_mul(
            combN[:, tt:tt + 1], t8[:, 4:5], t8[:, 5:6]), inc=("p_vector", 1))
    s_ev = 0
    for tt in range(16):
        P.wait("tensor", "p_vector", v_done)
        P.wait("tensor", "p_scalar", s_ev)
        t_tr = P.add("tensor", lambda e, tt=tt: e.transpose(
            psT[0:1, 0:128], combN[:, tt:tt + 1], ident_f[:]),
            inc=("p_tensor", 1))
        P.wait("scalar", "p_tensor", t_tr)
        s_ev = P.add("scalar", lambda e, tt=tt: e.copy(
            crow[0:1, tt * 128:(tt + 1) * 128], psT[0:1, 0:128]),
            inc=("p_scalar", 1))
    P.wait("vector", "p_scalar", s_ev)
    v_cb = P.add("vector", lambda e: e.tensor_copy(crow_bf[:], crow[:]),
                 inc=("p_vector", 1))
    P.wait("tensor", "p_vector", v_cb)
    s_ev = 0
    for tc in range(4):
        P.wait("tensor", "p_scalar", s_ev)
        t_bc = P.add("tensor", lambda e, tc=tc: e.matmul(
            psBC, ones_bf[0:1, :], crow_bf[0:1, tc * 512:(tc + 1) * 512],
            start=True, stop=True), inc=("p_tensor", 1))
        P.wait("scalar", "p_tensor", t_bc)
        s_ev = P.add("scalar", lambda e, tc=tc: e.copy(
            comb[:, tc * 512:(tc + 1) * 512], psBC), inc=("p_scalar", 1))


def emit_l0f(P, aps):
    nc = P.nc
    x_in, x_out = aps["x_in"], aps["x_out"]
    with Scope(nc) as S:
        cpool = S.sb("cpool", [128, 8], F32)
        ones_f = S.sb("onesf", [128, 128], F32)
        ones_bf = S.sb("onesbf", [128, 128], BF16)
        ident_f = S.sb("identf", [128, 128], F32)
        st_mean = S.sb("stmean", [128, 512], F32)
        st_msq = S.sb("stmsq", [128, 512], F32)
        st_var = S.sb("stvar", [128, 512], F32)
        big_bf = S.sb("bigbf", [128, NDC * T], BF16)
        acc = S.sb("acc", [128, NDC * 1024], F32)
        comb = S.sb("comb", [128, 2048], BF16)
        pspool = S.ps("pspool", [128, 4096], F32)
        stats = {"mean": st_mean, "msq": st_msq, "var": st_var, "rstd": st_var}
        xln = big_bf

        _load(P, cpool[:], aps["consts"])
        ldc = _load(P, ones_f[:], aps["ones128"])
        ldi = _load(P, ident_f[:], aps["identity"])
        P.wait("vector", "ld", ldi)
        P.add("vector", lambda e: e.tensor_copy(ones_bf[:], ones_f[:]),
              inc=("p_vector", 1))

        with Scope(nc) as S1:
            xf = S1.sb("xf", [128, NDC * 512], F32)
            xb = S1.sb("xb", [128, NDC * 512], BF16)
            xsq = S1.sb("xsq", [128, NDC * 512], BF16)
            emit_ln_in(P, x_in, xln, xf, xb, xsq, ones_bf, stats, cpool,
                       pspool[:, 0:512], pspool[:, 512:1024])
            P.flush()

        with Scope(nc) as S2:
            stage = S2.sb("stage", [128, 4096], F32)
            rwb = S2.sb("rwb", [128, 128], BF16)
            crow = S2.sb("crow", [1, 2048], F32)
            crow_bf = S2.sb("crowbf", [1, 2048], BF16)
            combN = S2.sb("combN", [128, 16], F32)
            t8 = S2.sb("t8", [128, 24], F32)
            emit_router(P, dict(xln=xln, stage=stage, rwb=rwb, comb=comb,
                                crow=crow, crow_bf=crow_bf, combN=combN, t8=t8,
                                ident_f=ident_f, ones_bf=ones_bf, cpool=cpool,
                                pspool=pspool), aps)
            if "dbg_comb" in aps:
                P.wait("gpsimd", "p_scalar", P.val("p_scalar"))
                _store(P, aps["dbg_comb"], comb[:])
                P.wait("sync", "gd", P.val("gd"))
            P.flush()

        for half in range(2):
            soft_barrier(P)
            with Scope(nc) as SA:
                sync_fence(P)
                stage = SA.sb("stageF%d" % half, [128, 4096], F32)
                wbuf = SA.sb("wbufF%d" % half, [128, 8192], BF16)
                hT = SA.sb("hT%d" % half, [128, 4096], BF16)
                hsil = SA.sb("hsil%d" % half, [128, 4096], BF16)
                Bf = dict(xln=xln, acc=acc, stage=stage, wbuf=wbuf, hT=hT,
                          hsil=hsil, pspool=pspool)
                emit_ffn_pass(P, Bf, half, aps["sh_gate_T"], aps["sh_up_T"],
                              aps["sh_down_T"], 16, first_pass=True)
                emit_ffn_pass(P, Bf, half, aps["r_gate_T"], aps["r_up_T"],
                              aps["r_down_T"], 2, first_pass=False, comb=comb)
                if half == 1 and "dbg_acc" in aps:
                    P.wait("gpsimd", "p_vector", P.val("p_vector"))
                    _store(P, aps["dbg_acc"], acc[:])
                    P.wait("sync", "gd", P.val("gd"))
                P.flush()
            soft_barrier(P)
            with Scope(nc) as SB:
                sync_fence(P)
                y_tc = SB.sb("y_tc%d" % half, [128, NDC * 512], BF16)
                xf2 = SB.sb("xf2_%d" % half, [128, NDC * 512], F32)
                xsq_dc = SB.sb("xsqdc%d" % half, [128, 1024], BF16)
                acc3 = acc[:].rearrange("p (o t) -> p o t", t=1024)
                for tcl in range(2):
                    tc = half * 2 + tcl
                    s_cast = P.add("scalar", lambda e, tcl=tcl: e.copy(
                        _v3(y_tc), acc3[:, :, tcl * 512:(tcl + 1) * 512]),
                        inc=("p_scalar", 1))
                    P.wait("tensor", "p_scalar", s_cast)
                    P.wait("sync", "gd", P.val("gd"))
                    st, v_res = emit_out_ln_residual(
                        P, tc, x_in, x_out, y_tc, xf2, xsq_dc, ones_bf,
                        stats, cpool, pspool[:, 3072:3584], pspool[:, 3584:4096],
                        dbg_stats=(2, aps["dbg_mean"], aps["dbg_rstd"])
                        if "dbg_mean" in aps else None)
                    P.wait("scalar", "p_vector", v_res)
                P.flush()
    P.wait("gpsimd", "gd", P.val("gd"))
    P.flush()


L0F_KEYS = ("consts", "ones128", "identity", "rw_T", "sh_gate_T", "sh_up_T",
            "sh_down_T", "r_gate_T", "r_up_T", "r_down_T")


def build_l0f(H):
    nc = bass.Bass("TRN2", target_bir_lowering=False)
    aps = {}
    for name in L0F_KEYS:
        aps[name] = nc.dram_tensor(name, list(H[name].shape), F32,
                                   kind="ExternalInput").ap()
    aps["x_in"] = nc.dram_tensor("x_in", [D, T], F32, kind="ExternalInput").ap()
    aps["x_out"] = nc.dram_tensor("x_out", [D, T], F32, kind="ExternalOutput").ap()
    with nc.Block() as block:
        P = Prog(nc, block)
        emit_l0f(P, aps)
    return nc


def emit_l1f(P, aps):
    nc = P.nc
    x_in, x_out = aps["x_in"], aps["x_out"]
    with Scope(nc) as S:
        cpool = S.sb("cpool", [128, 8], F32)
        ones_f = S.sb("onesf", [128, 128], F32)
        ones_bf = S.sb("onesbf", [128, 128], BF16)
        st_mean = S.sb("stmean", [128, 512], F32)
        st_msq = S.sb("stmsq", [128, 512], F32)
        st_var = S.sb("stvar", [128, 512], F32)
        big_bf = S.sb("bigbf", [128, NDC * T], BF16)
        acc = S.sb("acc", [128, NDC * 1024], F32)
        gsT = S.sb("gsT", [128, 44], F32)
        usT = S.sb("usT", [128, 44], F32)
        dsT = S.sb("dsT", [128, 16], F32)
        pspool = S.ps("pspool", [128, 4096], F32)
        stats = {"mean": st_mean, "msq": st_msq, "var": st_var, "rstd": st_var}
        xln = big_bf

        _load(P, cpool[:], aps["consts"])
        _load(P, ones_f[:], aps["ones128"])
        _load(P, gsT[:], aps["gate_s"])
        _load(P, usT[:], aps["up_s"])
        ldc = _load(P, dsT[:], aps["down_s"])
        P.wait("vector", "ld", ldc)
        P.add("vector", lambda e: e.tensor_copy(ones_bf[:], ones_f[:]),
              inc=("p_vector", 1))

        with Scope(nc) as S1:
            xf = S1.sb("xf", [128, NDC * 512], F32)
            xb = S1.sb("xb", [128, NDC * 512], BF16)
            xsq = S1.sb("xsq", [128, NDC * 512], BF16)
            emit_ln_in(P, x_in, xln, xf, xb, xsq, ones_bf, stats, cpool,
                       pspool[:, 0:512], pspool[:, 512:1024])
            P.flush()

        for half in range(2):
            soft_barrier(P)
            with Scope(nc) as SA:
                sync_fence(P)
                stage = SA.sb("stageF%d" % half, [128, 4096], F32)
                wbuf = SA.sb("wbufF%d" % half, [128, 8192], BF16)
                hT = SA.sb("hT%d" % half, [128, 4096], BF16)
                hsil = SA.sb("hsil%d" % half, [128, 4096], BF16)
                Bf = dict(xln=xln, acc=acc, stage=stage, wbuf=wbuf, hT=hT,
                          hsil=hsil, pspool=pspool)
                emit_ffn_pass(P, Bf, half, aps["f_gate_T"], aps["f_up_T"],
                              aps["f_down_T"], 11, first_pass=True,
                              gs=gsT, us=usT, ds=dsT)
                P.flush()
            soft_barrier(P)
            with Scope(nc) as SB:
                sync_fence(P)
                y_tc = SB.sb("y_tc%d" % half, [128, NDC * 512], BF16)
                xf2 = SB.sb("xf2_%d" % half, [128, NDC * 512], F32)
                xsq_dc = SB.sb("xsqdc%d" % half, [128, 1024], BF16)
                acc3 = acc[:].rearrange("p (o t) -> p o t", t=1024)
                for tcl in range(2):
                    tc = half * 2 + tcl
                    s_cast = P.add("scalar", lambda e, tcl=tcl: e.copy(
                        _v3(y_tc), acc3[:, :, tcl * 512:(tcl + 1) * 512]),
                        inc=("p_scalar", 1))
                    P.wait("tensor", "p_scalar", s_cast)
                    st, v_res = emit_out_ln_residual(
                        P, tc, x_in, x_out, y_tc, xf2, xsq_dc, ones_bf,
                        stats, cpool, pspool[:, 3072:3584], pspool[:, 3584:4096])
                    P.wait("scalar", "p_vector", v_res)
                P.flush()
    P.wait("gpsimd", "gd", P.val("gd"))
    P.flush()


L1F_KEYS = ("consts", "ones128", "f_gate_T", "f_up_T", "f_down_T",
            "gate_s", "up_s", "down_s")


def build_l1f(H):
    nc = bass.Bass("TRN2", target_bir_lowering=False)
    aps = {}
    for name in L1F_KEYS:
        aps[name] = nc.dram_tensor(name, list(H[name].shape), F32,
                                   kind="ExternalInput").ap()
    aps["x_in"] = nc.dram_tensor("x_in", [D, T], F32, kind="ExternalInput").ap()
    aps["x_out"] = nc.dram_tensor("x_out", [D, T], F32, kind="ExternalOutput").ap()
    with nc.Block() as block:
        P = Prog(nc, block)
        emit_l1f(P, aps)
    return nc



def _l1a_projections(P, aps, B):
    nc = P.nc
    xln, tmpT, BC, wdt, seln_bf, stage, pspool = (
        B["xln"], B["tmpT"], B["BC"], B["wdt"], B["seln_bf"], B["stage"], B["pspool"])
    wbc, wdd = B["wbc"], B["wdd"]
    sync_fence(P)
    ld = _load(P, stage[:, 0:512].rearrange("p (c i) -> p c i", i=32),
               aps["WBC"].rearrange("(c p) i -> p c i", p=128))
    P.wait("vector", "ld", ld)
    P.add("vector", lambda e: e.tensor_copy(wbc[:], stage[:, 0:512]),
          inc=("p_vector", 1))
    ld = _load(P, stage[:, 1024:3072].rearrange("p (c i) -> p c i", i=128),
               aps["dt_down"].rearrange("(c p) i -> p c i", p=128))
    P.wait("vector", "ld", ld)
    P.add("vector", lambda e: e.tensor_copy(wdd[:], stage[:, 1024:3072]),
          inc=("p_vector", 1))
    P.wait("sync", "p_vector", P.val("p_vector"))
    ld = _load(P, stage[:, 0:1024], aps["dtw_T"][:, 0:1024])
    ld = _load(P, stage[:, 1024:2048], aps["dtw_T"][:, 1024:2048])
    P.wait("vector", "ld", ld)
    P.add("vector", lambda e: e.tensor_copy(wdt[:], stage[:, 0:2048]),
          inc=("p_vector", 1))
    P.wait("sync", "p_vector", P.val("p_vector"))
    ld = _load(P, stage[0:32, 0:4096], aps["seln"])
    P.wait("vector", "ld", ld)
    v_all = P.add("vector", lambda e: e.tensor_copy(seln_bf[:], stage[0:32, 0:4096]),
                  inc=("p_vector", 1))
    P.wait("tensor", "p_vector", v_all)
    ev = 0
    for tc in range(NTC):
        P.wait("tensor", "p_scalar", ev)
        def mmbc(e, tc=tc):
            for dc in range(NDC):
                ins = e.matmul(pspool[0:32, 0:512],
                               wbc[:, dc * 32:(dc + 1) * 32],
                               xln[:, dc * T + tc * 512:dc * T + (tc + 1) * 512],
                               start=(dc == 0), stop=(dc == NDC - 1))
            return ins
        t_bc = P.add("tensor", mmbc, inc=("p_tensor", 1))
        P.wait("scalar", "p_tensor", t_bc)
        ev = P.add("scalar", lambda e, tc=tc: e.copy(
            BC[:, tc * 512:(tc + 1) * 512], pspool[0:32, 0:512]),
            inc=("p_scalar", 1))
        P.wait("tensor", "p_scalar", ev)
        def mmtmp(e, tc=tc):
            for dc in range(NDC):
                ins = e.matmul(pspool[:, 512:1024],
                               wdd[:, dc * 128:(dc + 1) * 128],
                               xln[:, dc * T + tc * 512:dc * T + (tc + 1) * 512],
                               start=(dc == 0), stop=(dc == NDC - 1))
            return ins
        t_tm = P.add("tensor", mmtmp, inc=("p_tensor", 1))
        P.wait("scalar", "p_tensor", t_tm)
        ev = P.add("scalar", lambda e, tc=tc: e.copy(
            tmpT[:, tc * 512:(tc + 1) * 512], pspool[:, 512:1024]),
            inc=("p_scalar", 1))
    P.wait("tensor", "p_scalar", ev)


def _l1a_scan_tc(P, B, tc):
    nc = P.nc
    xln, tmpT, BC, wdt, seln_bf = B["xln"], B["tmpT"], B["BC"], B["wdt"], B["seln_bf"]
    A_all, hstate, dtb, cpool = B["A_all"], B["hstate"], B["dtb"], B["cpool"]
    y_acc, pspool = B["y_acc"], B["pspool"]
    Bball, Cball = B["Bball"], B["Cball"]
    spa, spb, delta, du = B["spa"], B["spb"], B["delta"], B["du"]
    dAt, ut, ht, yt = B["dAt"], B["ut"], B["ht"], B["yt"]
    psB2 = pspool[:, 0:512]
    psC2 = pspool[:, 512:1024]
    psDt = pspool[:, 1024:1536]
    tsl = slice(tc * 512, (tc + 1) * 512)

    ev = P.val("p_scalar")
    for n in range(16):
        P.wait("tensor", "p_scalar", ev)
        def mmb(e, n=n):
            e.matmul(psB2, seln_bf[:, n * 128:(n + 1) * 128], BC[:, tsl],
                     start=True, stop=True)
            return e.matmul(psC2, seln_bf[:, (16 + n) * 128:(17 + n) * 128],
                            BC[:, tsl], start=True, stop=True)
        t_b = P.add("tensor", mmb, inc=("p_tensor", 1))
        P.wait("scalar", "p_tensor", t_b)
        def evb(e, n=n):
            e.copy(Bball[:, n * 512:(n + 1) * 512], psB2)
            return e.copy(Cball[:, n * 512:(n + 1) * 512], psC2)
        ev = P.add("scalar", evb, inc=("p_scalar", 1))

    for dc in range(16):
        P.wait("tensor", "p_scalar", P.val("p_scalar"))
        P.wait("tensor", "p_vector", P.val("p_vector"))
        t_dt = P.add("tensor", lambda e, dc=dc: e.matmul(
            psDt, wdt[:, dc * 128:(dc + 1) * 128], tmpT[:, tsl],
            start=True, stop=True), inc=("p_tensor", 1))
        P.wait("scalar", "p_tensor", t_dt)
        P.addw("scalar", lambda e, dc=dc: e.activation(
            spa[:], psDt, AF.Abs, bias=dtb[:, dc:dc + 1]))
        P.addw("scalar", lambda e: e.activation(spa[:], spa[:], AF.Exp,
                                                scale=cpool[:, 4:5]))
        P.addw("scalar", lambda e: e.activation(spa[:], spa[:], AF.Ln,
                                                bias=cpool[:, 2:3]))
        s_sp = P.add("scalar", lambda e, dc=dc: e.activation(
            spb[:], psDt, AF.Relu, bias=dtb[:, dc:dc + 1]), inc=("p_scalar", 1))
        P.wait("vector", "p_scalar", s_sp)
        P.addw("vector", lambda e: e.tensor_add(delta[:], spa[:], spb[:]))
        v_du = P.addw("vector", lambda e, dc=dc: e.tensor_mul(
            delta[:] if False else du[:], delta[:],
            xln[:, dc * T + tc * 512:dc * T + (tc + 1) * 512]))
        for n in range(16):
            col = dc * 16 + n
            P.wait("scalar", "p_vector", P.val("p_vector"))
            s_dA = P.add("scalar", lambda e, col=col: e.activation(
                dAt[:], delta[:], AF.Exp, scale=A_all[:, col:col + 1]),
                inc=("p_scalar", 1))
            P.addw("vector", lambda e, n=n: e.tensor_mul(
                ut[:], du[:], Bball[:, n * 512:(n + 1) * 512]))
            P.wait("vector", "p_scalar", s_dA)
            P.addw("vector", lambda e, col=col: e.tensor_tensor_scan(
                ht[:], dAt[:], ut[:], hstate[:, col:col + 1], ALU.mult, ALU.add))
            P.addw("vector", lambda e, col=col: e.tensor_copy(
                hstate[:, col:col + 1], ht[:, 511:512]))
            if n == 0:
                P.addw("vector", lambda e, dc=dc, n=n: e.tensor_mul(
                    y_acc[:, dc * 512:(dc + 1) * 512], ht[:],
                    Cball[:, n * 512:(n + 1) * 512]))
            else:
                P.addw("vector", lambda e, n=n: e.tensor_mul(
                    yt[:], ht[:], Cball[:, n * 512:(n + 1) * 512]))
                P.addw("vector", lambda e, dc=dc: e.tensor_add(
                    y_acc[:, dc * 512:(dc + 1) * 512],
                    y_acc[:, dc * 512:(dc + 1) * 512], yt[:]))


def _l1a_outproj(P, aps, B, tc, x_in, x_out):
    nc = P.nc
    xln, y_acc, pspool = B["xln"], B["y_acc"], B["pspool"]
    stage, wop, y_out = B["stage2"], B["wop"], B["y_out"]
    opb, Dv, cpool, ones_bf, stats = B["opb"], B["Dv"], B["cpool"], B["ones_bf"], B["stats"]
    xf2, xsq_dc = B["xf2"], B["xsq_dc"]
    psO = [pspool[:, 2048:2560], pspool[:, 2560:3072]]
    ybf = B["ybf"]
    s_yb = P.add("vector", lambda e: e.tensor_copy(ybf[:], y_acc[:]),
                 inc=("p_vector", 1))
    ev_prev = [0, 0]
    relu_guard = 0
    for p in range(8):
        for ch in range(4):
            sync_fence(P)
            ld = _load(P, stage[:].rearrange("p (c i) -> p c i", i=256),
                       aps["op_T"][512 * ch:512 * (ch + 1), p * 256:(p + 1) * 256]
                       .rearrange("(c p2) i -> p2 c i", p2=128))
            P.wait("vector", "ld", ld)
            P.wait("vector", "p_tensor", P.val("p_tensor"))
            P.add("vector", lambda e, ch=ch: e.tensor_copy(
                wop[:, ch * 4 * 256:(ch + 1) * 4 * 256], stage[:]),
                inc=("p_vector", 1))
        v_wp = P.val("p_vector")
        P.wait("tensor", "p_vector", v_wp)
        for ot in range(2):
            oc = 2 * p + ot
            pp = oc % 2
            P.wait("tensor", "p_vector", ev_prev[pp])
            def mko(e, ot=ot, pp=pp):
                for dc in range(NDC):
                    ins = e.matmul(
                        psO[pp],
                        wop[:, dc * 256 + ot * 128:dc * 256 + (ot + 1) * 128],
                        ybf[:, dc * 512:(dc + 1) * 512],
                        start=(dc == 0), stop=(dc == NDC - 1))
                return ins
            t_o = P.add("tensor", mko, inc=("p_tensor", 1))
            P.wait("scalar", "p_tensor", t_o)
            P.wait("scalar", "p_vector", relu_guard)
            s_r = P.add("scalar", lambda e, pp=pp, oc=oc: e.activation(
                B["relu_t"][:], psO[pp], AF.Relu, bias=opb[:, oc:oc + 1]),
                inc=("p_scalar", 1))
            P.wait("vector", "p_scalar", s_r)
            ev_prev[pp] = P.add("vector", lambda e, oc=oc: e.scalar_tensor_tensor(
                y_out[:, oc * 512:(oc + 1) * 512],
                xln[:, oc * T + tc * 512:oc * T + (tc + 1) * 512],
                Dv[:, oc:oc + 1], B["relu_t"][:], ALU.mult, ALU.add),
                inc=("p_vector", 1))
            relu_guard = ev_prev[pp]
    P.wait("scalar", "p_vector", P.val("p_vector"))
    st, v_y = emit_out_ln_residual(
        P, tc, x_in, x_out, y_out, xf2, xsq_dc, ones_bf,
        stats, cpool, pspool[:, 3072:3584], pspool[:, 3584:4096])
    return v_y


def emit_l1a(P, aps):
    nc = P.nc
    x_in, x_out = aps["x_in"], aps["x_out"]
    with Scope(nc) as S:
        cpool = S.sb("cpool", [128, 8], F32)
        ones_f = S.sb("onesf", [128, 128], F32)
        ones_bf = S.sb("onesbf", [128, 128], BF16)
        st_mean = S.sb("stmean", [128, 512], F32)
        st_msq = S.sb("stmsq", [128, 512], F32)
        st_var = S.sb("stvar", [128, 512], F32)
        big_bf = S.sb("bigbf", [128, NDC * T], BF16)
        y_acc = S.sb("y_acc", [128, 16 * 512], F32)
        tmpT = S.sb("tmpT", [128, T], BF16)
        BC = S.sb("BC", [32, T], BF16)
        wdt = S.sb("wdt", [128, T], BF16)
        seln_bf = S.sb("selnbf", [32, 4096], BF16)
        A_all = S.sb("A_all", [128, 256], F32)
        hstate = S.sb("hstate", [128, 256], F32)
        dtb = S.sb("dtb", [128, 16], F32)
        opb = S.sb("opb", [128, 16], F32)
        Dv = S.sb("Dv", [128, 16], F32)
        pspool = S.ps("pspool", [128, 4096], F32)
        stats = {"mean": st_mean, "msq": st_msq, "var": st_var, "rstd": st_var}
        xln = big_bf

        _load(P, cpool[:], aps["consts"])
        _load(P, ones_f[:], aps["ones128"])
        _load(P, dtb[:], aps["dt_bias"])
        _load(P, opb[:], aps["op_bias"])
        _load(P, Dv[:], aps["Dvec"])
        ldc = _load(P, A_all[:], aps["A_log"])
        P.wait("vector", "ld", ldc)
        P.add("vector", lambda e: e.tensor_copy(ones_bf[:], ones_f[:]),
              inc=("p_vector", 1))
        P.add("vector", lambda e: e.memset(hstate[:], 0.0), inc=("p_vector", 1))
        v_al = P.val("p_vector")
        P.wait("scalar", "p_vector", v_al)
        s_a = P.add("scalar", lambda e: e.activation(A_all[:], A_all[:], AF.Exp),
                    inc=("p_scalar", 1))
        P.wait("vector", "p_scalar", s_a)
        P.addw("vector", lambda e: e.tensor_scalar_mul(A_all[:], A_all[:], -1.0))

        with Scope(nc) as S1:
            xf = S1.sb("xf", [128, NDC * 512], F32)
            xb = S1.sb("xb", [128, NDC * 512], BF16)
            xsq = S1.sb("xsq", [128, NDC * 512], BF16)
            emit_ln_in(P, x_in, xln, xf, xb, xsq, ones_bf, stats, cpool,
                       pspool[:, 0:512], pspool[:, 512:1024])
            P.flush()

        with Scope(nc) as S2:
            stage = S2.sb("stageP", [128, 4096], F32)
            wbc = S2.sb("wbc", [128, 512], BF16)
            wdd = S2.sb("wdd", [128, 2048], BF16)
            _l1a_projections(P, aps, dict(
                xln=xln, tmpT=tmpT, BC=BC, wdt=wdt, seln_bf=seln_bf,
                stage=stage, wbc=wbc, wdd=wdd, pspool=pspool))
            P.flush()

        for tc in range(NTC):
            soft_barrier(P)
            with Scope(nc) as SS:
                Bball = SS.sb("Bball%d" % tc, [128, 16 * 512], BF16)
                Cball = SS.sb("Cball%d" % tc, [128, 16 * 512], BF16)
                spa = SS.sb("spa%d" % tc, [128, 512], F32)
                spb = SS.sb("spb%d" % tc, [128, 512], F32)
                delta = SS.sb("delta%d" % tc, [128, 512], F32)
                du = SS.sb("du%d" % tc, [128, 512], F32)
                dAt = SS.sb("dAt%d" % tc, [128, 512], F32)
                ut = SS.sb("ut%d" % tc, [128, 512], F32)
                ht = SS.sb("ht%d" % tc, [128, 512], F32)
                yt = SS.sb("yt%d" % tc, [128, 512], F32)
                _l1a_scan_tc(P, dict(
                    xln=xln, tmpT=tmpT, BC=BC, wdt=wdt, seln_bf=seln_bf,
                    A_all=A_all, hstate=hstate, dtb=dtb, cpool=cpool,
                    y_acc=y_acc, pspool=pspool, Bball=Bball, Cball=Cball,
                    spa=spa, spb=spb, delta=delta, du=du, dAt=dAt, ut=ut,
                    ht=ht, yt=yt), tc)
                P.flush()
            soft_barrier(P)
            with Scope(nc) as SO:
                stage2 = SO.sb("stageO%d" % tc, [128, 1024], F32)
                wop = SO.sb("wop%d" % tc, [128, NDC * 256], BF16)
                ybf = SO.sb("ybf%d" % tc, [128, 16 * 512], BF16)
                y_out = SO.sb("yout%d" % tc, [128, NDC * 512], BF16)
                relu_t = SO.sb("relu%d" % tc, [128, 512], BF16)
                xf2 = SO.sb("xf2_%d" % tc, [128, NDC * 512], F32)
                xsq_dc = SO.sb("xsqdc%d" % tc, [128, 1024], BF16)
                _l1a_outproj(P, aps, dict(
                    xln=xln, y_acc=y_acc, pspool=pspool, stage2=stage2,
                    wop=wop, ybf=ybf, y_out=y_out, relu_t=relu_t,
                    opb=opb, Dv=Dv, cpool=cpool, ones_bf=ones_bf, stats=stats,
                    xf2=xf2, xsq_dc=xsq_dc), tc, x_in, x_out)
                P.flush()
    P.wait("gpsimd", "gd", P.val("gd"))
    P.flush()


L1A_KEYS = ("consts", "ones128", "WBC", "dt_down", "dtw_T", "dt_bias",
            "A_log", "Dvec", "op_T", "op_bias", "seln")


def build_l1a(H):
    nc = bass.Bass("TRN2", target_bir_lowering=False)
    aps = {}
    for name in L1A_KEYS:
        aps[name] = nc.dram_tensor(name, list(H[name].shape), F32,
                                   kind="ExternalInput").ap()
    aps["x_in"] = nc.dram_tensor("x_in", [D, T], F32, kind="ExternalInput").ap()
    aps["x_out"] = nc.dram_tensor("x_out", [D, T], F32, kind="ExternalOutput").ap()
    with nc.Block() as block:
        P = Prog(nc, block)
        emit_l1a(P, aps)
    return nc


ALL_KEYS = tuple(dict.fromkeys(L0A_KEYS + L0F_KEYS + L1A_KEYS + L1F_KEYS))


def build_fused(H):
    nc = bass.Bass("TRN2", target_bir_lowering=False)
    aps = {}
    for name in ALL_KEYS:
        aps[name] = nc.dram_tensor(name, list(H[name].shape), F32,
                                   kind="ExternalInput").ap()
    x_in = nc.dram_tensor("x_in", [D, T], F32, kind="ExternalInput").ap()
    x_out = nc.dram_tensor("x_out", [D, T], F32, kind="ExternalOutput").ap()
    xs_a = nc.dram_tensor("xs_a", [D, T], F32, kind="Internal").ap()
    xs_b = nc.dram_tensor("xs_b", [D, T], F32, kind="Internal").ap()
    with nc.Block() as block:
        P = Prog(nc, block)
        for emit, xin, xout in ((emit_l0a, x_in, xs_a),
                                (emit_l0f, xs_a, xs_b),
                                (emit_l1a, xs_b, xs_a),
                                (emit_l1f, xs_a, x_out)):
            a = dict(aps)
            a["x_in"], a["x_out"] = xin, xout
            emit(P, a)
            soft_barrier(P)
            sync_fence(P)
        P.wait("gpsimd", "gd", P.val("gd"))
        P.flush()
    return nc


FUSED = True


def kernel(**inputs):
    if FUSED:
        return kernel_fused(**inputs)
    return kernel_unfused(**inputs)


def kernel_fused(**inputs):
    H = _prep_host(inputs)
    x = np.asarray(inputs["L0_attn_x"], np.float32)
    ncores = 8
    nc = build_fused(H)
    common = {k: H[k] for k in ALL_KEYS}
    in_maps = [dict(common, x_in=np.ascontiguousarray(x[b].T))
               for b in range(ncores)]
    res = run_bass_kernel_spmd(nc, in_maps, core_ids=list(range(ncores)))
    out = np.stack([np.asarray(r["x_out"]).T for r in res.results])
    return out.astype(np.float32)


# version 04
# speedup vs baseline: 1.5049x; 1.5049x over previous
import numpy as np
import concourse.bass as bass
import concourse.mybir as mybir
from concourse.bass_utils import run_bass_kernel_spmd

F32 = mybir.dt.float32
BF16 = mybir.dt.bfloat16
AF = mybir.ActivationFunctionType
ALU = mybir.AluOpType
AX = mybir.AxisListType

D = 2048
T = 2048
NDC = D // 128
NTC = T // 512
EPS = 1e-5


def _prep_host(inputs):
    H = {}
    f32 = np.float32

    H["identity"] = np.eye(128, dtype=f32)
    H["ones128"] = np.ones((128, 128), dtype=f32)
    half = 32
    inv_freq = 1.0 / (10000.0 ** (np.arange(half, dtype=np.float64) * 2.0 / 64.0))
    pos = np.arange(T, dtype=np.float64)
    ang = pos[None, :] * inv_freq[:, None]
    cos = np.cos(ang).astype(f32)
    sin = np.sin(ang).astype(f32)
    H["cos"] = cos
    H["sin"] = sin
    am = np.zeros((128, 4, 512), dtype=f32)
    for p in range(4):
        for b in range(4):
            blk = am[:, p, 128 * b:128 * (b + 1)]
            if b > p:
                blk[:] = -30000.0
            elif b == p:
                r = np.arange(128)
                blk[:] = np.where(r[None, :] <= r[:, None], 0.0, -30000.0)
    H["smask"] = am.reshape(128, 4 * 512)
    pm = np.zeros((128, 4, 512), dtype=f32)
    for p in range(4):
        for b in range(4):
            blk = pm[:, p, 128 * b:128 * (b + 1)]
            if b < p:
                blk[:] = 0.0
            elif b > p:
                blk[:] = 1.0
            else:
                r = np.arange(128)
                blk[:] = np.where(r[:, None] <= r[None, :], 1.0, 0.0)
    H["ptmask"] = pm.reshape(128, 4 * 512)
    cst = np.zeros((128, 8), dtype=f32)
    cst[:, 0] = EPS
    cst[:, 1] = 1.0 / D
    cst[:, 2] = 1.0
    cst[:, 3] = 1.0 / np.sqrt(128.0)
    cst[:, 4] = -1.0
    cst[:, 5] = -0.5
    H["consts"] = cst
    seln = np.zeros((32, 32 * 128), dtype=f32)
    for j in range(32):
        seln[j, j * 128:(j + 1) * 128] = 1.0
    H["seln"] = seln

    W_uq = np.asarray(inputs["L0_attn_W_uq"], f32).reshape(512, 16, 64)
    W_qr = np.asarray(inputs["L0_attn_W_qr"], f32).reshape(512, 16, 64)
    W_uk = np.asarray(inputs["L0_attn_W_uk"], f32).reshape(512, 16, 64)
    W_kr = np.asarray(inputs["L0_attn_W_kr"], f32).reshape(512, 16, 64)
    q_comb = np.concatenate([W_qr[:, :, :32], W_qr[:, :, 32:], W_uq], axis=2)
    k_comb = np.concatenate([W_kr[:, :, :32], W_kr[:, :, 32:], W_uk], axis=2)
    H["Wq_comb"] = np.ascontiguousarray(q_comb.reshape(512, 2048))
    H["Wk_comb"] = np.ascontiguousarray(k_comb.reshape(512, 2048))
    H["W_dkv"] = np.asarray(inputs["L0_attn_W_dkv"], f32)
    H["W_dq"] = np.asarray(inputs["L0_attn_W_dq"], f32)
    H["W_uv"] = np.asarray(inputs["L0_attn_W_uv"], f32)
    H["W_out"] = np.asarray(inputs["L0_attn_W_out"], f32)
    H["b_out"] = np.ascontiguousarray(
        np.asarray(inputs["L0_attn_b_out"], f32).reshape(16, 128).T)

    H["rw_T"] = np.ascontiguousarray(np.asarray(inputs["L0_ffn_router_weight"], f32).T)
    H["sh_gate_T"] = np.ascontiguousarray(np.asarray(inputs["L0_ffn_sh_gate"], f32).T)
    H["sh_up_T"] = np.ascontiguousarray(np.asarray(inputs["L0_ffn_sh_up"], f32).T)
    H["sh_down_T"] = np.ascontiguousarray(np.asarray(inputs["L0_ffn_sh_down"], f32).T)
    H["r_gate_T"] = np.ascontiguousarray(np.asarray(inputs["L0_ffn_r_gate"], f32).T)
    H["r_up_T"] = np.ascontiguousarray(np.asarray(inputs["L0_ffn_r_up"], f32).T)
    H["r_down_T"] = np.ascontiguousarray(np.asarray(inputs["L0_ffn_r_down"], f32).T)

    W = np.asarray(inputs["L1_attn_x_proj_weight"], f32).T
    H["dt_down"] = np.ascontiguousarray(W[:, :128])
    H["WBC"] = np.ascontiguousarray(W[:, 128:160])
    H["dtw_T"] = np.ascontiguousarray(np.asarray(inputs["L1_attn_dt_proj_weight"], f32).T)
    H["dt_bias"] = np.ascontiguousarray(
        np.asarray(inputs["L1_attn_dt_proj_bias"], f32).reshape(16, 128).T)
    H["A_log"] = np.ascontiguousarray(
        np.asarray(inputs["L1_attn_A_log"], f32).reshape(16, 128, 16).transpose(1, 0, 2).reshape(128, 16 * 16))
    H["Dvec"] = np.ascontiguousarray(
        np.asarray(inputs["L1_attn_D"], f32).reshape(16, 128).T)
    H["op_T"] = np.ascontiguousarray(np.asarray(inputs["L1_attn_out_proj_weight"], f32).T)
    H["op_bias"] = np.ascontiguousarray(
        np.asarray(inputs["L1_attn_out_proj_bias"], f32).reshape(16, 128).T)

    H["f_gate_T"] = np.ascontiguousarray(np.asarray(inputs["L1_ffn_gate_w"], f32).T)
    H["f_up_T"] = np.ascontiguousarray(np.asarray(inputs["L1_ffn_up_w"], f32).T)
    H["f_down_T"] = np.ascontiguousarray(np.asarray(inputs["L1_ffn_down_w"], f32).T)
    H["gate_s"] = np.ascontiguousarray(
        np.asarray(inputs["L1_ffn_gate_s"], f32).reshape(44, 128).T)
    H["up_s"] = np.ascontiguousarray(
        np.asarray(inputs["L1_ffn_up_s"], f32).reshape(44, 128).T)
    H["down_s"] = np.ascontiguousarray(
        np.asarray(inputs["L1_ffn_down_s"], f32).reshape(16, 128).T)
    return H



ENGINES = ("sync", "tensor", "scalar", "vector", "gpsimd")


class Prog:

    def __init__(self, nc, block):
        self.nc = nc
        self.block = block
        self.ops = {e: [] for e in ENGINES}
        self.semh = {}
        self.cnt = {}
        self._guards = []

    def sem(self, name):
        if name not in self.semh:
            cm = self.nc.semaphore(name)
            self.semh[name] = cm.__enter__()
            self._guards.append(cm)
            self.cnt[name] = 0
        return self.semh[name]

    def bump(self, name, amt):
        self.sem(name)
        self.cnt[name] += amt
        return self.cnt[name]

    def emit(self, e, fn):
        self.ops[e].append(fn)

    def add(self, e, builder, inc=None):
        if inc is not None:
            name, amt = inc
            tgt = self.bump(name, amt)
            h = self.semh[name]

            def f(eng, builder=builder, h=h, amt=amt):
                builder(eng).then_inc(h, amt)
            self.emit(e, f)
            return tgt
        self.emit(e, lambda eng, builder=builder: builder(eng))
        return None

    def addw(self, e, builder):
        tgt = self.add(e, builder, inc=("p_" + e, 1))
        self.wait(e, "p_" + e, tgt)
        return tgt

    def wait(self, e, name, tgt):
        h = self.sem(name)
        self.emit(e, lambda eng, h=h, tgt=tgt: eng.wait_ge(h, tgt))

    def val(self, name):
        return self.cnt.get(name, 0)

    def flush(self):
        for e in ENGINES:
            fns = self.ops[e]
            if not fns:
                continue
            self.ops[e] = []

            def chain(eng, fns=fns):
                for fn in fns:
                    fn(eng)
            getattr(self.block, e)(chain)


class Scope:

    _n = 0

    def __init__(self, nc):
        self.nc = nc
        self.cms = []
        Scope._n += 1
        self.uid = Scope._n

    def __enter__(self):
        return self

    def sb(self, name, shape, dtype):
        cm = self.nc.sbuf_tensor("%s_u%d" % (name, self.uid), shape, dtype)
        t = cm.__enter__()
        self.cms.append(cm)
        return t

    def ps(self, name, shape, dtype=F32):
        cm = self.nc.psum_tensor("%s_u%d" % (name, self.uid), shape, dtype)
        t = cm.__enter__()
        self.cms.append(cm)
        return t

    def __exit__(self, *a):
        for cm in reversed(self.cms):
            cm.__exit__(None, None, None)
        return False


def _load(P, dst, src, sem="ld"):
    tgt = P.bump(sem, 16)
    h = P.semh[sem]
    P.emit("sync", lambda eng, dst=dst, src=src, h=h: eng.dma_start(out=dst, in_=src).then_inc(h, 16))
    return tgt


def _store(P, dst, src, sem="gd"):
    tgt = P.bump(sem, 16)
    h = P.semh[sem]
    P.emit("gpsimd", lambda eng, dst=dst, src=src, h=h: eng.dma_start(out=dst, in_=src).then_inc(h, 16))
    return tgt


def chunk_dram(ap, tc, width=512):
    return ap[:, tc * width:(tc + 1) * width].rearrange("(c p) t -> p c t", p=128)


def _v3(tile, w=512):
    return tile[:].rearrange("p (c t) -> p c t", t=w)



SC = 1.0 / np.sqrt(128.0)


def emit_ln_in(P, x_in_ap, xln, xf, xb, xsq, ones_bf, stats, cpool, psA, psB):
    nc = P.nc
    mean, msq, var, rstd = stats["mean"], stats["msq"], stats["var"], stats["rstd"]
    v_norm_prev = P.val("p_vector")
    v_var_prev = 0
    g_prev = P.val("gd")
    for tc in range(NTC):
        P.wait("sync", "p_vector", v_norm_prev)
        ld = _load(P, _v3(xf), chunk_dram(x_in_ap, tc))
        P.wait("vector", "ld", ld)
        v_cast = P.add("vector", lambda e, xb=xb, xf=xf: e.tensor_copy(xb[:], xf[:]),
                       inc=("p_vector", 1))
        P.wait("scalar", "p_vector", v_cast)
        s_sq = P.add("scalar", lambda e, xsq=xsq, xb=xb: e.square(xsq[:], xb[:]),
                     inc=("p_scalar", 1))
        P.wait("tensor", "p_vector", v_var_prev)
        P.wait("tensor", "p_scalar", s_sq)
        def mk_stats(e, xb=xb, xsq=xsq, psA=psA, psB=psB, ones_bf=ones_bf):
            xb3 = _v3(xb)
            xq3 = _v3(xsq)
            for dc in range(NDC):
                e.matmul(psA[:], ones_bf[:], xb3[:, dc, :], start=(dc == 0), stop=(dc == NDC - 1))
            for dc in range(NDC):
                ins = e.matmul(psB[:], ones_bf[:], xq3[:, dc, :], start=(dc == 0), stop=(dc == NDC - 1))
            return ins
        t_stats = P.add("tensor", mk_stats, inc=("p_tensor", 1))
        P.wait("vector", "p_tensor", t_stats)
        P.addw("vector", lambda e: e.tensor_scalar_mul(mean[:], psA[:], 1.0 / D))
        P.addw("vector", lambda e: e.tensor_mul(msq[:], mean[:], mean[:]))
        v_var = P.add("vector",
                      lambda e: e.scalar_tensor_tensor(var[:], psB[:], 1.0 / D,
                                                       msq[:], ALU.mult, ALU.subtract),
                      inc=("p_vector", 1))
        v_var_prev = v_var
        P.wait("scalar", "p_vector", v_var)
        P.addw("scalar", lambda e: e.activation(msq[:], var[:], AF.Ln,
                                                bias=cpool[:, 0:1]))
        s_rstd = P.add("scalar",
                       lambda e: e.activation(rstd[:], msq[:], AF.Exp,
                                              scale=cpool[:, 5:6]),
                       inc=("p_scalar", 1))
        P.wait("vector", "p_scalar", s_rstd)
        xf3 = _v3(xf)
        xl3 = xln[:].rearrange("p (c t) -> p c t", t=T)
        for dc in range(NDC):
            P.addw("vector", lambda e, dc=dc: e.tensor_sub(msq[:], xf3[:, dc, :], mean[:]))
            last = P.add("vector",
                         lambda e, dc=dc, tc=tc: e.tensor_mul(
                             xl3[:, dc, tc * 512:(tc + 1) * 512], msq[:], rstd[:]),
                         inc=("p_vector", 1) if dc == NDC - 1 else None)
            if dc < NDC - 1:
                P.wait("vector", "p_vector", P.val("p_vector"))
        v_norm_prev = last
    return v_norm_prev


def emit_out_ln_residual(P, tc, x_in_ap, x_out_ap, y_tc, xf, xsq_dc, ones_bf,
                         stats, cpool, psA, psB, dbg_stats=None):
    mean, msq, var, rstd = stats["mean"], stats["msq"], stats["var"], stats["rstd"]
    y3 = _v3(y_tc)
    xq3 = xsq_dc[:].rearrange("p (c t) -> p c t", t=512)
    t_used = [0, 0]
    sq_done = [0, 0]
    for g in range(2):
        P.wait("scalar", "p_tensor", t_used[g - 1] if g else 0)
        def sq(e, g=g):
            return e.square(xsq_dc[:], y3[:, g * 8:(g + 1) * 8, :])
        sq_done[g] = P.add("scalar", sq, inc=("p_scalar", 1))
        P.wait("tensor", "p_scalar", sq_done[g])
        def mm(e, g=g):
            for j in range(8):
                dc = g * 8 + j
                e.matmul(psA[:], ones_bf[:], y3[:, dc, :],
                         start=(dc == 0), stop=(dc == NDC - 1))
            for j in range(8):
                dc = g * 8 + j
                ins = e.matmul(psB[:], ones_bf[:], xq3[:, j, :],
                               start=(dc == 0), stop=(dc == NDC - 1))
            return ins
        t_used[g] = P.add("tensor", mm, inc=("p_tensor", 1))
    t_stats = t_used[1]
    P.wait("vector", "p_tensor", t_stats)
    P.addw("vector", lambda e: e.tensor_scalar_mul(mean[:], psA[:], 1.0 / D))
    P.addw("vector", lambda e: e.tensor_mul(msq[:], mean[:], mean[:]))
    v_var = P.add("vector", lambda e: e.scalar_tensor_tensor(
        var[:], psB[:], 1.0 / D, msq[:], ALU.mult, ALU.subtract),
        inc=("p_vector", 1))
    P.wait("scalar", "p_vector", v_var)
    P.addw("scalar", lambda e: e.activation(msq[:], var[:], AF.Ln, bias=cpool[:, 0:1]))
    s_rstd = P.add("scalar", lambda e: e.activation(rstd[:], msq[:], AF.Exp,
                                                    scale=cpool[:, 5:6]),
                   inc=("p_scalar", 1))
    if dbg_stats is not None and tc == dbg_stats[0]:
        P.wait("gpsimd", "p_scalar", s_rstd)
        _store(P, dbg_stats[1], mean[:])
        _store(P, dbg_stats[2], rstd[:])
        P.wait("sync", "gd", P.val("gd"))
    sync_fence(P)
    xf3 = xf[:].rearrange("p (c t) -> p c t", t=512)
    st = 0
    for g in range(2):
        P.wait("sync", "gd", st if isinstance(st, int) else 0)
        if g == 1:
            P.wait("sync", "gd", st)
        ld = _load(P, xf3, chunk_dram(x_in_ap, tc)[:, g * 8:(g + 1) * 8, :])
        P.wait("vector", "ld", ld)
        if g == 0:
            P.wait("vector", "p_scalar", s_rstd)
        for j in range(8):
            dc = g * 8 + j
            P.addw("vector", lambda e, dc=dc: e.tensor_sub(msq[:], y3[:, dc, :], mean[:]))
            P.addw("vector", lambda e: e.tensor_mul(msq[:], msq[:], rstd[:]))
            P.add("vector", lambda e, j=j: e.tensor_add(
                xf3[:, j, :], xf3[:, j, :], msq[:]),
                inc=("p_vector", 1) if j == 7 else None)
        v_res = P.val("p_vector")
        P.wait("gpsimd", "p_vector", v_res)
        st = _store(P, chunk_dram(x_out_ap, tc)[:, g * 8:(g + 1) * 8, :], xf3)
    return st, v_res




def _l0a_phaseC(P, aps, xln, ckvT, cqT, stage, wbuf, pspool):
    sync_fence(P)
    psC = [pspool[:, 0:512], pspool[:, 512:1024]]
    ev_prev = [0, 0]
    for wkey, dstT in (("W_dkv", ckvT), ("W_dq", cqT)):
        P.wait("sync", "p_vector", P.val("p_vector"))
        ldw = _load(P, stage[:].rearrange("p (c i) -> p c i", i=512),
                    aps[wkey].rearrange("(c p) i -> p c i", p=128))
        P.wait("vector", "ld", ldw)
        v_w = P.add("vector", lambda e, wbuf=wbuf, stage=stage:
                    e.tensor_copy(wbuf[:], stage[:]), inc=("p_vector", 1))
        if wkey == "W_dq" and "dbg_wbuf" in aps:
            P.wait("gpsimd", "p_vector", v_w)
            _store(P, aps["dbg_wbuf"], wbuf[:])
            P.wait("tensor", "gd", P.val("gd"))
        P.wait("tensor", "p_vector", v_w)
        for ct in range(4):
            for tc in range(4):
                g = ct * 4 + tc
                pp = g % 2
                P.wait("tensor", "p_scalar", ev_prev[pp])
                def mk(e, pp=pp, ct=ct, tc=tc, wbuf=wbuf, xln=xln):
                    for dc in range(NDC):
                        ins = e.matmul(
                            psC[pp],
                            wbuf[:, dc * 512 + ct * 128: dc * 512 + (ct + 1) * 128],
                            xln[:, dc * T + tc * 512: dc * T + (tc + 1) * 512],
                            start=(dc == 0), stop=(dc == NDC - 1))
                    return ins
                t_g = P.add("tensor", mk, inc=("p_tensor", 1))
                P.wait("scalar", "p_tensor", t_g)
                def ev(e, pp=pp, ct=ct, tc=tc, dstT=dstT):
                    return e.copy(
                        dstT[:, ct * T + tc * 512: ct * T + (tc + 1) * 512],
                        psC[pp])
                ev_prev[pp] = P.add("scalar", ev, inc=("p_scalar", 1))
        P.wait("vector", "p_tensor", P.val("p_tensor"))
    P.wait("tensor", "p_scalar", P.val("p_scalar"))


def _l0a_loadD(P, aps, cosT, sinT, smask, ptmask, wq, wk, wuv, stage2):
    sync_fence(P)
    ld1 = _load(P, stage2[0:32, 0:2048], aps["cos"])
    ld1 = _load(P, stage2[0:32, 2048:4096], aps["sin"])
    P.wait("vector", "ld", ld1)
    P.add("vector", lambda e: e.tensor_copy(cosT[:], stage2[0:32, 0:2048]))
    v_t = P.add("vector", lambda e: e.tensor_copy(sinT[:], stage2[0:32, 2048:4096]),
                inc=("p_vector", 1))
    _load(P, smask[:], aps["smask"])
    P.wait("sync", "p_vector", v_t)
    ld2 = _load(P, stage2[:, 0:2048], aps["ptmask"])
    P.wait("vector", "ld", ld2)
    v_t = P.add("vector", lambda e: e.tensor_copy(ptmask[:], stage2[:, 0:2048]),
                inc=("p_vector", 1))
    for wkey, dst in (("Wq_comb", wq), ("Wk_comb", wk), ("W_uv", wuv)):
        for ch in range(2):
            P.wait("sync", "p_vector", P.val("p_vector"))
            ldw = _load(P, stage2[:].rearrange("p (c i) -> p c i", i=2048),
                        aps[wkey][256 * ch:256 * (ch + 1), :]
                        .rearrange("(c p) i -> p c i", p=128))
            P.wait("vector", "ld", ldw)
            P.add("vector", lambda e, dst=dst, ch=ch, stage2=stage2:
                  e.tensor_copy(dst[:, ch * 2 * T:(ch + 1) * 2 * T], stage2[:]),
                  inc=("p_vector", 1))


def _l0a_phaseE(P, aps, ctxT, stage3, woutp, y_tc, xf2, xsq_dc, ones_bf, bout,
                stats, cpool, pspool, x_in, x_out):
    sync_fence(P)
    psO = [pspool[:, 0:512], pspool[:, 512:1024]]
    ev_prev = [0, 0]
    v_res_prev = 0
    for tc in range(NTC):
        for p in range(4):
            for ch in range(2):
                P.wait("sync", "p_vector", P.val("p_vector"))
                ldw = _load(P, stage3[:].rearrange("p (c i) -> p c i", i=512),
                            aps["W_out"][1024 * ch:1024 * (ch + 1),
                                         p * 512:(p + 1) * 512]
                            .rearrange("(c p2) i -> p2 c i", p2=128))
                P.wait("vector", "ld", ldw)
                P.wait("vector", "p_tensor", P.val("p_tensor"))
                P.add("vector", lambda e, ch=ch, woutp=woutp, stage3=stage3:
                      e.tensor_copy(
                          woutp[:, ch * 8 * 512:(ch + 1) * 8 * 512], stage3[:]),
                      inc=("p_vector", 1))
            v_wp = P.val("p_vector")
            P.wait("tensor", "p_vector", v_wp)
            for ot in range(4):
                oc = 4 * p + ot
                pp = oc % 2
                P.wait("tensor", "p_scalar", ev_prev[pp])
                def mko(e, ot=ot, pp=pp, tc=tc, woutp=woutp, ctxT=ctxT):
                    for dc in range(NDC):
                        ins = e.matmul(
                            psO[pp],
                            woutp[:, dc * 512 + ot * 128:dc * 512 + (ot + 1) * 128],
                            ctxT[:, dc * T + tc * 512:dc * T + (tc + 1) * 512],
                            start=(dc == 0), stop=(dc == NDC - 1))
                    return ins
                t_o = P.add("tensor", mko, inc=("p_tensor", 1))
                P.wait("scalar", "p_tensor", t_o)
                P.wait("scalar", "p_vector", v_res_prev)
                def evo(e, pp=pp, oc=oc, y_tc=y_tc, bout=bout):
                    return e.activation(
                        y_tc[:, oc * 512:(oc + 1) * 512], psO[pp],
                        AF.Identity, bias=bout[:, oc:oc + 1])
                ev_prev[pp] = P.add("scalar", evo, inc=("p_scalar", 1))
        P.wait("sync", "gd", P.val("gd"))
        P.wait("tensor", "p_scalar", P.val("p_scalar"))
        st, v_res_prev = emit_out_ln_residual(
            P, tc, x_in, x_out, y_tc, xf2, xsq_dc, ones_bf,
            stats, cpool, pspool[:, 1024:1536], pspool[:, 1536:2048])


def emit_l0a(P, aps):
    nc = P.nc
    x_in, x_out = aps["x_in"], aps["x_out"]

    with Scope(nc) as S:
        cpool = S.sb("cpool", [128, 8], F32)
        ones_f = S.sb("onesf", [128, 128], F32)
        ones_bf = S.sb("onesbf", [128, 128], BF16)
        ident_f = S.sb("identf", [128, 128], F32)
        ident_bf = S.sb("identbf", [128, 128], BF16)
        ones2_bf = S.sb("ones2bf", [2, 128], BF16)
        bout = S.sb("bout", [128, 16], F32)
        st_mean = S.sb("stmean", [128, 512], F32)
        st_msq = S.sb("stmsq", [128, 512], F32)
        st_var = S.sb("stvar", [128, 512], F32)
        big_bf = S.sb("bigbf", [128, NDC * T], BF16)
        ckvT = S.sb("ckvT", [128, 4 * T], BF16)
        cqT = S.sb("cqT", [128, 4 * T], BF16)
        pspool = S.ps("pspool", [128, 4096], F32)

        stats = {"mean": st_mean, "msq": st_msq, "var": st_var, "rstd": st_var}
        xln = big_bf
        ctxT = big_bf

        _load(P, cpool[:], aps["consts"])
        _load(P, ones_f[:], aps["ones128"])
        _load(P, bout[:], aps["b_out"])
        ldi = _load(P, ident_f[:], aps["identity"])
        P.wait("vector", "ld", ldi)
        P.add("vector", lambda e: e.tensor_copy(ones_bf[:], ones_f[:]))
        P.add("vector", lambda e: e.tensor_copy(ones2_bf[:], ones_f[0:2, :]))
        P.add("vector", lambda e: e.tensor_copy(ident_bf[:], ident_f[:]),
              inc=("p_vector", 1))

        with Scope(nc) as S1:
            xf = S1.sb("xf", [128, NDC * 512], F32)
            xb = S1.sb("xb", [128, NDC * 512], BF16)
            xsq = S1.sb("xsq", [128, NDC * 512], BF16)
            emit_ln_in(P, x_in, xln, xf, xb, xsq, ones_bf, stats, cpool,
                       pspool[:, 0:512], pspool[:, 512:1024])
            P.flush()

        with Scope(nc) as S2:
            stage = S2.sb("stage", [128, 8192], F32)
            wbuf = S2.sb("wbuf", [128, NDC * 512], BF16)
            _l0a_phaseC(P, aps, xln, ckvT, cqT, stage, wbuf, pspool)
            if "dbg_ckvT" in aps:
                P.wait("gpsimd", "p_scalar", P.val("p_scalar"))
                _store(P, aps["dbg_ckvT"], ckvT[:])
                _store(P, aps["dbg_cqT"], cqT[:])
            P.flush()

        with Scope(nc) as S3:
            cosT = S3.sb("cosT", [32, T], BF16)
            sinT = S3.sb("sinT", [32, T], BF16)
            smask = S3.sb("sb_smask", [128, 2048], F32)
            ptmask = S3.sb("sb_ptmask", [128, 2048], BF16)
            wq = S3.sb("wq", [128, 4 * T], BF16)
            wk = S3.sb("wk", [128, 4 * T], BF16)
            wuv = S3.sb("wuv", [128, 4 * T], BF16)
            with Scope(nc) as S3a:
                stage2 = S3a.sb("stage2", [128, 4096], F32)
                _l0a_loadD(P, aps, cosT, sinT, smask, ptmask, wq, wk, wuv, stage2)
                P.flush()
            P.wait("tensor", "p_vector", P.val("p_vector"))

            with Scope(nc) as S3b:
                qh = S3b.sb("qh", [128, T], BF16)
                kh = S3b.sb("kh", [128, T], BF16)
                vh = S3b.sb("vh", [128, 16 * 128], BF16)
                ta = S3b.sb("ta", [32, 1024], F32)
                tb = S3b.sb("tb", [32, 1024], F32)
                junk = S3b.sb("junk", [128, 512], F32)
                racc = S3b.sb("racc", [128, 64], F32)
                nmx = S3b.sb("nmx", [128, 4], F32)
                nms_all = S3b.sb("nmsall", [128, 16], F32)
                scl16 = S3b.sb("scl16", [128, 48], F32)
                rowsrc = S3b.sb("rowsrc", [128, 32], BF16)
                Lrow2 = S3b.sb("Lrow2", [2, T], BF16)
                lrow_s = S3b.sb("lrows", [32, 128], BF16)
                ptb = [S3b.sb("ptb%d" % i, [128, 512], BF16) for i in range(4)]
                lrow_sc = nc.dram_tensor("lrow_sc_l0a", [32, 128], BF16,
                                         kind="Internal").ap()
                emit_l0a_heads(P, dict(
                    xln=xln, ckvT=ckvT, cqT=cqT, wq=wq, wk=wk, wuv=wuv,
                    cosT=cosT, sinT=sinT, smask=smask, ptmask=ptmask, ctxT=ctxT,
                    qh=qh, kh=kh, vh=vh, ta=ta, tb=tb, junk=junk, racc=racc,
                    nmx=nmx, nms_all=nms_all, scl16=scl16, rowsrc=rowsrc,
                    Lrow2=Lrow2, lrow_s=lrow_s, lrow_sc=lrow_sc, ptb=ptb,
                    psS=pspool[:, 0:2048],
                    psCtx=pspool[:, 3072:3584],
                    psRow32=pspool[0:32, 3584:3648].bitcast(BF16),
                    ones2_bf=ones2_bf, ident_bf=ident_bf, cpool=cpool))
                P.flush()

        soft_barrier(P)
        with Scope(nc) as S4:
            stage3 = S4.sb("stage3", [128, 4096], F32)
            woutp = S4.sb("woutp", [128, NDC * 512], BF16)
            y_tc = S4.sb("y_tc", [128, NDC * 512], BF16)
            xf2 = S4.sb("xf2", [128, 4096], F32)
            xsq_dc = S4.sb("xsqdc", [128, 4096], BF16)
            _l0a_phaseE(P, aps, ctxT, stage3, woutp, y_tc, xf2, xsq_dc,
                        ones_bf, bout, stats, cpool, pspool, x_in, x_out)
            P.flush()
    P.wait("gpsimd", "gd", P.val("gd"))
    P.flush()



def sync_fence(P):
    for e in ("tensor", "scalar", "vector"):
        P.wait("sync", "p_" + e, P.val("p_" + e))
    P.wait("sync", "gd", P.val("gd"))


def soft_barrier(P, engines=("tensor", "scalar", "vector")):
    vals = {e: P.val("p_" + e) for e in engines}
    for e in engines:
        for o in engines:
            if o != e:
                P.wait(e, "p_" + o, vals[o])


def emit_l0a_heads(P, B):
    xln, ckvT, cqT = B["xln"], B["ckvT"], B["cqT"]
    wq, wk, wuv = B["wq"], B["wk"], B["wuv"]
    cosT, sinT, smask, ptmask = B["cosT"], B["sinT"], B["smask"], B["ptmask"]
    ctxT, qh, kh, vh = B["ctxT"], B["qh"], B["kh"], B["vh"]
    ta, tb, junk, racc, nmx = B["ta"], B["tb"], B["junk"], B["racc"], B["nmx"]
    nms_all, scl16, rowsrc = B["nms_all"], B["scl16"], B["rowsrc"]
    Lrow2, ptb = B["Lrow2"], B["ptb"]
    psQ = B["psS"]
    ps4 = [psQ[:, i * 512:(i + 1) * 512] for i in range(4)]
    psCtx, psRow32 = B["psCtx"], B["psRow32"]
    lrow_s, lrow_sc = B["lrow_s"], B["lrow_sc"]
    ones2_bf, ident_bf, cpool = B["ones2_bf"], B["ident_bf"], B["cpool"]
    SQ = float(np.sqrt(128.0))
    hi_view = rowsrc[:].rearrange("p (q two) -> p q two", two=2)[:, :, 0]
    lo_view = rowsrc[:].rearrange("p (q two) -> p q two", two=2)[:, :, 1]

    for h in range(16):
        soft_barrier(P)
        rope_v = 0
        base_s = 0
        for side in ("q", "k"):
            wqk = wq if side == "q" else wk
            srcT = cqT if side == "q" else ckvT
            dsth = qh if side == "q" else kh
            P.wait("tensor", "p_vector", rope_v)
            P.wait("tensor", "p_scalar", base_s)
            def mkp(e, wqk=wqk, srcT=srcT, h=h):
                for tc in range(4):
                    for ct in range(4):
                        ins = e.matmul(
                            ps4[tc],
                            wqk[:, ct * T + h * 128:ct * T + (h + 1) * 128],
                            srcT[:, ct * T + tc * 512:ct * T + (tc + 1) * 512],
                            start=(ct == 0), stop=(ct == 3))
                return ins
            t_p = P.add("tensor", mkp, inc=("p_tensor", 1))
            P.wait("vector", "p_tensor", t_p)
            for hh in range(2):
                hsl = slice(hh * 1024, (hh + 1) * 1024)
                def r1(e, hsl=hsl):
                    e.tensor_mul(ta[:], psQ[0:32, hsl], cosT[:, hsl])
                    return e.tensor_mul(tb[:], psQ[32:64, hsl], sinT[:, hsl])
                P.addw("vector", r1)
                P.add("vector", lambda e, dsth=dsth, hsl=hsl: e.tensor_sub(
                    dsth[0:32, hsl], ta[:], tb[:]))
                def r2(e, hsl=hsl):
                    e.tensor_mul(ta[:], psQ[32:64, hsl], cosT[:, hsl])
                    return e.tensor_mul(tb[:], psQ[0:32, hsl], sinT[:, hsl])
                P.addw("vector", r2)
                rope_v = P.add("vector", lambda e, dsth=dsth, hsl=hsl: e.tensor_add(
                    dsth[32:64, hsl], ta[:], tb[:]), inc=("p_vector", 1))
            P.wait("scalar", "p_tensor", t_p)
            base_s = P.add("scalar", lambda e, dsth=dsth: e.copy(
                dsth[64:128, :], psQ[64:128, :]), inc=("p_scalar", 1))
        ev_v = 0
        P.wait("tensor", "p_vector", rope_v)
        for gv in range(4):
            P.wait("tensor", "p_scalar", ev_v)
            def mkv(e, gv=gv, h=h):
                for i in range(4):
                    tt = gv * 4 + i
                    for ct in range(4):
                        ins = e.matmul(
                            ps4[i][:, 0:128],
                            ckvT[:, ct * T + tt * 128:ct * T + (tt + 1) * 128],
                            wuv[:, ct * T + h * 128:ct * T + (h + 1) * 128],
                            start=(ct == 0), stop=(ct == 3))
                return ins
            t_v = P.add("tensor", mkv, inc=("p_tensor", 1))
            P.wait("scalar", "p_tensor", t_v)
            def evv(e, gv=gv):
                for i in range(4):
                    tt = gv * 4 + i
                    ins = e.copy(vh[:, tt * 128:(tt + 1) * 128], ps4[i][:, 0:128])
                return ins
            ev_v = P.add("scalar", evv, inc=("p_scalar", 1))

        v_ms = P.add("vector", lambda e: e.memset(racc[:], 0.0),
                     inc=("p_vector", 1))
        s_exp = ev_v
        for qi in range(16):
            nch = qi // 4 + 1
            P.wait("tensor", "p_scalar", s_exp)
            def mks(e, qi=qi, nch=nch):
                for kc in range(nch):
                    ins = e.matmul(
                        psQ[:, kc * 512:(kc + 1) * 512],
                        qh[:, qi * 128:(qi + 1) * 128],
                        kh[:, kc * 512:(kc + 1) * 512],
                        start=True, stop=True)
                return ins
            t_s = P.add("tensor", mks, inc=("p_tensor", 1))
            P.wait("vector", "p_tensor", t_s)
            if qi == 0:
                P.wait("vector", "p_vector", v_ms)
            dg = (nch - 1) * 512
            P.addw("vector", lambda e, dg=dg, qi=qi: e.tensor_add(
                psQ[:, dg:dg + 512], psQ[:, dg:dg + 512],
                smask[:, (qi % 4) * 512:(qi % 4 + 1) * 512]))
            def mkmaxes(e, nch=nch):
                for kc in range(nch):
                    ins = e.reduce_max(nmx[:, kc:kc + 1],
                                       psQ[:, kc * 512:(kc + 1) * 512],
                                       axis=AX.X, negate=True)
                return ins
            P.addw("vector", mkmaxes)
            P.addw("vector", lambda e, nch=nch: e.tensor_reduce(
                scl16[:, 47:48], nmx[:, 0:nch], axis=AX.X, op=ALU.min))
            v_m = P.add("vector", lambda e, qi=qi: e.tensor_scalar_mul(
                nms_all[:, qi:qi + 1], scl16[:, 47:48], SC), inc=("p_vector", 1))
            P.wait("scalar", "p_vector", v_m)
            def mkexp(e, qi=qi, nch=nch):
                for kc in range(nch):
                    ins = e.activation(junk[:], psQ[:, kc * 512:(kc + 1) * 512],
                                       AF.Exp, bias=nms_all[:, qi:qi + 1],
                                       scale=cpool[:, 3:4],
                                       accum_out=racc[:, 4 * qi + kc:4 * qi + kc + 1])
                return ins
            s_exp = P.add("scalar", mkexp, inc=("p_scalar", 1))

        P.wait("vector", "p_scalar", s_exp)
        r3 = racc[:].rearrange("p (q k) -> p q k", k=4)
        def mksum(e):
            for qi in range(16):
                ins = e.tensor_reduce(scl16[:, qi:qi + 1], r3[:, qi, :],
                                      axis=AX.X, op=ALU.add)
            return ins
        v_rs = P.add("vector", mksum, inc=("p_vector", 1))
        P.wait("scalar", "p_vector", v_rs)
        s_ln = P.add("scalar", lambda e: e.activation(
            scl16[:, 16:32], scl16[:, 0:16], AF.Ln), inc=("p_scalar", 1))
        P.wait("vector", "p_scalar", s_ln)
        P.addw("vector", lambda e: e.tensor_sub(scl16[:, 32:48], nms_all[:],
                                                scl16[:, 16:32]))
        P.addw("vector", lambda e: e.tensor_scalar_mul(scl16[:, 32:48],
                                                       scl16[:, 32:48], SQ))
        P.addw("vector", lambda e: e.tensor_copy(hi_view, scl16[:, 32:48]))
        P.addw("vector", lambda e: e.tensor_copy(scl16[:, 0:16], hi_view))
        P.addw("vector", lambda e: e.tensor_sub(scl16[:, 16:32], scl16[:, 32:48],
                                                scl16[:, 0:16]))
        v_row = P.add("vector", lambda e: e.tensor_copy(lo_view, scl16[:, 16:32]),
                      inc=("p_vector", 1))
        P.wait("tensor", "p_vector", v_row)
        t_tr = P.add("tensor", lambda e: e.transpose(psRow32, rowsrc[:, 0:32],
                                                     ident_bf[:]),
                     inc=("p_tensor", 1))
        P.wait("scalar", "p_tensor", t_tr)
        s_lr = P.add("scalar", lambda e: e.copy(lrow_s[:], psRow32),
                     inc=("p_scalar", 1))
        P.wait("gpsimd", "p_scalar", s_lr)
        P.wait("gpsimd", "ld", P.val("ld"))
        st_lr = _store(P, lrow_sc, lrow_s[:])
        P.wait("sync", "gd", st_lr)
        ld_lr = _load(P, Lrow2[:].rearrange("two (q j) -> two q j", j=128),
                      lrow_sc.rearrange("(q two) j -> two q j", two=2))
        P.wait("tensor", "ld", ld_lr)

        exp_g = s_exp
        mask_g = 0
        ctx_mark = 0
        s_ctx_prev = 0
        for tqc in range(4):
            ntt = 4 * tqc + 4
            groups = [list(range(g, min(g + 4, ntt))) for g in range(0, ntt, 4)]
            for gi, tiles in enumerate(groups):
                last_grp = (gi == len(groups) - 1)
                P.wait("tensor", "p_scalar", exp_g)
                def mkpt(e, tiles=tiles, tqc=tqc):
                    for i, tt in enumerate(tiles):
                        e.matmul(ps4[i],
                                 kh[:, tt * 128:(tt + 1) * 128],
                                 qh[:, tqc * 512:(tqc + 1) * 512],
                                 start=True, stop=False)
                        ins = e.matmul(ps4[i], ones2_bf[:],
                                       Lrow2[0:2, tqc * 512:(tqc + 1) * 512],
                                       start=False, stop=True)
                    return ins
                t_pt = P.add("tensor", mkpt, inc=("p_tensor", 1))
                P.wait("scalar", "p_tensor", t_pt)
                P.wait("scalar", "p_tensor", ctx_mark)
                def mkexp2(e, tiles=tiles):
                    for i in range(len(tiles)):
                        ins = e.activation(ptb[i][:], ps4[i], AF.Exp,
                                           scale=cpool[:, 3:4])
                    return ins
                exp_g = P.add("scalar", mkexp2, inc=("p_scalar", 1))
                if last_grp:
                    P.wait("vector", "p_scalar", exp_g)
                    def mkm2(e, tiles=tiles, tqc=tqc):
                        for i, tt in enumerate(tiles):
                            pat = tt - 4 * tqc
                            ins = e.tensor_mul(ptb[i][:], ptb[i][:],
                                               ptmask[:, pat * 512:(pat + 1) * 512])
                        return ins
                    mask_g = P.add("vector", mkm2, inc=("p_vector", 1))
                    P.wait("tensor", "p_vector", mask_g)
                else:
                    P.wait("tensor", "p_scalar", exp_g)
                if gi == 0:
                    P.wait("tensor", "p_scalar", s_ctx_prev)
                def mkctx(e, tiles=tiles, ntt=ntt):
                    for i, tt in enumerate(tiles):
                        ins = e.matmul(psCtx, vh[:, tt * 128:(tt + 1) * 128],
                                       ptb[i][:], start=(tt == 0),
                                       stop=(tt == ntt - 1))
                    return ins
                ctx_mark = P.add("tensor", mkctx, inc=("p_tensor", 1))
            P.wait("scalar", "p_tensor", ctx_mark)
            def evctx(e, h=h, tqc=tqc):
                return e.copy(ctxT[:, h * T + tqc * 512:h * T + (tqc + 1) * 512],
                              psCtx)
            s_ctx_prev = P.add("scalar", evctx, inc=("p_scalar", 1))





L0A_KEYS = ("consts", "ones128", "identity", "cos", "sin", "smask", "ptmask",
            "b_out", "W_dkv", "W_dq", "Wq_comb", "Wk_comb", "W_uv", "W_out")


def build_l0a(H):
    nc = bass.Bass("TRN2", target_bir_lowering=False)
    aps = {}
    for name in L0A_KEYS:
        aps[name] = nc.dram_tensor(name, list(H[name].shape), F32,
                                   kind="ExternalInput").ap()
    aps["x_in"] = nc.dram_tensor("x_in", [D, T], F32, kind="ExternalInput").ap()
    aps["x_out"] = nc.dram_tensor("x_out", [D, T], F32, kind="ExternalOutput").ap()
    with nc.Block() as block:
        P = Prog(nc, block)
        emit_l0a(P, aps)
    return nc


def kernel_unfused(**inputs):
    H = _prep_host(inputs)
    x = np.asarray(inputs["L0_attn_x"], np.float32)
    ncores = 8
    cores = list(range(ncores))
    xT = [np.ascontiguousarray(x[b].T) for b in range(ncores)]

    for build, keys in ((build_l0a, L0A_KEYS), (build_l0f, L0F_KEYS),
                        (build_l1a, L1A_KEYS), (build_l1f, L1F_KEYS)):
        nc = build(H)
        common = {k: H[k] for k in keys}
        in_maps = [dict(common, x_in=xT[b]) for b in range(ncores)]
        res = run_bass_kernel_spmd(nc, in_maps, core_ids=cores)
        xT = [np.asarray(r["x_out"]) for r in res.results]

    out = np.stack([t.T for t in xT]).astype(np.float32)
    return out



def _ffn_emit_load(P, st, stage_bufs, slabctx, wdst, src_ap, mode, I, guard_t):
    for j in range(4):
        sb = stage_bufs[slabctx["i"] % 2]
        P.wait("sync", "p_scalar", slabctx["guards"][slabctx["i"] % 2])
        if j == 0:
            P.wait("sync", "p_tensor", guard_t)
        if mode == "cols":
            sl = src_ap[j * 512:(j + 1) * 512, I * 512:(I + 1) * 512]
            nsub = 4
        else:
            sl = src_ap[I * 512 + j * 128:I * 512 + (j + 1) * 128, :]
            nsub = 1
        ld = _load(P, sb[:].rearrange("p (c i) -> p c i", c=nsub),
                   sl.rearrange("(c p) i -> p c i", p=128))
        P.wait("scalar", "ld", ld)
        if j == 0:
            P.wait("scalar", "p_tensor", guard_t)
        tgt = P.add("scalar", lambda e, j=j, sb=sb, wdst=wdst: e.copy(
            wdst[:, j * 2048:(j + 1) * 2048], sb[:]), inc=("p_scalar", 1))
        slabctx["guards"][slabctx["i"] % 2] = tgt
        slabctx["i"] += 1
    return tgt


def emit_ffn_pass(P, B, half, wg_ap, wu_ap, wd_ap, nchunks, first_pass,
                  gs=None, us=None, ds=None, comb=None, sc_base=0):
    nc = P.nc
    xln, acc = B["xln"], B["acc"]
    stages_b, wbufs = B["stages"], B["wbufs"]
    hT, hsil = B["hT"], B["hsil"]
    pspool = B["pspool"]
    psG = [pspool[:, 0:512], pspool[:, 512:1024]]
    psU = [pspool[:, 1024:1536], pspool[:, 1536:2048]]
    psD = [pspool[:, 2048:2560], pspool[:, 2560:3072]]
    tok0 = half * 1024

    stages = []
    for I in range(nchunks):
        stages.append(("g", I, wg_ap, "cols"))
        stages.append(("u", I, wu_ap, "cols"))
        stages.append(("d", I, wd_ap, "rows"))
    slabctx = B.setdefault("slabctx", {"i": 0, "guards": [0, 0]})
    t_done = {-1: P.val("p_tensor"), -2: P.val("p_tensor")}
    cast_done = {}

    def emit_load_for(s):
        if s >= len(stages):
            return
        kind, I, ap, mode = stages[s]
        cast_done[s] = _ffn_emit_load(P, None, stages_b, slabctx,
                                      wbufs[s % 2], ap, mode, I,
                                      t_done[s - 2])

    emit_load_for(0)
    sil_done = [0, 0]
    h_done = [0, 0]
    acc_done = [0, 0]
    v_hall = 0
    for s, (kind, I, ap, mode) in enumerate(stages):
        emit_load_for(s + 1)
        wbuf = wbufs[s % 2]
        P.wait("tensor", "p_scalar", cast_done[s])
        if kind == "g":
            for isub in range(4):
                g_isub = sc_base + I * 4 + isub
                for tch in range(2):
                    pp = (isub * 2 + tch) % 2
                    P.wait("tensor", "p_scalar", sil_done[pp])
                    def mmg(e, isub=isub, tch=tch, pp=pp, wbuf=wbuf):
                        for dc in range(NDC):
                            ins = e.matmul(psG[pp],
                                           wbuf[:, dc * 512 + isub * 128:dc * 512 + (isub + 1) * 128],
                                           xln[:, dc * T + tok0 + tch * 512:dc * T + tok0 + (tch + 1) * 512],
                                           start=(dc == 0), stop=(dc == NDC - 1))
                        return ins
                    t_g = P.add("tensor", mmg, inc=("p_tensor", 1))
                    P.wait("scalar", "p_tensor", t_g)
                    hs = slice(isub * 1024 + tch * 512, isub * 1024 + (tch + 1) * 512)
                    if gs is not None:
                        def mksil(e, pp=pp, hs=hs, g_isub=g_isub):
                            return e.activation(hsil[:, hs], psG[pp], AF.Silu,
                                                scale=gs[:, g_isub:g_isub + 1])
                    else:
                        def mksil(e, pp=pp, hs=hs):
                            return e.activation(hsil[:, hs], psG[pp], AF.Silu)
                    sil_done[pp] = P.add("scalar", mksil, inc=("p_scalar", 1))
            t_done[s] = P.val("p_tensor")
        elif kind == "u":
            P.wait("tensor", "p_scalar", max(sil_done))
            for isub in range(4):
                g_isub = sc_base + I * 4 + isub
                for tch in range(2):
                    pp = (isub * 2 + tch) % 2
                    P.wait("tensor", "p_vector", h_done[pp])
                    def mmu(e, isub=isub, tch=tch, pp=pp, wbuf=wbuf):
                        for dc in range(NDC):
                            ins = e.matmul(psU[pp],
                                           wbuf[:, dc * 512 + isub * 128:dc * 512 + (isub + 1) * 128],
                                           xln[:, dc * T + tok0 + tch * 512:dc * T + tok0 + (tch + 1) * 512],
                                           start=(dc == 0), stop=(dc == NDC - 1))
                        return ins
                    t_u = P.add("tensor", mmu, inc=("p_tensor", 1))
                    P.wait("vector", "p_tensor", t_u)
                    hs = slice(isub * 1024 + tch * 512, isub * 1024 + (tch + 1) * 512)
                    if us is not None:
                        def mkh(e, pp=pp, hs=hs, g_isub=g_isub):
                            return e.scalar_tensor_tensor(
                                hT[:, hs], psU[pp], us[:, g_isub:g_isub + 1],
                                hsil[:, hs], ALU.mult, ALU.mult)
                    else:
                        def mkh(e, pp=pp, hs=hs):
                            return e.tensor_mul(hT[:, hs], psU[pp], hsil[:, hs])
                    v_h = P.add("vector", mkh, inc=("p_vector", 1))
                    if comb is not None:
                        P.wait("vector", "p_vector", v_h)
                        def mkcomb(e, hs=hs, tch=tch):
                            return e.tensor_mul(
                                hT[:, hs], hT[:, hs],
                                comb[:, tok0 + tch * 512:tok0 + (tch + 1) * 512])
                        v_h = P.add("vector", mkcomb, inc=("p_vector", 1))
                    h_done[pp] = v_h
            t_done[s] = P.val("p_tensor")
            v_hall = P.val("p_vector")
        else:
            P.wait("tensor", "p_vector", v_hall)
            for ot in range(16):
                for tch in range(2):
                    pp = (ot * 2 + tch) % 2
                    P.wait("tensor", "p_vector", acc_done[pp])
                    def mmd(e, ot=ot, tch=tch, pp=pp, wbuf=wbuf):
                        for isub in range(4):
                            ins = e.matmul(
                                psD[pp],
                                wbuf[:, isub * 2048 + ot * 128:isub * 2048 + (ot + 1) * 128],
                                hT[:, isub * 1024 + tch * 512:isub * 1024 + (tch + 1) * 512],
                                start=(isub == 0), stop=(isub == 3))
                        return ins
                    t_d = P.add("tensor", mmd, inc=("p_tensor", 1))
                    P.wait("vector", "p_tensor", t_d)
                    asl = slice(ot * 1024 + tch * 512, ot * 1024 + (tch + 1) * 512)
                    if first_pass and I == 0:
                        if ds is not None:
                            def mkacc(e, pp=pp, asl=asl, ot=ot):
                                return e.tensor_single_scalar(
                                    acc[:, asl], psD[pp], ds[:, ot:ot + 1], ALU.mult)
                        else:
                            def mkacc(e, pp=pp, asl=asl):
                                return e.tensor_copy(acc[:, asl], psD[pp])
                    else:
                        if ds is not None:
                            def mkacc(e, pp=pp, asl=asl, ot=ot):
                                return e.scalar_tensor_tensor(
                                    acc[:, asl], psD[pp], ds[:, ot:ot + 1],
                                    acc[:, asl], ALU.mult, ALU.add)
                        else:
                            def mkacc(e, pp=pp, asl=asl):
                                return e.tensor_add(acc[:, asl], acc[:, asl], psD[pp])
                    acc_done[pp] = P.add("vector", mkacc, inc=("p_vector", 1))
            t_done[s] = P.val("p_tensor")
    P.wait("tensor", "p_vector", P.val("p_vector"))
    P.wait("tensor", "p_scalar", P.val("p_scalar"))


def emit_router(P, B, aps):
    nc = P.nc
    xln, stage = B["xln"], B["stage"]
    rwb, comb, crow, crow_bf = B["rwb"], B["comb"], B["crow"], B["crow_bf"]
    combN, t8 = B["combN"], B["t8"]
    ident_f, ones_bf, cpool = B["ident_f"], B["ones_bf"], B["cpool"]
    pspool = B["pspool"]
    psL = pspool[:, 0:512]
    psT = pspool[0:1, 512:640]
    psBC = pspool[:, 1024:1536]

    P.wait("sync", "p_vector", P.val("p_vector"))
    ld = _load(P, stage[:, 0:128].rearrange("p (c e) -> p c e", e=8),
               aps["rw_T"].rearrange("(c p) e -> p c e", p=128))
    P.wait("vector", "ld", ld)
    v_rw = P.add("vector", lambda e: e.tensor_copy(rwb[:], stage[:, 0:128]),
                 inc=("p_vector", 1))
    P.wait("tensor", "p_vector", v_rw)
    v_done = 0
    for tt in range(16):
        P.wait("tensor", "p_vector", v_done)
        def mml(e, tt=tt):
            for dc in range(NDC):
                ins = e.matmul(psL[:, 0:8],
                               xln[:, dc * T + tt * 128:dc * T + tt * 128 + 128],
                               rwb[:, dc * 8:(dc + 1) * 8],
                               start=(dc == 0), stop=(dc == NDC - 1))
            return ins
        t_l = P.add("tensor", mml, inc=("p_tensor", 1))
        P.wait("vector", "p_tensor", t_l)
        P.addw("vector", lambda e: e.reduce_max(t8[:, 0:1], psL[:, 0:8],
                                                axis=AX.X, negate=True))
        P.addw("vector", lambda e: e.tensor_scalar_mul(t8[:, 1:2], t8[:, 0:1], -1.0))
        P.addw("vector", lambda e: e.tensor_single_scalar(
            t8[:, 8:16], psL[:, 0:8], t8[:, 1:2], ALU.is_ge))
        P.addw("vector", lambda e: e.scalar_tensor_tensor(
            t8[:, 16:24], t8[:, 8:16], -30000.0, psL[:, 0:8], ALU.mult, ALU.add))
        v_m2 = P.addw("vector", lambda e: e.reduce_max(
            t8[:, 2:3], t8[:, 16:24], axis=AX.X, negate=True))
        P.wait("scalar", "p_vector", v_m2)
        s_e = P.add("scalar", lambda e: e.activation(
            t8[:, 3:4], t8[:, 2:3], AF.Exp, scale=cpool[:, 4:5], bias=t8[:, 0:1]),
            inc=("p_scalar", 1))
        P.wait("vector", "p_scalar", s_e)
        P.addw("vector", lambda e: e.tensor_scalar_add(t8[:, 4:5], t8[:, 3:4], 1.0))
        P.addw("vector", lambda e: e.reciprocal(t8[:, 5:6], t8[:, 4:5]))
        v_done = P.add("vector", lambda e, tt=tt: e.tensor_mul(
            combN[:, tt:tt + 1], t8[:, 4:5], t8[:, 5:6]), inc=("p_vector", 1))
    s_ev = 0
    for tt in range(16):
        P.wait("tensor", "p_vector", v_done)
        P.wait("tensor", "p_scalar", s_ev)
        t_tr = P.add("tensor", lambda e, tt=tt: e.transpose(
            psT[0:1, 0:128], combN[:, tt:tt + 1], ident_f[:]),
            inc=("p_tensor", 1))
        P.wait("scalar", "p_tensor", t_tr)
        s_ev = P.add("scalar", lambda e, tt=tt: e.copy(
            crow[0:1, tt * 128:(tt + 1) * 128], psT[0:1, 0:128]),
            inc=("p_scalar", 1))
    P.wait("vector", "p_scalar", s_ev)
    v_cb = P.add("vector", lambda e: e.tensor_copy(crow_bf[:], crow[:]),
                 inc=("p_vector", 1))
    P.wait("tensor", "p_vector", v_cb)
    s_ev = 0
    for tc in range(4):
        P.wait("tensor", "p_scalar", s_ev)
        t_bc = P.add("tensor", lambda e, tc=tc: e.matmul(
            psBC, ones_bf[0:1, :], crow_bf[0:1, tc * 512:(tc + 1) * 512],
            start=True, stop=True), inc=("p_tensor", 1))
        P.wait("scalar", "p_tensor", t_bc)
        s_ev = P.add("scalar", lambda e, tc=tc: e.copy(
            comb[:, tc * 512:(tc + 1) * 512], psBC), inc=("p_scalar", 1))


def emit_l0f(P, aps):
    nc = P.nc
    x_in, x_out = aps["x_in"], aps["x_out"]
    with Scope(nc) as S:
        cpool = S.sb("cpool", [128, 8], F32)
        ones_f = S.sb("onesf", [128, 128], F32)
        ones_bf = S.sb("onesbf", [128, 128], BF16)
        ident_f = S.sb("identf", [128, 128], F32)
        st_mean = S.sb("stmean", [128, 512], F32)
        st_msq = S.sb("stmsq", [128, 512], F32)
        st_var = S.sb("stvar", [128, 512], F32)
        big_bf = S.sb("bigbf", [128, NDC * T], BF16)
        acc = S.sb("acc", [128, NDC * 1024], F32)
        comb = S.sb("comb", [128, 2048], BF16)
        pspool = S.ps("pspool", [128, 4096], F32)
        stats = {"mean": st_mean, "msq": st_msq, "var": st_var, "rstd": st_var}
        xln = big_bf

        _load(P, cpool[:], aps["consts"])
        ldc = _load(P, ones_f[:], aps["ones128"])
        ldi = _load(P, ident_f[:], aps["identity"])
        P.wait("vector", "ld", ldi)
        P.add("vector", lambda e: e.tensor_copy(ones_bf[:], ones_f[:]),
              inc=("p_vector", 1))

        with Scope(nc) as S1:
            xf = S1.sb("xf", [128, NDC * 512], F32)
            xb = S1.sb("xb", [128, NDC * 512], BF16)
            xsq = S1.sb("xsq", [128, NDC * 512], BF16)
            emit_ln_in(P, x_in, xln, xf, xb, xsq, ones_bf, stats, cpool,
                       pspool[:, 0:512], pspool[:, 512:1024])
            P.flush()

        with Scope(nc) as S2:
            stage = S2.sb("stage", [128, 4096], F32)
            rwb = S2.sb("rwb", [128, 128], BF16)
            crow = S2.sb("crow", [1, 2048], F32)
            crow_bf = S2.sb("crowbf", [1, 2048], BF16)
            combN = S2.sb("combN", [128, 16], F32)
            t8 = S2.sb("t8", [128, 24], F32)
            emit_router(P, dict(xln=xln, stage=stage, rwb=rwb, comb=comb,
                                crow=crow, crow_bf=crow_bf, combN=combN, t8=t8,
                                ident_f=ident_f, ones_bf=ones_bf, cpool=cpool,
                                pspool=pspool), aps)
            if "dbg_comb" in aps:
                P.wait("gpsimd", "p_scalar", P.val("p_scalar"))
                _store(P, aps["dbg_comb"], comb[:])
                P.wait("sync", "gd", P.val("gd"))
            P.flush()

        for half in range(2):
            soft_barrier(P)
            with Scope(nc) as SA:
                sync_fence(P)
                stA = SA.sb("stA%d" % half, [128, 2048], F32)
                stB = SA.sb("stB%d" % half, [128, 2048], F32)
                wA = SA.sb("wA%d" % half, [128, 8192], BF16)
                wB = SA.sb("wB%d" % half, [128, 8192], BF16)
                hT = SA.sb("hT%d" % half, [128, 4096], BF16)
                hsil = SA.sb("hsil%d" % half, [128, 4096], BF16)
                Bf = dict(xln=xln, acc=acc, stages=[stA, stB], wbufs=[wA, wB],
                          hT=hT, hsil=hsil, pspool=pspool)
                emit_ffn_pass(P, Bf, half, aps["sh_gate_T"], aps["sh_up_T"],
                              aps["sh_down_T"], 16, first_pass=True)
                emit_ffn_pass(P, Bf, half, aps["r_gate_T"], aps["r_up_T"],
                              aps["r_down_T"], 2, first_pass=False, comb=comb)
                if half == 1 and "dbg_acc" in aps:
                    P.wait("gpsimd", "p_vector", P.val("p_vector"))
                    _store(P, aps["dbg_acc"], acc[:])
                    P.wait("sync", "gd", P.val("gd"))
                P.flush()
            soft_barrier(P)
            with Scope(nc) as SB:
                sync_fence(P)
                y_tc = SB.sb("y_tc%d" % half, [128, NDC * 512], BF16)
                xf2 = SB.sb("xf2_%d" % half, [128, 4096], F32)
                xsq_dc = SB.sb("xsqdc%d" % half, [128, 4096], BF16)
                acc3 = acc[:].rearrange("p (o t) -> p o t", t=1024)
                for tcl in range(2):
                    tc = half * 2 + tcl
                    s_cast = P.add("scalar", lambda e, tcl=tcl: e.copy(
                        _v3(y_tc), acc3[:, :, tcl * 512:(tcl + 1) * 512]),
                        inc=("p_scalar", 1))
                    P.wait("tensor", "p_scalar", s_cast)
                    P.wait("sync", "gd", P.val("gd"))
                    st, v_res = emit_out_ln_residual(
                        P, tc, x_in, x_out, y_tc, xf2, xsq_dc, ones_bf,
                        stats, cpool, pspool[:, 3072:3584], pspool[:, 3584:4096],
                        dbg_stats=(2, aps["dbg_mean"], aps["dbg_rstd"])
                        if "dbg_mean" in aps else None)
                    P.wait("scalar", "p_vector", v_res)
                P.flush()
    P.wait("gpsimd", "gd", P.val("gd"))
    P.flush()


L0F_KEYS = ("consts", "ones128", "identity", "rw_T", "sh_gate_T", "sh_up_T",
            "sh_down_T", "r_gate_T", "r_up_T", "r_down_T")


def build_l0f(H):
    nc = bass.Bass("TRN2", target_bir_lowering=False)
    aps = {}
    for name in L0F_KEYS:
        aps[name] = nc.dram_tensor(name, list(H[name].shape), F32,
                                   kind="ExternalInput").ap()
    aps["x_in"] = nc.dram_tensor("x_in", [D, T], F32, kind="ExternalInput").ap()
    aps["x_out"] = nc.dram_tensor("x_out", [D, T], F32, kind="ExternalOutput").ap()
    with nc.Block() as block:
        P = Prog(nc, block)
        emit_l0f(P, aps)
    return nc


def emit_l1f(P, aps):
    nc = P.nc
    x_in, x_out = aps["x_in"], aps["x_out"]
    with Scope(nc) as S:
        cpool = S.sb("cpool", [128, 8], F32)
        ones_f = S.sb("onesf", [128, 128], F32)
        ones_bf = S.sb("onesbf", [128, 128], BF16)
        st_mean = S.sb("stmean", [128, 512], F32)
        st_msq = S.sb("stmsq", [128, 512], F32)
        st_var = S.sb("stvar", [128, 512], F32)
        big_bf = S.sb("bigbf", [128, NDC * T], BF16)
        acc = S.sb("acc", [128, NDC * 1024], F32)
        gsT = S.sb("gsT", [128, 44], F32)
        usT = S.sb("usT", [128, 44], F32)
        dsT = S.sb("dsT", [128, 16], F32)
        pspool = S.ps("pspool", [128, 4096], F32)
        stats = {"mean": st_mean, "msq": st_msq, "var": st_var, "rstd": st_var}
        xln = big_bf

        _load(P, cpool[:], aps["consts"])
        _load(P, ones_f[:], aps["ones128"])
        _load(P, gsT[:], aps["gate_s"])
        _load(P, usT[:], aps["up_s"])
        ldc = _load(P, dsT[:], aps["down_s"])
        P.wait("vector", "ld", ldc)
        P.add("vector", lambda e: e.tensor_copy(ones_bf[:], ones_f[:]),
              inc=("p_vector", 1))

        with Scope(nc) as S1:
            xf = S1.sb("xf", [128, NDC * 512], F32)
            xb = S1.sb("xb", [128, NDC * 512], BF16)
            xsq = S1.sb("xsq", [128, NDC * 512], BF16)
            emit_ln_in(P, x_in, xln, xf, xb, xsq, ones_bf, stats, cpool,
                       pspool[:, 0:512], pspool[:, 512:1024])
            P.flush()

        for half in range(2):
            soft_barrier(P)
            with Scope(nc) as SA:
                sync_fence(P)
                stA = SA.sb("stA%d" % half, [128, 2048], F32)
                stB = SA.sb("stB%d" % half, [128, 2048], F32)
                wA = SA.sb("wA%d" % half, [128, 8192], BF16)
                wB = SA.sb("wB%d" % half, [128, 8192], BF16)
                hT = SA.sb("hT%d" % half, [128, 4096], BF16)
                hsil = SA.sb("hsil%d" % half, [128, 4096], BF16)
                Bf = dict(xln=xln, acc=acc, stages=[stA, stB], wbufs=[wA, wB],
                          hT=hT, hsil=hsil, pspool=pspool)
                emit_ffn_pass(P, Bf, half, aps["f_gate_T"], aps["f_up_T"],
                              aps["f_down_T"], 11, first_pass=True,
                              gs=gsT, us=usT, ds=dsT)
                P.flush()
            soft_barrier(P)
            with Scope(nc) as SB:
                sync_fence(P)
                y_tc = SB.sb("y_tc%d" % half, [128, NDC * 512], BF16)
                xf2 = SB.sb("xf2_%d" % half, [128, 4096], F32)
                xsq_dc = SB.sb("xsqdc%d" % half, [128, 4096], BF16)
                acc3 = acc[:].rearrange("p (o t) -> p o t", t=1024)
                for tcl in range(2):
                    tc = half * 2 + tcl
                    s_cast = P.add("scalar", lambda e, tcl=tcl: e.copy(
                        _v3(y_tc), acc3[:, :, tcl * 512:(tcl + 1) * 512]),
                        inc=("p_scalar", 1))
                    P.wait("tensor", "p_scalar", s_cast)
                    st, v_res = emit_out_ln_residual(
                        P, tc, x_in, x_out, y_tc, xf2, xsq_dc, ones_bf,
                        stats, cpool, pspool[:, 3072:3584], pspool[:, 3584:4096])
                    P.wait("scalar", "p_vector", v_res)
                P.flush()
    P.wait("gpsimd", "gd", P.val("gd"))
    P.flush()


L1F_KEYS = ("consts", "ones128", "f_gate_T", "f_up_T", "f_down_T",
            "gate_s", "up_s", "down_s")


def build_l1f(H):
    nc = bass.Bass("TRN2", target_bir_lowering=False)
    aps = {}
    for name in L1F_KEYS:
        aps[name] = nc.dram_tensor(name, list(H[name].shape), F32,
                                   kind="ExternalInput").ap()
    aps["x_in"] = nc.dram_tensor("x_in", [D, T], F32, kind="ExternalInput").ap()
    aps["x_out"] = nc.dram_tensor("x_out", [D, T], F32, kind="ExternalOutput").ap()
    with nc.Block() as block:
        P = Prog(nc, block)
        emit_l1f(P, aps)
    return nc



def _l1a_projections(P, aps, B):
    nc = P.nc
    xln, tmpT, BC, wdt, seln_bf, stage, pspool = (
        B["xln"], B["tmpT"], B["BC"], B["wdt"], B["seln_bf"], B["stage"], B["pspool"])
    wbc, wdd = B["wbc"], B["wdd"]
    sync_fence(P)
    ld = _load(P, stage[:, 0:512].rearrange("p (c i) -> p c i", i=32),
               aps["WBC"].rearrange("(c p) i -> p c i", p=128))
    P.wait("vector", "ld", ld)
    P.add("vector", lambda e: e.tensor_copy(wbc[:], stage[:, 0:512]),
          inc=("p_vector", 1))
    ld = _load(P, stage[:, 1024:3072].rearrange("p (c i) -> p c i", i=128),
               aps["dt_down"].rearrange("(c p) i -> p c i", p=128))
    P.wait("vector", "ld", ld)
    P.add("vector", lambda e: e.tensor_copy(wdd[:], stage[:, 1024:3072]),
          inc=("p_vector", 1))
    P.wait("sync", "p_vector", P.val("p_vector"))
    ld = _load(P, stage[:, 0:1024], aps["dtw_T"][:, 0:1024])
    ld = _load(P, stage[:, 1024:2048], aps["dtw_T"][:, 1024:2048])
    P.wait("vector", "ld", ld)
    P.add("vector", lambda e: e.tensor_copy(wdt[:], stage[:, 0:2048]),
          inc=("p_vector", 1))
    P.wait("sync", "p_vector", P.val("p_vector"))
    ld = _load(P, stage[0:32, 0:4096], aps["seln"])
    P.wait("vector", "ld", ld)
    v_all = P.add("vector", lambda e: e.tensor_copy(seln_bf[:], stage[0:32, 0:4096]),
                  inc=("p_vector", 1))
    P.wait("tensor", "p_vector", v_all)
    ev = 0
    for tc in range(NTC):
        P.wait("tensor", "p_scalar", ev)
        def mmbc(e, tc=tc):
            for dc in range(NDC):
                ins = e.matmul(pspool[0:32, 0:512],
                               wbc[:, dc * 32:(dc + 1) * 32],
                               xln[:, dc * T + tc * 512:dc * T + (tc + 1) * 512],
                               start=(dc == 0), stop=(dc == NDC - 1))
            return ins
        t_bc = P.add("tensor", mmbc, inc=("p_tensor", 1))
        P.wait("scalar", "p_tensor", t_bc)
        ev = P.add("scalar", lambda e, tc=tc: e.copy(
            BC[:, tc * 512:(tc + 1) * 512], pspool[0:32, 0:512]),
            inc=("p_scalar", 1))
        P.wait("tensor", "p_scalar", ev)
        def mmtmp(e, tc=tc):
            for dc in range(NDC):
                ins = e.matmul(pspool[:, 512:1024],
                               wdd[:, dc * 128:(dc + 1) * 128],
                               xln[:, dc * T + tc * 512:dc * T + (tc + 1) * 512],
                               start=(dc == 0), stop=(dc == NDC - 1))
            return ins
        t_tm = P.add("tensor", mmtmp, inc=("p_tensor", 1))
        P.wait("scalar", "p_tensor", t_tm)
        ev = P.add("scalar", lambda e, tc=tc: e.copy(
            tmpT[:, tc * 512:(tc + 1) * 512], pspool[:, 512:1024]),
            inc=("p_scalar", 1))
    P.wait("tensor", "p_scalar", ev)


def _l1a_scan_tc(P, B, tc):
    nc = P.nc
    xln, tmpT, BC, wdt, seln_bf = B["xln"], B["tmpT"], B["BC"], B["wdt"], B["seln_bf"]
    A_all, hstate, dtb, cpool = B["A_all"], B["hstate"], B["dtb"], B["cpool"]
    y_acc, pspool = B["y_acc"], B["pspool"]
    Bball, Cball = B["Bball"], B["Cball"]
    spa, spb, delta, du = B["spa"], B["spb"], B["delta"], B["du"]
    dAt, ut, ht, yt = B["dAt"], B["ut"], B["ht"], B["yt"]
    psB2 = pspool[:, 0:512]
    psC2 = pspool[:, 512:1024]
    psDt = pspool[:, 1024:1536]
    tsl = slice(tc * 512, (tc + 1) * 512)

    ev = P.val("p_scalar")
    for n in range(16):
        P.wait("tensor", "p_scalar", ev)
        def mmb(e, n=n):
            e.matmul(psB2, seln_bf[:, n * 128:(n + 1) * 128], BC[:, tsl],
                     start=True, stop=True)
            return e.matmul(psC2, seln_bf[:, (16 + n) * 128:(17 + n) * 128],
                            BC[:, tsl], start=True, stop=True)
        t_b = P.add("tensor", mmb, inc=("p_tensor", 1))
        P.wait("scalar", "p_tensor", t_b)
        def evb(e, n=n):
            e.copy(Bball[:, n * 512:(n + 1) * 512], psB2)
            return e.copy(Cball[:, n * 512:(n + 1) * 512], psC2)
        ev = P.add("scalar", evb, inc=("p_scalar", 1))

    for dc in range(16):
        P.wait("tensor", "p_scalar", P.val("p_scalar"))
        P.wait("tensor", "p_vector", P.val("p_vector"))
        P.wait("tensor", "p_gpsimd", P.val("p_gpsimd"))
        t_dt = P.add("tensor", lambda e, dc=dc: e.matmul(
            psDt, wdt[:, dc * 128:(dc + 1) * 128], tmpT[:, tsl],
            start=True, stop=True), inc=("p_tensor", 1))
        P.wait("scalar", "p_tensor", t_dt)
        P.addw("scalar", lambda e, dc=dc: e.activation(
            spa[:], psDt, AF.Abs, bias=dtb[:, dc:dc + 1]))
        P.addw("scalar", lambda e: e.activation(spa[:], spa[:], AF.Exp,
                                                scale=cpool[:, 4:5]))
        P.addw("scalar", lambda e: e.activation(spa[:], spa[:], AF.Ln,
                                                bias=cpool[:, 2:3]))
        s_sp = P.add("scalar", lambda e, dc=dc: e.activation(
            spb[:], psDt, AF.Relu, bias=dtb[:, dc:dc + 1]), inc=("p_scalar", 1))
        P.wait("vector", "p_scalar", s_sp)
        P.addw("vector", lambda e: e.tensor_add(delta[:], spa[:], spb[:]))
        v_du = P.add("vector", lambda e, dc=dc: e.tensor_mul(
            du[:], delta[:], xln[:, dc * T + tc * 512:dc * T + (tc + 1) * 512]),
            inc=("p_vector", 1))
        P.wait("gpsimd", "p_vector", v_du)
        P.wait("scalar", "p_vector", v_du)

        scan_done = [0, 0]
        yC_done = [0, 0]
        u_done = [0, 0]
        hcopy_done = [0, 0]
        yadd_done = [0, 0]
        for n in range(16):
            pp = n % 2
            col = dc * 16 + n
            P.wait("scalar", "p_vector", scan_done[pp])
            s_dA = P.add("scalar", lambda e, col=col, pp=pp: e.activation(
                dAt[:, pp * 512:(pp + 1) * 512], delta[:], AF.Exp,
                scale=A_all[:, col:col + 1]), inc=("p_scalar", 1))
            P.wait("gpsimd", "p_vector", scan_done[pp])
            g_u = P.add("gpsimd", lambda e, n=n, pp=pp: e.tensor_mul(
                ut[:, pp * 512:(pp + 1) * 512], du[:],
                Bball[:, n * 512:(n + 1) * 512]), inc=("p_gpsimd", 1))
            P.wait("vector", "p_scalar", s_dA)
            P.wait("vector", "p_gpsimd", g_u)
            P.wait("vector", "p_scalar", hcopy_done[pp])
            v_scan = P.add("vector", lambda e, col=col, pp=pp: e.tensor_tensor_scan(
                ht[:, pp * 512:(pp + 1) * 512],
                dAt[:, pp * 512:(pp + 1) * 512],
                ut[:, pp * 512:(pp + 1) * 512],
                hstate[:, col:col + 1], ALU.mult, ALU.add), inc=("p_vector", 1))
            scan_done[pp] = v_scan
            P.wait("scalar", "p_vector", v_scan)
            hcopy_done[pp] = P.add("scalar", lambda e, col=col, pp=pp: e.copy(
                hstate[:, col:col + 1],
                ht[:, pp * 512 + 511:pp * 512 + 512]), inc=("p_scalar", 1))
            P.wait("vector", "p_vector", yadd_done[pp])
            if n == 0:
                yC_done[pp] = P.add("vector", lambda e, dc=dc, n=n, pp=pp:
                                    e.tensor_mul(
                    y_acc[:, dc * 512:(dc + 1) * 512],
                    ht[:, pp * 512:(pp + 1) * 512],
                    Cball[:, n * 512:(n + 1) * 512]), inc=("p_vector", 1))
                yadd_done[pp] = yC_done[pp]
            else:
                yC_done[pp] = P.add("vector", lambda e, n=n, pp=pp: e.tensor_mul(
                    yt[:, pp * 512:(pp + 1) * 512],
                    ht[:, pp * 512:(pp + 1) * 512],
                    Cball[:, n * 512:(n + 1) * 512]), inc=("p_vector", 1))
                P.wait("vector", "p_vector", yC_done[pp])
                yadd_done[pp] = P.add("vector", lambda e, dc=dc, pp=pp:
                                      e.tensor_add(
                    y_acc[:, dc * 512:(dc + 1) * 512],
                    y_acc[:, dc * 512:(dc + 1) * 512],
                    yt[:, pp * 512:(pp + 1) * 512]), inc=("p_vector", 1))


def _l1a_outproj(P, aps, B, tc, x_in, x_out):
    nc = P.nc
    xln, y_acc, pspool = B["xln"], B["y_acc"], B["pspool"]
    stages, wops, y_out = B["stages"], B["wops"], B["y_out"]
    opb, Dv, cpool, ones_bf, stats = B["opb"], B["Dv"], B["cpool"], B["ones_bf"], B["stats"]
    xf2, xsq_dc = B["xf2"], B["xsq_dc"]
    psO = [pspool[:, 2048:2560], pspool[:, 2560:3072]]
    ybf = B["ybf"]
    s_yb = P.add("vector", lambda e: e.tensor_copy(ybf[:], y_acc[:]),
                 inc=("p_vector", 1))
    P.wait("scalar", "p_vector", s_yb)

    slabctx = B.setdefault("oslab", {"i": 0, "guards": [0, 0]})
    mm_done = {-1: P.val("p_tensor"), -2: P.val("p_tensor")}
    cast_done = {}

    def emit_load(p):
        if p >= 8:
            return
        for j in range(4):
            sb = stages[slabctx["i"] % 2]
            P.wait("sync", "p_scalar", slabctx["guards"][slabctx["i"] % 2])
            if j == 0:
                P.wait("sync", "p_tensor", mm_done[p - 2])
            ld = _load(P, sb[:].rearrange("p (c i) -> p c i", i=256),
                       aps["op_T"][512 * j:512 * (j + 1), p * 256:(p + 1) * 256]
                       .rearrange("(c p2) i -> p2 c i", p2=128))
            P.wait("scalar", "ld", ld)
            if j == 0:
                P.wait("scalar", "p_tensor", mm_done[p - 2])
            tgt = P.add("scalar", lambda e, j=j, sb=sb, p=p: e.copy(
                wops[p % 2][:, j * 1024:(j + 1) * 1024], sb[:]),
                inc=("p_scalar", 1))
            slabctx["guards"][slabctx["i"] % 2] = tgt
            slabctx["i"] += 1
        cast_done[p] = tgt

    emit_load(0)
    relu_guard = 0
    ev_prev = [0, 0]
    P.wait("tensor", "p_vector", s_yb)
    for p in range(8):
        emit_load(p + 1)
        P.wait("tensor", "p_scalar", cast_done[p])
        wop = wops[p % 2]
        for ot in range(2):
            oc = 2 * p + ot
            pp = oc % 2
            P.wait("tensor", "p_vector", ev_prev[pp])
            def mko(e, ot=ot, pp=pp, wop=wop):
                for dc in range(NDC):
                    ins = e.matmul(
                        psO[pp],
                        wop[:, dc * 256 + ot * 128:dc * 256 + (ot + 1) * 128],
                        ybf[:, dc * 512:(dc + 1) * 512],
                        start=(dc == 0), stop=(dc == NDC - 1))
                return ins
            t_o = P.add("tensor", mko, inc=("p_tensor", 1))
            P.wait("scalar", "p_tensor", t_o)
            P.wait("scalar", "p_vector", relu_guard)
            s_r = P.add("scalar", lambda e, pp=pp, oc=oc: e.activation(
                B["relu_t"][:, pp * 512:(pp + 1) * 512], psO[pp], AF.Relu,
                bias=opb[:, oc:oc + 1]), inc=("p_scalar", 1))
            P.wait("vector", "p_scalar", s_r)
            ev_prev[pp] = P.add("vector", lambda e, oc=oc, pp=pp:
                                e.scalar_tensor_tensor(
                y_out[:, oc * 512:(oc + 1) * 512],
                xln[:, oc * T + tc * 512:oc * T + (tc + 1) * 512],
                Dv[:, oc:oc + 1], B["relu_t"][:, pp * 512:(pp + 1) * 512],
                ALU.mult, ALU.add), inc=("p_vector", 1))
        mm_done[p] = P.val("p_tensor")
    P.wait("scalar", "p_vector", P.val("p_vector"))
    st, v_y = emit_out_ln_residual(
        P, tc, x_in, x_out, y_out, xf2, xsq_dc, ones_bf,
        stats, cpool, pspool[:, 3072:3584], pspool[:, 3584:4096])
    return v_y


def emit_l1a(P, aps):
    nc = P.nc
    x_in, x_out = aps["x_in"], aps["x_out"]
    with Scope(nc) as S:
        cpool = S.sb("cpool", [128, 8], F32)
        ones_f = S.sb("onesf", [128, 128], F32)
        ones_bf = S.sb("onesbf", [128, 128], BF16)
        st_mean = S.sb("stmean", [128, 512], F32)
        st_msq = S.sb("stmsq", [128, 512], F32)
        st_var = S.sb("stvar", [128, 512], F32)
        big_bf = S.sb("bigbf", [128, NDC * T], BF16)
        y_acc = S.sb("y_acc", [128, 16 * 512], F32)
        tmpT = S.sb("tmpT", [128, T], BF16)
        BC = S.sb("BC", [32, T], BF16)
        wdt = S.sb("wdt", [128, T], BF16)
        seln_bf = S.sb("selnbf", [32, 4096], BF16)
        A_all = S.sb("A_all", [128, 256], F32)
        hstate = S.sb("hstate", [128, 256], F32)
        dtb = S.sb("dtb", [128, 16], F32)
        opb = S.sb("opb", [128, 16], F32)
        Dv = S.sb("Dv", [128, 16], F32)
        pspool = S.ps("pspool", [128, 4096], F32)
        stats = {"mean": st_mean, "msq": st_msq, "var": st_var, "rstd": st_var}
        xln = big_bf

        _load(P, cpool[:], aps["consts"])
        _load(P, ones_f[:], aps["ones128"])
        _load(P, dtb[:], aps["dt_bias"])
        _load(P, opb[:], aps["op_bias"])
        _load(P, Dv[:], aps["Dvec"])
        ldc = _load(P, A_all[:], aps["A_log"])
        P.wait("vector", "ld", ldc)
        P.add("vector", lambda e: e.tensor_copy(ones_bf[:], ones_f[:]),
              inc=("p_vector", 1))
        P.add("vector", lambda e: e.memset(hstate[:], 0.0), inc=("p_vector", 1))
        v_al = P.val("p_vector")
        P.wait("scalar", "p_vector", v_al)
        s_a = P.add("scalar", lambda e: e.activation(A_all[:], A_all[:], AF.Exp),
                    inc=("p_scalar", 1))
        P.wait("vector", "p_scalar", s_a)
        P.addw("vector", lambda e: e.tensor_scalar_mul(A_all[:], A_all[:], -1.0))

        with Scope(nc) as S1:
            xf = S1.sb("xf", [128, NDC * 512], F32)
            xb = S1.sb("xb", [128, NDC * 512], BF16)
            xsq = S1.sb("xsq", [128, NDC * 512], BF16)
            emit_ln_in(P, x_in, xln, xf, xb, xsq, ones_bf, stats, cpool,
                       pspool[:, 0:512], pspool[:, 512:1024])
            P.flush()

        with Scope(nc) as S2:
            stage = S2.sb("stageP", [128, 4096], F32)
            wbc = S2.sb("wbc", [128, 512], BF16)
            wdd = S2.sb("wdd", [128, 2048], BF16)
            _l1a_projections(P, aps, dict(
                xln=xln, tmpT=tmpT, BC=BC, wdt=wdt, seln_bf=seln_bf,
                stage=stage, wbc=wbc, wdd=wdd, pspool=pspool))
            P.flush()

        for tc in range(NTC):
            soft_barrier(P)
            with Scope(nc) as SS:
                Bball = SS.sb("Bball%d" % tc, [128, 16 * 512], BF16)
                Cball = SS.sb("Cball%d" % tc, [128, 16 * 512], BF16)
                spa = SS.sb("spa%d" % tc, [128, 512], F32)
                spb = SS.sb("spb%d" % tc, [128, 512], F32)
                delta = SS.sb("delta%d" % tc, [128, 512], F32)
                du = SS.sb("du%d" % tc, [128, 512], F32)
                dAt = SS.sb("dAt%d" % tc, [128, 1024], F32)
                ut = SS.sb("ut%d" % tc, [128, 1024], F32)
                ht = SS.sb("ht%d" % tc, [128, 1024], BF16)
                yt = SS.sb("yt%d" % tc, [128, 1024], BF16)
                _l1a_scan_tc(P, dict(
                    xln=xln, tmpT=tmpT, BC=BC, wdt=wdt, seln_bf=seln_bf,
                    A_all=A_all, hstate=hstate, dtb=dtb, cpool=cpool,
                    y_acc=y_acc, pspool=pspool, Bball=Bball, Cball=Cball,
                    spa=spa, spb=spb, delta=delta, du=du, dAt=dAt, ut=ut,
                    ht=ht, yt=yt), tc)
                P.flush()
            soft_barrier(P)
            with Scope(nc) as SO:
                sync_fence(P)
                stO = [SO.sb("stOA%d" % tc, [128, 1024], F32),
                       SO.sb("stOB%d" % tc, [128, 1024], F32)]
                wops = [SO.sb("wopA%d" % tc, [128, NDC * 256], BF16),
                        SO.sb("wopB%d" % tc, [128, NDC * 256], BF16)]
                ybf = SO.sb("ybf%d" % tc, [128, 16 * 512], BF16)
                y_out = SO.sb("yout%d" % tc, [128, NDC * 512], BF16)
                relu_t = SO.sb("relu%d" % tc, [128, 1024], BF16)
                xf2 = SO.sb("xf2_%d" % tc, [128, 4096], F32)
                xsq_dc = ybf[:, 0:4096]
                _l1a_outproj(P, aps, dict(
                    xln=xln, y_acc=y_acc, pspool=pspool, stages=stO,
                    wops=wops, ybf=ybf, y_out=y_out, relu_t=relu_t,
                    opb=opb, Dv=Dv, cpool=cpool, ones_bf=ones_bf, stats=stats,
                    xf2=xf2, xsq_dc=xsq_dc), tc, x_in, x_out)
                P.flush()
    P.wait("gpsimd", "gd", P.val("gd"))
    P.flush()


L1A_KEYS = ("consts", "ones128", "WBC", "dt_down", "dtw_T", "dt_bias",
            "A_log", "Dvec", "op_T", "op_bias", "seln")


def build_l1a(H):
    nc = bass.Bass("TRN2", target_bir_lowering=False)
    aps = {}
    for name in L1A_KEYS:
        aps[name] = nc.dram_tensor(name, list(H[name].shape), F32,
                                   kind="ExternalInput").ap()
    aps["x_in"] = nc.dram_tensor("x_in", [D, T], F32, kind="ExternalInput").ap()
    aps["x_out"] = nc.dram_tensor("x_out", [D, T], F32, kind="ExternalOutput").ap()
    with nc.Block() as block:
        P = Prog(nc, block)
        emit_l1a(P, aps)
    return nc


ALL_KEYS = tuple(dict.fromkeys(L0A_KEYS + L0F_KEYS + L1A_KEYS + L1F_KEYS))


def build_fused(H):
    nc = bass.Bass("TRN2", target_bir_lowering=False)
    aps = {}
    for name in ALL_KEYS:
        aps[name] = nc.dram_tensor(name, list(H[name].shape), F32,
                                   kind="ExternalInput").ap()
    x_in = nc.dram_tensor("x_in", [D, T], F32, kind="ExternalInput").ap()
    x_out = nc.dram_tensor("x_out", [D, T], F32, kind="ExternalOutput").ap()
    xs_a = nc.dram_tensor("xs_a", [D, T], F32, kind="Internal").ap()
    xs_b = nc.dram_tensor("xs_b", [D, T], F32, kind="Internal").ap()
    with nc.Block() as block:
        P = Prog(nc, block)
        for emit, xin, xout in ((emit_l0a, x_in, xs_a),
                                (emit_l0f, xs_a, xs_b),
                                (emit_l1a, xs_b, xs_a),
                                (emit_l1f, xs_a, x_out)):
            a = dict(aps)
            a["x_in"], a["x_out"] = xin, xout
            emit(P, a)
            soft_barrier(P)
            sync_fence(P)
        P.wait("gpsimd", "gd", P.val("gd"))
        P.flush()
    return nc


FUSED = True


def kernel(**inputs):
    if FUSED:
        return kernel_fused(**inputs)
    return kernel_unfused(**inputs)


def kernel_fused(**inputs):
    H = _prep_host(inputs)
    x = np.asarray(inputs["L0_attn_x"], np.float32)
    ncores = 8
    nc = build_fused(H)
    common = {k: H[k] for k in ALL_KEYS}
    in_maps = [dict(common, x_in=np.ascontiguousarray(x[b].T))
               for b in range(ncores)]
    res = run_bass_kernel_spmd(nc, in_maps, core_ids=list(range(ncores)))
    out = np.stack([np.asarray(r["x_out"]).T for r in res.results])
    return out.astype(np.float32)


# version 07
# speedup vs baseline: 1.5258x; 1.0139x over previous
import numpy as np
import concourse.bass as bass
import concourse.mybir as mybir
from concourse.bass_utils import run_bass_kernel_spmd

F32 = mybir.dt.float32
BF16 = mybir.dt.bfloat16
AF = mybir.ActivationFunctionType
ALU = mybir.AluOpType
AX = mybir.AxisListType

D = 2048
T = 2048
NDC = D // 128
NTC = T // 512
EPS = 1e-5


def _prep_host(inputs):
    H = {}
    f32 = np.float32

    H["identity"] = np.eye(128, dtype=f32)
    H["ones128"] = np.ones((128, 128), dtype=f32)
    half = 32
    inv_freq = 1.0 / (10000.0 ** (np.arange(half, dtype=np.float64) * 2.0 / 64.0))
    pos = np.arange(T, dtype=np.float64)
    ang = pos[None, :] * inv_freq[:, None]
    cos = np.cos(ang).astype(f32)
    sin = np.sin(ang).astype(f32)
    H["cos"] = cos
    H["sin"] = sin
    am = np.zeros((128, 4, 512), dtype=f32)
    for p in range(4):
        for b in range(4):
            blk = am[:, p, 128 * b:128 * (b + 1)]
            if b > p:
                blk[:] = -30000.0
            elif b == p:
                r = np.arange(128)
                blk[:] = np.where(r[None, :] <= r[:, None], 0.0, -30000.0)
    H["smask"] = am.reshape(128, 4 * 512)
    pm = np.zeros((128, 4, 512), dtype=f32)
    for p in range(4):
        for b in range(4):
            blk = pm[:, p, 128 * b:128 * (b + 1)]
            if b < p:
                blk[:] = 0.0
            elif b > p:
                blk[:] = 1.0
            else:
                r = np.arange(128)
                blk[:] = np.where(r[:, None] <= r[None, :], 1.0, 0.0)
    H["ptmask"] = pm.reshape(128, 4 * 512)
    cst = np.zeros((128, 8), dtype=f32)
    cst[:, 0] = EPS
    cst[:, 1] = 1.0 / D
    cst[:, 2] = 1.0
    cst[:, 3] = 1.0 / np.sqrt(128.0)
    cst[:, 4] = -1.0
    cst[:, 5] = -0.5
    H["consts"] = cst
    seln = np.zeros((32, 32 * 128), dtype=f32)
    for j in range(32):
        seln[j, j * 128:(j + 1) * 128] = 1.0
    H["seln"] = seln

    W_uq = np.asarray(inputs["L0_attn_W_uq"], f32).reshape(512, 16, 64)
    W_qr = np.asarray(inputs["L0_attn_W_qr"], f32).reshape(512, 16, 64)
    W_uk = np.asarray(inputs["L0_attn_W_uk"], f32).reshape(512, 16, 64)
    W_kr = np.asarray(inputs["L0_attn_W_kr"], f32).reshape(512, 16, 64)
    q_comb = np.concatenate([W_qr[:, :, :32], W_qr[:, :, 32:], W_uq], axis=2)
    k_comb = np.concatenate([W_kr[:, :, :32], W_kr[:, :, 32:], W_uk], axis=2)
    H["Wq_comb"] = np.ascontiguousarray(q_comb.reshape(512, 2048))
    H["Wk_comb"] = np.ascontiguousarray(k_comb.reshape(512, 2048))
    H["W_dkv"] = np.asarray(inputs["L0_attn_W_dkv"], f32)
    H["W_dq"] = np.asarray(inputs["L0_attn_W_dq"], f32)
    H["W_uv"] = np.asarray(inputs["L0_attn_W_uv"], f32)
    H["W_out"] = np.asarray(inputs["L0_attn_W_out"], f32)
    H["b_out"] = np.ascontiguousarray(
        np.asarray(inputs["L0_attn_b_out"], f32).reshape(16, 128).T)

    H["rw_T"] = np.ascontiguousarray(np.asarray(inputs["L0_ffn_router_weight"], f32).T)
    H["sh_gate_T"] = np.ascontiguousarray(np.asarray(inputs["L0_ffn_sh_gate"], f32).T)
    H["sh_up_T"] = np.ascontiguousarray(np.asarray(inputs["L0_ffn_sh_up"], f32).T)
    H["sh_down_T"] = np.ascontiguousarray(np.asarray(inputs["L0_ffn_sh_down"], f32).T)
    H["r_gate_T"] = np.ascontiguousarray(np.asarray(inputs["L0_ffn_r_gate"], f32).T)
    H["r_up_T"] = np.ascontiguousarray(np.asarray(inputs["L0_ffn_r_up"], f32).T)
    H["r_down_T"] = np.ascontiguousarray(np.asarray(inputs["L0_ffn_r_down"], f32).T)

    W = np.asarray(inputs["L1_attn_x_proj_weight"], f32).T
    H["dt_down"] = np.ascontiguousarray(W[:, :128])
    H["WBC"] = np.ascontiguousarray(W[:, 128:160])
    H["dtw_T"] = np.ascontiguousarray(np.asarray(inputs["L1_attn_dt_proj_weight"], f32).T)
    H["dt_bias"] = np.ascontiguousarray(
        np.asarray(inputs["L1_attn_dt_proj_bias"], f32).reshape(16, 128).T)
    H["A_log"] = np.ascontiguousarray(
        np.asarray(inputs["L1_attn_A_log"], f32).reshape(16, 128, 16).transpose(1, 0, 2).reshape(128, 16 * 16))
    H["Dvec"] = np.ascontiguousarray(
        np.asarray(inputs["L1_attn_D"], f32).reshape(16, 128).T)
    H["op_T"] = np.ascontiguousarray(np.asarray(inputs["L1_attn_out_proj_weight"], f32).T)
    H["op_bias"] = np.ascontiguousarray(
        np.asarray(inputs["L1_attn_out_proj_bias"], f32).reshape(16, 128).T)

    H["f_gate_T"] = np.ascontiguousarray(np.asarray(inputs["L1_ffn_gate_w"], f32).T)
    H["f_up_T"] = np.ascontiguousarray(np.asarray(inputs["L1_ffn_up_w"], f32).T)
    H["f_down_T"] = np.ascontiguousarray(np.asarray(inputs["L1_ffn_down_w"], f32).T)
    H["gate_s"] = np.ascontiguousarray(
        np.asarray(inputs["L1_ffn_gate_s"], f32).reshape(44, 128).T)
    H["up_s"] = np.ascontiguousarray(
        np.asarray(inputs["L1_ffn_up_s"], f32).reshape(44, 128).T)
    H["down_s"] = np.ascontiguousarray(
        np.asarray(inputs["L1_ffn_down_s"], f32).reshape(16, 128).T)
    return H



ENGINES = ("sync", "tensor", "scalar", "vector", "gpsimd")


class Prog:

    def __init__(self, nc, block):
        self.nc = nc
        self.block = block
        self.ops = {e: [] for e in ENGINES}
        self.semh = {}
        self.cnt = {}
        self._guards = []

    def sem(self, name):
        if name not in self.semh:
            cm = self.nc.semaphore(name)
            self.semh[name] = cm.__enter__()
            self._guards.append(cm)
            self.cnt[name] = 0
        return self.semh[name]

    def bump(self, name, amt):
        self.sem(name)
        self.cnt[name] += amt
        return self.cnt[name]

    def emit(self, e, fn):
        self.ops[e].append(fn)

    def add(self, e, builder, inc=None):
        if inc is not None:
            name, amt = inc
            tgt = self.bump(name, amt)
            h = self.semh[name]

            def f(eng, builder=builder, h=h, amt=amt):
                builder(eng).then_inc(h, amt)
            self.emit(e, f)
            return tgt
        self.emit(e, lambda eng, builder=builder: builder(eng))
        return None

    def addw(self, e, builder):
        tgt = self.add(e, builder, inc=("p_" + e, 1))
        self.wait(e, "p_" + e, tgt)
        return tgt

    def wait(self, e, name, tgt):
        h = self.sem(name)
        self.emit(e, lambda eng, h=h, tgt=tgt: eng.wait_ge(h, tgt))

    def val(self, name):
        return self.cnt.get(name, 0)

    def flush(self):
        for e in ENGINES:
            fns = self.ops[e]
            if not fns:
                continue
            self.ops[e] = []

            def chain(eng, fns=fns):
                for fn in fns:
                    fn(eng)
            getattr(self.block, e)(chain)


class Scope:

    _n = 0

    def __init__(self, nc):
        self.nc = nc
        self.cms = []
        Scope._n += 1
        self.uid = Scope._n

    def __enter__(self):
        return self

    def sb(self, name, shape, dtype):
        cm = self.nc.sbuf_tensor("%s_u%d" % (name, self.uid), shape, dtype)
        t = cm.__enter__()
        self.cms.append(cm)
        return t

    def ps(self, name, shape, dtype=F32):
        cm = self.nc.psum_tensor("%s_u%d" % (name, self.uid), shape, dtype)
        t = cm.__enter__()
        self.cms.append(cm)
        return t

    def __exit__(self, *a):
        for cm in reversed(self.cms):
            cm.__exit__(None, None, None)
        return False


def _load(P, dst, src, sem="ld"):
    tgt = P.bump(sem, 16)
    h = P.semh[sem]
    P.emit("sync", lambda eng, dst=dst, src=src, h=h: eng.dma_start(out=dst, in_=src).then_inc(h, 16))
    return tgt


def _store(P, dst, src, sem="gd"):
    tgt = P.bump(sem, 16)
    h = P.semh[sem]
    P.emit("gpsimd", lambda eng, dst=dst, src=src, h=h: eng.dma_start(out=dst, in_=src).then_inc(h, 16))
    return tgt


def chunk_dram(ap, tc, width=512):
    return ap[:, tc * width:(tc + 1) * width].rearrange("(c p) t -> p c t", p=128)


def _v3(tile, w=512):
    return tile[:].rearrange("p (c t) -> p c t", t=w)



SC = 1.0 / np.sqrt(128.0)


def emit_ln_in(P, x_in_ap, xln, xf, xb, xsq, ones_bf, stats, cpool, psA, psB):
    nc = P.nc
    mean, msq, var, rstd = stats["mean"], stats["msq"], stats["var"], stats["rstd"]
    v_norm_prev = P.val("p_vector")
    v_var_prev = 0
    g_prev = P.val("gd")
    for tc in range(NTC):
        P.wait("sync", "p_vector", v_norm_prev)
        ld = _load(P, _v3(xf), chunk_dram(x_in_ap, tc))
        P.wait("vector", "ld", ld)
        v_cast = P.add("vector", lambda e, xb=xb, xf=xf: e.tensor_copy(xb[:], xf[:]),
                       inc=("p_vector", 1))
        P.wait("scalar", "p_vector", v_cast)
        s_sq = P.add("scalar", lambda e, xsq=xsq, xb=xb: e.square(xsq[:], xb[:]),
                     inc=("p_scalar", 1))
        P.wait("tensor", "p_vector", v_var_prev)
        P.wait("tensor", "p_scalar", s_sq)
        def mk_stats(e, xb=xb, xsq=xsq, psA=psA, psB=psB, ones_bf=ones_bf):
            xb3 = _v3(xb)
            xq3 = _v3(xsq)
            for dc in range(NDC):
                e.matmul(psA[:], ones_bf[:], xb3[:, dc, :], start=(dc == 0), stop=(dc == NDC - 1))
            for dc in range(NDC):
                ins = e.matmul(psB[:], ones_bf[:], xq3[:, dc, :], start=(dc == 0), stop=(dc == NDC - 1))
            return ins
        t_stats = P.add("tensor", mk_stats, inc=("p_tensor", 1))
        P.wait("vector", "p_tensor", t_stats)
        P.addw("vector", lambda e: e.tensor_scalar_mul(mean[:], psA[:], 1.0 / D))
        P.addw("vector", lambda e: e.tensor_mul(msq[:], mean[:], mean[:]))
        v_var = P.add("vector",
                      lambda e: e.scalar_tensor_tensor(var[:], psB[:], 1.0 / D,
                                                       msq[:], ALU.mult, ALU.subtract),
                      inc=("p_vector", 1))
        v_var_prev = v_var
        P.wait("scalar", "p_vector", v_var)
        P.addw("scalar", lambda e: e.activation(msq[:], var[:], AF.Ln,
                                                bias=cpool[:, 0:1]))
        s_rstd = P.add("scalar",
                       lambda e: e.activation(rstd[:], msq[:], AF.Exp,
                                              scale=cpool[:, 5:6]),
                       inc=("p_scalar", 1))
        P.wait("vector", "p_scalar", s_rstd)
        xf3 = _v3(xf)
        xl3 = xln[:].rearrange("p (c t) -> p c t", t=T)
        for dc in range(NDC):
            P.addw("vector", lambda e, dc=dc: e.tensor_sub(msq[:], xf3[:, dc, :], mean[:]))
            last = P.add("vector",
                         lambda e, dc=dc, tc=tc: e.tensor_mul(
                             xl3[:, dc, tc * 512:(tc + 1) * 512], msq[:], rstd[:]),
                         inc=("p_vector", 1) if dc == NDC - 1 else None)
            if dc < NDC - 1:
                P.wait("vector", "p_vector", P.val("p_vector"))
        v_norm_prev = last
    return v_norm_prev


def emit_out_ln_residual(P, tc, x_in_ap, x_out_ap, y_tc, xf, xsq_dc, ones_bf,
                         stats, cpool, psA, psB, dbg_stats=None):
    mean, msq, var, rstd = stats["mean"], stats["msq"], stats["var"], stats["rstd"]
    y3 = _v3(y_tc)
    xq3 = xsq_dc[:].rearrange("p (c t) -> p c t", t=512)
    t_used = [0, 0]
    sq_done = [0, 0]
    for g in range(2):
        P.wait("scalar", "p_tensor", t_used[g - 1] if g else 0)
        def sq(e, g=g):
            return e.square(xsq_dc[:], y3[:, g * 8:(g + 1) * 8, :])
        sq_done[g] = P.add("scalar", sq, inc=("p_scalar", 1))
        P.wait("tensor", "p_scalar", sq_done[g])
        def mm(e, g=g):
            for j in range(8):
                dc = g * 8 + j
                e.matmul(psA[:], ones_bf[:], y3[:, dc, :],
                         start=(dc == 0), stop=(dc == NDC - 1))
            for j in range(8):
                dc = g * 8 + j
                ins = e.matmul(psB[:], ones_bf[:], xq3[:, j, :],
                               start=(dc == 0), stop=(dc == NDC - 1))
            return ins
        t_used[g] = P.add("tensor", mm, inc=("p_tensor", 1))
    t_stats = t_used[1]
    P.wait("vector", "p_tensor", t_stats)
    P.addw("vector", lambda e: e.tensor_scalar_mul(mean[:], psA[:], 1.0 / D))
    P.addw("vector", lambda e: e.tensor_mul(msq[:], mean[:], mean[:]))
    v_var = P.add("vector", lambda e: e.scalar_tensor_tensor(
        var[:], psB[:], 1.0 / D, msq[:], ALU.mult, ALU.subtract),
        inc=("p_vector", 1))
    P.wait("scalar", "p_vector", v_var)
    P.addw("scalar", lambda e: e.activation(msq[:], var[:], AF.Ln, bias=cpool[:, 0:1]))
    s_rstd = P.add("scalar", lambda e: e.activation(rstd[:], msq[:], AF.Exp,
                                                    scale=cpool[:, 5:6]),
                   inc=("p_scalar", 1))
    if dbg_stats is not None and tc == dbg_stats[0]:
        P.wait("gpsimd", "p_scalar", s_rstd)
        _store(P, dbg_stats[1], mean[:])
        _store(P, dbg_stats[2], rstd[:])
        P.wait("sync", "gd", P.val("gd"))
    sync_fence(P)
    xf3 = xf[:].rearrange("p (c t) -> p c t", t=512)
    st = 0
    for g in range(2):
        P.wait("sync", "gd", st if isinstance(st, int) else 0)
        if g == 1:
            P.wait("sync", "gd", st)
        ld = _load(P, xf3, chunk_dram(x_in_ap, tc)[:, g * 8:(g + 1) * 8, :])
        P.wait("vector", "ld", ld)
        if g == 0:
            P.wait("vector", "p_scalar", s_rstd)
        for j in range(8):
            dc = g * 8 + j
            P.addw("vector", lambda e, dc=dc: e.tensor_sub(msq[:], y3[:, dc, :], mean[:]))
            P.addw("vector", lambda e: e.tensor_mul(msq[:], msq[:], rstd[:]))
            P.add("vector", lambda e, j=j: e.tensor_add(
                xf3[:, j, :], xf3[:, j, :], msq[:]),
                inc=("p_vector", 1) if j == 7 else None)
        v_res = P.val("p_vector")
        P.wait("gpsimd", "p_vector", v_res)
        st = _store(P, chunk_dram(x_out_ap, tc)[:, g * 8:(g + 1) * 8, :], xf3)
    return st, v_res




def _l0a_phaseC(P, aps, xln, ckvT, cqT, stage, wbuf, pspool):
    sync_fence(P)
    psC = [pspool[:, 0:512], pspool[:, 512:1024]]
    ev_prev = [0, 0]
    for wkey, dstT in (("W_dkv", ckvT), ("W_dq", cqT)):
        P.wait("sync", "p_vector", P.val("p_vector"))
        ldw = _load(P, stage[:].rearrange("p (c i) -> p c i", i=512),
                    aps[wkey].rearrange("(c p) i -> p c i", p=128))
        P.wait("vector", "ld", ldw)
        v_w = P.add("vector", lambda e, wbuf=wbuf, stage=stage:
                    e.tensor_copy(wbuf[:], stage[:]), inc=("p_vector", 1))
        if wkey == "W_dq" and "dbg_wbuf" in aps:
            P.wait("gpsimd", "p_vector", v_w)
            _store(P, aps["dbg_wbuf"], wbuf[:])
            P.wait("tensor", "gd", P.val("gd"))
        P.wait("tensor", "p_vector", v_w)
        for ct in range(4):
            for tc in range(4):
                g = ct * 4 + tc
                pp = g % 2
                P.wait("tensor", "p_scalar", ev_prev[pp])
                def mk(e, pp=pp, ct=ct, tc=tc, wbuf=wbuf, xln=xln):
                    for dc in range(NDC):
                        ins = e.matmul(
                            psC[pp],
                            wbuf[:, dc * 512 + ct * 128: dc * 512 + (ct + 1) * 128],
                            xln[:, dc * T + tc * 512: dc * T + (tc + 1) * 512],
                            start=(dc == 0), stop=(dc == NDC - 1))
                    return ins
                t_g = P.add("tensor", mk, inc=("p_tensor", 1))
                P.wait("scalar", "p_tensor", t_g)
                def ev(e, pp=pp, ct=ct, tc=tc, dstT=dstT):
                    return e.copy(
                        dstT[:, ct * T + tc * 512: ct * T + (tc + 1) * 512],
                        psC[pp])
                ev_prev[pp] = P.add("scalar", ev, inc=("p_scalar", 1))
        P.wait("vector", "p_tensor", P.val("p_tensor"))
    P.wait("tensor", "p_scalar", P.val("p_scalar"))


def _l0a_loadD(P, aps, cosT, sinT, smask, ptmask, wq, wk, wuv, stage2):
    sync_fence(P)
    ld1 = _load(P, stage2[0:32, 0:2048], aps["cos"])
    ld1 = _load(P, stage2[0:32, 2048:4096], aps["sin"])
    P.wait("vector", "ld", ld1)
    P.add("vector", lambda e: e.tensor_copy(cosT[:], stage2[0:32, 0:2048]))
    v_t = P.add("vector", lambda e: e.tensor_copy(sinT[:], stage2[0:32, 2048:4096]),
                inc=("p_vector", 1))
    _load(P, smask[:], aps["smask"])
    P.wait("sync", "p_vector", v_t)
    ld2 = _load(P, stage2[:, 0:2048], aps["ptmask"])
    P.wait("vector", "ld", ld2)
    v_t = P.add("vector", lambda e: e.tensor_copy(ptmask[:], stage2[:, 0:2048]),
                inc=("p_vector", 1))
    for wkey, dst in (("Wq_comb", wq), ("Wk_comb", wk), ("W_uv", wuv)):
        for ch in range(2):
            P.wait("sync", "p_vector", P.val("p_vector"))
            ldw = _load(P, stage2[:].rearrange("p (c i) -> p c i", i=2048),
                        aps[wkey][256 * ch:256 * (ch + 1), :]
                        .rearrange("(c p) i -> p c i", p=128))
            P.wait("vector", "ld", ldw)
            P.add("vector", lambda e, dst=dst, ch=ch, stage2=stage2:
                  e.tensor_copy(dst[:, ch * 2 * T:(ch + 1) * 2 * T], stage2[:]),
                  inc=("p_vector", 1))


def _l0a_phaseE(P, aps, ctxT, stage3, woutp, y_tc, xf2, xsq_dc, ones_bf, bout,
                stats, cpool, pspool, x_in, x_out):
    sync_fence(P)
    psO = [pspool[:, 0:512], pspool[:, 512:1024]]
    ev_prev = [0, 0]
    v_res_prev = 0
    for tc in range(NTC):
        for p in range(4):
            for ch in range(2):
                P.wait("sync", "p_vector", P.val("p_vector"))
                ldw = _load(P, stage3[:].rearrange("p (c i) -> p c i", i=512),
                            aps["W_out"][1024 * ch:1024 * (ch + 1),
                                         p * 512:(p + 1) * 512]
                            .rearrange("(c p2) i -> p2 c i", p2=128))
                P.wait("vector", "ld", ldw)
                P.wait("vector", "p_tensor", P.val("p_tensor"))
                P.add("vector", lambda e, ch=ch, woutp=woutp, stage3=stage3:
                      e.tensor_copy(
                          woutp[:, ch * 8 * 512:(ch + 1) * 8 * 512], stage3[:]),
                      inc=("p_vector", 1))
            v_wp = P.val("p_vector")
            P.wait("tensor", "p_vector", v_wp)
            for ot in range(4):
                oc = 4 * p + ot
                pp = oc % 2
                P.wait("tensor", "p_scalar", ev_prev[pp])
                def mko(e, ot=ot, pp=pp, tc=tc, woutp=woutp, ctxT=ctxT):
                    for dc in range(NDC):
                        ins = e.matmul(
                            psO[pp],
                            woutp[:, dc * 512 + ot * 128:dc * 512 + (ot + 1) * 128],
                            ctxT[:, dc * T + tc * 512:dc * T + (tc + 1) * 512],
                            start=(dc == 0), stop=(dc == NDC - 1))
                    return ins
                t_o = P.add("tensor", mko, inc=("p_tensor", 1))
                P.wait("scalar", "p_tensor", t_o)
                P.wait("scalar", "p_vector", v_res_prev)
                def evo(e, pp=pp, oc=oc, y_tc=y_tc, bout=bout):
                    return e.activation(
                        y_tc[:, oc * 512:(oc + 1) * 512], psO[pp],
                        AF.Identity, bias=bout[:, oc:oc + 1])
                ev_prev[pp] = P.add("scalar", evo, inc=("p_scalar", 1))
        P.wait("sync", "gd", P.val("gd"))
        P.wait("tensor", "p_scalar", P.val("p_scalar"))
        st, v_res_prev = emit_out_ln_residual(
            P, tc, x_in, x_out, y_tc, xf2, xsq_dc, ones_bf,
            stats, cpool, pspool[:, 1024:1536], pspool[:, 1536:2048])


def emit_l0a(P, aps):
    nc = P.nc
    x_in, x_out = aps["x_in"], aps["x_out"]

    with Scope(nc) as S:
        cpool = S.sb("cpool", [128, 8], F32)
        ones_f = S.sb("onesf", [128, 128], F32)
        ones_bf = S.sb("onesbf", [128, 128], BF16)
        ident_f = S.sb("identf", [128, 128], F32)
        ident_bf = S.sb("identbf", [128, 128], BF16)
        ones2_bf = S.sb("ones2bf", [2, 128], BF16)
        bout = S.sb("bout", [128, 16], F32)
        st_mean = S.sb("stmean", [128, 512], F32)
        st_msq = S.sb("stmsq", [128, 512], F32)
        st_var = S.sb("stvar", [128, 512], F32)
        big_bf = S.sb("bigbf", [128, NDC * T], BF16)
        ckvT = S.sb("ckvT", [128, 4 * T], BF16)
        cqT = S.sb("cqT", [128, 4 * T], BF16)
        pspool = S.ps("pspool", [128, 4096], F32)

        stats = {"mean": st_mean, "msq": st_msq, "var": st_var, "rstd": st_var}
        xln = big_bf
        ctxT = big_bf

        _load(P, cpool[:], aps["consts"])
        _load(P, ones_f[:], aps["ones128"])
        _load(P, bout[:], aps["b_out"])
        ldi = _load(P, ident_f[:], aps["identity"])
        P.wait("vector", "ld", ldi)
        P.add("vector", lambda e: e.tensor_copy(ones_bf[:], ones_f[:]))
        P.add("vector", lambda e: e.tensor_copy(ones2_bf[:], ones_f[0:2, :]))
        P.add("vector", lambda e: e.tensor_copy(ident_bf[:], ident_f[:]),
              inc=("p_vector", 1))

        with Scope(nc) as S1:
            xf = S1.sb("xf", [128, NDC * 512], F32)
            xb = S1.sb("xb", [128, NDC * 512], BF16)
            xsq = S1.sb("xsq", [128, NDC * 512], BF16)
            emit_ln_in(P, x_in, xln, xf, xb, xsq, ones_bf, stats, cpool,
                       pspool[:, 0:512], pspool[:, 512:1024])
            P.flush()

        with Scope(nc) as S2:
            stage = S2.sb("stage", [128, 8192], F32)
            wbuf = S2.sb("wbuf", [128, NDC * 512], BF16)
            _l0a_phaseC(P, aps, xln, ckvT, cqT, stage, wbuf, pspool)
            if "dbg_ckvT" in aps:
                P.wait("gpsimd", "p_scalar", P.val("p_scalar"))
                _store(P, aps["dbg_ckvT"], ckvT[:])
                _store(P, aps["dbg_cqT"], cqT[:])
            P.flush()

        with Scope(nc) as S3:
            cosT = S3.sb("cosT", [32, T], BF16)
            sinT = S3.sb("sinT", [32, T], BF16)
            smask = S3.sb("sb_smask", [128, 2048], F32)
            ptmask = S3.sb("sb_ptmask", [128, 2048], BF16)
            wq = S3.sb("wq", [128, 4 * T], BF16)
            wk = S3.sb("wk", [128, 4 * T], BF16)
            wuv = S3.sb("wuv", [128, 4 * T], BF16)
            with Scope(nc) as S3a:
                stage2 = S3a.sb("stage2", [128, 4096], F32)
                _l0a_loadD(P, aps, cosT, sinT, smask, ptmask, wq, wk, wuv, stage2)
                P.flush()
            P.wait("tensor", "p_vector", P.val("p_vector"))

            with Scope(nc) as S3b:
                qh = S3b.sb("qh", [128, T], BF16)
                kh = S3b.sb("kh", [128, T], BF16)
                vh = S3b.sb("vh", [128, 16 * 128], BF16)
                ta = S3b.sb("ta", [32, 1024], F32)
                tb = S3b.sb("tb", [32, 1024], F32)
                junk = S3b.sb("junk", [128, 512], F32)
                racc = S3b.sb("racc", [128, 64], F32)
                nmx = S3b.sb("nmx", [128, 8], F32)
                nms_all = S3b.sb("nmsall", [128, 16], F32)
                scl16 = S3b.sb("scl16", [128, 48], F32)
                rowsrc = S3b.sb("rowsrc", [128, 32], BF16)
                Lrow2 = S3b.sb("Lrow2", [2, T], BF16)
                lrow_s = S3b.sb("lrows", [32, 128], BF16)
                ptb = [S3b.sb("ptb%d" % i, [128, 512], BF16) for i in range(4)]
                lrow_sc = nc.dram_tensor("lrow_sc_l0a", [32, 128], BF16,
                                         kind="Internal").ap()
                emit_l0a_heads(P, dict(
                    xln=xln, ckvT=ckvT, cqT=cqT, wq=wq, wk=wk, wuv=wuv,
                    cosT=cosT, sinT=sinT, smask=smask, ptmask=ptmask, ctxT=ctxT,
                    qh=qh, kh=kh, vh=vh, ta=ta, tb=tb, junk=junk, racc=racc,
                    nmx=nmx, nms_all=nms_all, scl16=scl16, rowsrc=rowsrc,
                    Lrow2=Lrow2, lrow_s=lrow_s, lrow_sc=lrow_sc, ptb=ptb,
                    psS=pspool[:, 0:2048],
                    psFull=pspool[:, 0:4096],
                    psCtx=pspool[:, 3072:3584],
                    psRow32=pspool[0:32, 3584:3648].bitcast(BF16),
                    ones2_bf=ones2_bf, ident_bf=ident_bf, cpool=cpool))
                P.flush()

        soft_barrier(P)
        with Scope(nc) as S4:
            stage3 = S4.sb("stage3", [128, 4096], F32)
            woutp = S4.sb("woutp", [128, NDC * 512], BF16)
            y_tc = S4.sb("y_tc", [128, NDC * 512], BF16)
            xf2 = S4.sb("xf2", [128, 4096], F32)
            xsq_dc = S4.sb("xsqdc", [128, 4096], BF16)
            _l0a_phaseE(P, aps, ctxT, stage3, woutp, y_tc, xf2, xsq_dc,
                        ones_bf, bout, stats, cpool, pspool, x_in, x_out)
            P.flush()
    P.wait("gpsimd", "gd", P.val("gd"))
    P.flush()



def sync_fence(P):
    for e in ("tensor", "scalar", "vector"):
        P.wait("sync", "p_" + e, P.val("p_" + e))
    P.wait("sync", "gd", P.val("gd"))


def soft_barrier(P, engines=("tensor", "scalar", "vector")):
    vals = {e: P.val("p_" + e) for e in engines}
    for e in engines:
        for o in engines:
            if o != e:
                P.wait(e, "p_" + o, vals[o])


def emit_l0a_heads(P, B):
    xln, ckvT, cqT = B["xln"], B["ckvT"], B["cqT"]
    wq, wk, wuv = B["wq"], B["wk"], B["wuv"]
    cosT, sinT, smask, ptmask = B["cosT"], B["sinT"], B["smask"], B["ptmask"]
    ctxT, qh, kh, vh = B["ctxT"], B["qh"], B["kh"], B["vh"]
    ta, tb, junk, racc, nmx = B["ta"], B["tb"], B["junk"], B["racc"], B["nmx"]
    nms_all, scl16, rowsrc = B["nms_all"], B["scl16"], B["rowsrc"]
    Lrow2, ptb = B["Lrow2"], B["ptb"]
    psQ = B["psS"]
    ps4 = [psQ[:, i * 512:(i + 1) * 512] for i in range(4)]
    psCtx, psRow32 = B["psCtx"], B["psRow32"]
    lrow_s, lrow_sc = B["lrow_s"], B["lrow_sc"]
    ones2_bf, ident_bf, cpool = B["ones2_bf"], B["ident_bf"], B["cpool"]
    SQ = float(np.sqrt(128.0))
    hi_view = rowsrc[:].rearrange("p (q two) -> p q two", two=2)[:, :, 0]
    lo_view = rowsrc[:].rearrange("p (q two) -> p q two", two=2)[:, :, 1]

    for h in range(16):
        soft_barrier(P)
        rope_v = 0
        base_s = 0
        for side in ("q", "k"):
            wqk = wq if side == "q" else wk
            srcT = cqT if side == "q" else ckvT
            dsth = qh if side == "q" else kh
            P.wait("tensor", "p_vector", rope_v)
            P.wait("tensor", "p_scalar", base_s)
            def mkp(e, wqk=wqk, srcT=srcT, h=h):
                for tc in range(4):
                    for ct in range(4):
                        ins = e.matmul(
                            ps4[tc],
                            wqk[:, ct * T + h * 128:ct * T + (h + 1) * 128],
                            srcT[:, ct * T + tc * 512:ct * T + (tc + 1) * 512],
                            start=(ct == 0), stop=(ct == 3))
                return ins
            t_p = P.add("tensor", mkp, inc=("p_tensor", 1))
            P.wait("vector", "p_tensor", t_p)
            for hh in range(2):
                hsl = slice(hh * 1024, (hh + 1) * 1024)
                def r1(e, hsl=hsl):
                    e.tensor_mul(ta[:], psQ[0:32, hsl], cosT[:, hsl])
                    return e.tensor_mul(tb[:], psQ[32:64, hsl], sinT[:, hsl])
                P.addw("vector", r1)
                P.add("vector", lambda e, dsth=dsth, hsl=hsl: e.tensor_sub(
                    dsth[0:32, hsl], ta[:], tb[:]))
                def r2(e, hsl=hsl):
                    e.tensor_mul(ta[:], psQ[32:64, hsl], cosT[:, hsl])
                    return e.tensor_mul(tb[:], psQ[0:32, hsl], sinT[:, hsl])
                P.addw("vector", r2)
                rope_v = P.add("vector", lambda e, dsth=dsth, hsl=hsl: e.tensor_add(
                    dsth[32:64, hsl], ta[:], tb[:]), inc=("p_vector", 1))
            P.wait("scalar", "p_tensor", t_p)
            base_s = P.add("scalar", lambda e, dsth=dsth: e.copy(
                dsth[64:128, :], psQ[64:128, :]), inc=("p_scalar", 1))
        ev_v = 0
        P.wait("tensor", "p_vector", rope_v)
        for gv in range(4):
            P.wait("tensor", "p_scalar", ev_v)
            def mkv(e, gv=gv, h=h):
                for i in range(4):
                    tt = gv * 4 + i
                    for ct in range(4):
                        ins = e.matmul(
                            ps4[i][:, 0:128],
                            ckvT[:, ct * T + tt * 128:ct * T + (tt + 1) * 128],
                            wuv[:, ct * T + h * 128:ct * T + (h + 1) * 128],
                            start=(ct == 0), stop=(ct == 3))
                return ins
            t_v = P.add("tensor", mkv, inc=("p_tensor", 1))
            P.wait("scalar", "p_tensor", t_v)
            def evv(e, gv=gv):
                for i in range(4):
                    tt = gv * 4 + i
                    ins = e.copy(vh[:, tt * 128:(tt + 1) * 128], ps4[i][:, 0:128])
                return ins
            ev_v = P.add("scalar", evv, inc=("p_scalar", 1))

        v_ms = P.add("vector", lambda e: e.memset(racc[:], 0.0),
                     inc=("p_vector", 1))
        s_exp_par = [ev_v, ev_v]
        psQfull = B["psFull"]
        for qi in range(16):
            nch = qi // 4 + 1
            par = qi % 2
            psS = psQfull[:, par * 2048:par * 2048 + 2048]
            P.wait("tensor", "p_scalar", s_exp_par[par])
            def mks(e, qi=qi, nch=nch, psS=psS):
                for kc in range(nch):
                    ins = e.matmul(
                        psS[:, kc * 512:(kc + 1) * 512],
                        qh[:, qi * 128:(qi + 1) * 128],
                        kh[:, kc * 512:(kc + 1) * 512],
                        start=True, stop=True)
                return ins
            t_s = P.add("tensor", mks, inc=("p_tensor", 1))
            P.wait("vector", "p_tensor", t_s)
            if qi == 0:
                P.wait("vector", "p_vector", v_ms)
            dg = (nch - 1) * 512
            P.addw("vector", lambda e, dg=dg, qi=qi, psS=psS: e.tensor_add(
                psS[:, dg:dg + 512], psS[:, dg:dg + 512],
                smask[:, (qi % 4) * 512:(qi % 4 + 1) * 512]))
            nxq = nmx[:, par * 4:par * 4 + 4]
            def mkmaxes(e, nch=nch, psS=psS, nxq=nxq):
                for kc in range(nch):
                    ins = e.reduce_max(nxq[:, kc:kc + 1],
                                       psS[:, kc * 512:(kc + 1) * 512],
                                       axis=AX.X, negate=True)
                return ins
            P.addw("vector", mkmaxes)
            P.addw("vector", lambda e, nch=nch, nxq=nxq, par=par: e.tensor_reduce(
                scl16[:, 46 + par:47 + par], nxq[:, 0:nch], axis=AX.X, op=ALU.min))
            v_m = P.add("vector", lambda e, qi=qi, par=par: e.tensor_scalar_mul(
                nms_all[:, qi:qi + 1], scl16[:, 46 + par:47 + par], SC),
                inc=("p_vector", 1))
            P.wait("scalar", "p_vector", v_m)
            jk = junk[:, par * 512:(par + 1) * 512] if False else junk[:]
            def mkexp(e, qi=qi, nch=nch, psS=psS):
                for kc in range(nch):
                    ins = e.activation(junk[:], psS[:, kc * 512:(kc + 1) * 512],
                                       AF.Exp, bias=nms_all[:, qi:qi + 1],
                                       scale=cpool[:, 3:4],
                                       accum_out=racc[:, 4 * qi + kc:4 * qi + kc + 1])
                return ins
            s_exp_par[par] = P.add("scalar", mkexp, inc=("p_scalar", 1))
        s_exp = max(s_exp_par)

        P.wait("vector", "p_scalar", s_exp)
        r3 = racc[:].rearrange("p (q k) -> p q k", k=4)
        def mksum(e):
            for qi in range(16):
                ins = e.tensor_reduce(scl16[:, qi:qi + 1], r3[:, qi, :],
                                      axis=AX.X, op=ALU.add)
            return ins
        v_rs = P.add("vector", mksum, inc=("p_vector", 1))
        P.wait("scalar", "p_vector", v_rs)
        s_ln = P.add("scalar", lambda e: e.activation(
            scl16[:, 16:32], scl16[:, 0:16], AF.Ln), inc=("p_scalar", 1))
        P.wait("vector", "p_scalar", s_ln)
        P.addw("vector", lambda e: e.tensor_sub(scl16[:, 32:48], nms_all[:],
                                                scl16[:, 16:32]))
        P.addw("vector", lambda e: e.tensor_scalar_mul(scl16[:, 32:48],
                                                       scl16[:, 32:48], SQ))
        P.addw("vector", lambda e: e.tensor_copy(hi_view, scl16[:, 32:48]))
        P.addw("vector", lambda e: e.tensor_copy(scl16[:, 0:16], hi_view))
        P.addw("vector", lambda e: e.tensor_sub(scl16[:, 16:32], scl16[:, 32:48],
                                                scl16[:, 0:16]))
        v_row = P.add("vector", lambda e: e.tensor_copy(lo_view, scl16[:, 16:32]),
                      inc=("p_vector", 1))
        P.wait("tensor", "p_vector", v_row)
        t_tr = P.add("tensor", lambda e: e.transpose(psRow32, rowsrc[:, 0:32],
                                                     ident_bf[:]),
                     inc=("p_tensor", 1))
        P.wait("scalar", "p_tensor", t_tr)
        s_lr = P.add("scalar", lambda e: e.copy(lrow_s[:], psRow32),
                     inc=("p_scalar", 1))
        P.wait("gpsimd", "p_scalar", s_lr)
        P.wait("gpsimd", "ld", P.val("ld"))
        st_lr = _store(P, lrow_sc, lrow_s[:])
        P.wait("sync", "gd", st_lr)
        ld_lr = _load(P, Lrow2[:].rearrange("two (q j) -> two q j", j=128),
                      lrow_sc.rearrange("(q two) j -> two q j", two=2))
        P.wait("tensor", "ld", ld_lr)

        exp_g = s_exp
        mask_g = 0
        ctx_mark = 0
        s_ctx_prev = 0
        for tqc in range(4):
            ntt = 4 * tqc + 4
            groups = [list(range(g, min(g + 4, ntt))) for g in range(0, ntt, 4)]
            for gi, tiles in enumerate(groups):
                last_grp = (gi == len(groups) - 1)
                P.wait("tensor", "p_scalar", exp_g)
                def mkpt(e, tiles=tiles, tqc=tqc):
                    for i, tt in enumerate(tiles):
                        e.matmul(ps4[i],
                                 kh[:, tt * 128:(tt + 1) * 128],
                                 qh[:, tqc * 512:(tqc + 1) * 512],
                                 start=True, stop=False)
                        ins = e.matmul(ps4[i], ones2_bf[:],
                                       Lrow2[0:2, tqc * 512:(tqc + 1) * 512],
                                       start=False, stop=True)
                    return ins
                t_pt = P.add("tensor", mkpt, inc=("p_tensor", 1))
                P.wait("scalar", "p_tensor", t_pt)
                P.wait("scalar", "p_tensor", ctx_mark)
                def mkexp2(e, tiles=tiles):
                    for i in range(len(tiles)):
                        ins = e.activation(ptb[i][:], ps4[i], AF.Exp,
                                           scale=cpool[:, 3:4])
                    return ins
                exp_g = P.add("scalar", mkexp2, inc=("p_scalar", 1))
                if last_grp:
                    P.wait("vector", "p_scalar", exp_g)
                    def mkm2(e, tiles=tiles, tqc=tqc):
                        for i, tt in enumerate(tiles):
                            pat = tt - 4 * tqc
                            ins = e.tensor_mul(ptb[i][:], ptb[i][:],
                                               ptmask[:, pat * 512:(pat + 1) * 512])
                        return ins
                    mask_g = P.add("vector", mkm2, inc=("p_vector", 1))
                    P.wait("tensor", "p_vector", mask_g)
                else:
                    P.wait("tensor", "p_scalar", exp_g)
                if gi == 0:
                    P.wait("tensor", "p_scalar", s_ctx_prev)
                def mkctx(e, tiles=tiles, ntt=ntt):
                    for i, tt in enumerate(tiles):
                        ins = e.matmul(psCtx, vh[:, tt * 128:(tt + 1) * 128],
                                       ptb[i][:], start=(tt == 0),
                                       stop=(tt == ntt - 1))
                    return ins
                ctx_mark = P.add("tensor", mkctx, inc=("p_tensor", 1))
            P.wait("scalar", "p_tensor", ctx_mark)
            def evctx(e, h=h, tqc=tqc):
                return e.copy(ctxT[:, h * T + tqc * 512:h * T + (tqc + 1) * 512],
                              psCtx)
            s_ctx_prev = P.add("scalar", evctx, inc=("p_scalar", 1))





L0A_KEYS = ("consts", "ones128", "identity", "cos", "sin", "smask", "ptmask",
            "b_out", "W_dkv", "W_dq", "Wq_comb", "Wk_comb", "W_uv", "W_out")


def build_l0a(H):
    nc = bass.Bass("TRN2", target_bir_lowering=False)
    aps = {}
    for name in L0A_KEYS:
        aps[name] = nc.dram_tensor(name, list(H[name].shape), F32,
                                   kind="ExternalInput").ap()
    aps["x_in"] = nc.dram_tensor("x_in", [D, T], F32, kind="ExternalInput").ap()
    aps["x_out"] = nc.dram_tensor("x_out", [D, T], F32, kind="ExternalOutput").ap()
    with nc.Block() as block:
        P = Prog(nc, block)
        emit_l0a(P, aps)
    return nc


def kernel_unfused(**inputs):
    H = _prep_host(inputs)
    x = np.asarray(inputs["L0_attn_x"], np.float32)
    ncores = 8
    cores = list(range(ncores))
    xT = [np.ascontiguousarray(x[b].T) for b in range(ncores)]

    for build, keys in ((build_l0a, L0A_KEYS), (build_l0f, L0F_KEYS),
                        (build_l1a, L1A_KEYS), (build_l1f, L1F_KEYS)):
        nc = build(H)
        common = {k: H[k] for k in keys}
        in_maps = [dict(common, x_in=xT[b]) for b in range(ncores)]
        res = run_bass_kernel_spmd(nc, in_maps, core_ids=cores)
        xT = [np.asarray(r["x_out"]) for r in res.results]

    out = np.stack([t.T for t in xT]).astype(np.float32)
    return out



def _ffn_emit_load(P, st, stage_bufs, slabctx, wdst, src_ap, mode, I, guard_t):
    for j in range(4):
        sb = stage_bufs[slabctx["i"] % 2]
        P.wait("sync", "p_scalar", slabctx["guards"][slabctx["i"] % 2])
        if j == 0:
            P.wait("sync", "p_tensor", guard_t)
        if mode == "cols":
            sl = src_ap[j * 512:(j + 1) * 512, I * 512:(I + 1) * 512]
            nsub = 4
        else:
            sl = src_ap[I * 512 + j * 128:I * 512 + (j + 1) * 128, :]
            nsub = 1
        ld = _load(P, sb[:].rearrange("p (c i) -> p c i", c=nsub),
                   sl.rearrange("(c p) i -> p c i", p=128))
        P.wait("scalar", "ld", ld)
        if j == 0:
            P.wait("scalar", "p_tensor", guard_t)
        tgt = P.add("scalar", lambda e, j=j, sb=sb, wdst=wdst: e.copy(
            wdst[:, j * 2048:(j + 1) * 2048], sb[:]), inc=("p_scalar", 1))
        slabctx["guards"][slabctx["i"] % 2] = tgt
        slabctx["i"] += 1
    return tgt


def emit_ffn_pass(P, B, half, wg_ap, wu_ap, wd_ap, nchunks, first_pass,
                  gs=None, us=None, ds=None, comb=None, sc_base=0):
    nc = P.nc
    xln, acc = B["xln"], B["acc"]
    stages_b, wbufs = B["stages"], B["wbufs"]
    hT, hsil = B["hT"], B["hsil"]
    pspool = B["pspool"]
    bank = lambda i: pspool[:, i * 512:(i + 1) * 512]
    psGsets = [[bank(i) for i in range(4)], [bank(4 + i) for i in range(4)]]
    psDsets = [[bank(4 + i) for i in range(4)], [bank(i) for i in range(4)]]
    tok0 = half * 1024
    units = [(isub, tch) for isub in range(4) for tch in range(2)]

    stages = []
    for I in range(nchunks):
        stages.append(("g", I, wg_ap, "cols"))
        stages.append(("u", I, wu_ap, "cols"))
        stages.append(("d", I, wd_ap, "rows"))
    slabctx = B.setdefault("slabctx", {"i": 0, "guards": [0, 0]})
    t_done = {-1: P.val("p_tensor"), -2: P.val("p_tensor")}
    cast_done = {}

    def emit_load_for(s):
        if s >= len(stages):
            return
        kind, I, ap, mode = stages[s]
        cast_done[s] = _ffn_emit_load(P, None, stages_b, slabctx,
                                      wbufs[s % 2], ap, mode, I,
                                      t_done[s - 2])

    emit_load_for(0)
    sil_g = [0, 0]
    h_g = [0, 0]
    acc_g = [0, 0]
    for s, (kind, I, ap, mode) in enumerate(stages):
        emit_load_for(s + 1)
        wbuf = wbufs[s % 2]
        P.wait("tensor", "p_scalar", cast_done[s])
        if kind == "g":
            for g in range(2):
                P.wait("tensor", "p_scalar", sil_g[g])
                P.wait("tensor", "p_vector", h_g[g])
                P.wait("tensor", "p_vector", acc_g[1 - g])
                def mmg(e, g=g, wbuf=wbuf):
                    psA4 = psGsets[g]
                    for i in range(4):
                        isub, tch = units[g * 4 + i]
                        for dc in range(NDC):
                            ins = e.matmul(psA4[i],
                                           wbuf[:, dc * 512 + isub * 128:dc * 512 + (isub + 1) * 128],
                                           xln[:, dc * T + tok0 + tch * 512:dc * T + tok0 + (tch + 1) * 512],
                                           start=(dc == 0), stop=(dc == NDC - 1))
                    return ins
                t_g = P.add("tensor", mmg, inc=("p_tensor", 1))
                P.wait("scalar", "p_tensor", t_g)
                def mksil(e, g=g, I=I):
                    psA4 = psGsets[g]
                    for i in range(4):
                        isub, tch = units[g * 4 + i]
                        hs = slice(isub * 1024 + tch * 512, isub * 1024 + (tch + 1) * 512)
                        if gs is not None:
                            gi = sc_base + I * 4 + isub
                            ins = e.activation(hsil[:, hs], psA4[i], AF.Silu,
                                               scale=gs[:, gi:gi + 1])
                        else:
                            ins = e.activation(hsil[:, hs], psA4[i], AF.Silu)
                    return ins
                sil_g[g] = P.add("scalar", mksil, inc=("p_scalar", 1))
            t_done[s] = P.val("p_tensor")
        elif kind == "u":
            P.wait("tensor", "p_scalar", max(sil_g))
            for g in range(2):
                P.wait("tensor", "p_vector", h_g[g])
                P.wait("tensor", "p_vector", acc_g[1 - g])
                def mmu(e, g=g, wbuf=wbuf):
                    psA4 = psGsets[g]
                    for i in range(4):
                        isub, tch = units[g * 4 + i]
                        for dc in range(NDC):
                            ins = e.matmul(psA4[i],
                                           wbuf[:, dc * 512 + isub * 128:dc * 512 + (isub + 1) * 128],
                                           xln[:, dc * T + tok0 + tch * 512:dc * T + tok0 + (tch + 1) * 512],
                                           start=(dc == 0), stop=(dc == NDC - 1))
                    return ins
                t_u = P.add("tensor", mmu, inc=("p_tensor", 1))
                P.wait("vector", "p_tensor", t_u)
                def mkh(e, g=g, I=I):
                    psA4 = psGsets[g]
                    ins = None
                    for i in range(4):
                        isub, tch = units[g * 4 + i]
                        hs = slice(isub * 1024 + tch * 512, isub * 1024 + (tch + 1) * 512)
                        if us is not None:
                            gi = sc_base + I * 4 + isub
                            ins = e.scalar_tensor_tensor(
                                hT[:, hs], psA4[i], us[:, gi:gi + 1],
                                hsil[:, hs], ALU.mult, ALU.mult)
                        else:
                            ins = e.tensor_mul(hT[:, hs], psA4[i], hsil[:, hs])
                        if comb is not None:
                            ins = e.tensor_mul(hT[:, hs], hT[:, hs],
                                               comb[:, tok0 + tch * 512:tok0 + (tch + 1) * 512])
                    return ins
                h_g[g] = P.add("vector", mkh, inc=("p_vector", 1))
            t_done[s] = P.val("p_tensor")
        else:
            P.wait("tensor", "p_vector", max(h_g))
            dunits = [(ot, tch) for ot in range(16) for tch in range(2)]
            for g in range(8):
                P.wait("tensor", "p_vector", acc_g[g % 2])
                def mmd(e, g=g, wbuf=wbuf):
                    psD4 = psDsets[g % 2]
                    for i in range(4):
                        ot, tch = dunits[g * 4 + i]
                        for isub in range(4):
                            ins = e.matmul(
                                psD4[i],
                                wbuf[:, isub * 2048 + ot * 128:isub * 2048 + (ot + 1) * 128],
                                hT[:, isub * 1024 + tch * 512:isub * 1024 + (tch + 1) * 512],
                                start=(isub == 0), stop=(isub == 3))
                    return ins
                t_d = P.add("tensor", mmd, inc=("p_tensor", 1))
                P.wait("vector", "p_tensor", t_d)
                def mkacc(e, g=g, I=I):
                    psD4 = psDsets[g % 2]
                    ins = None
                    for i in range(4):
                        ot, tch = dunits[g * 4 + i]
                        asl = slice(ot * 1024 + tch * 512, ot * 1024 + (tch + 1) * 512)
                        if first_pass and I == 0:
                            if ds is not None:
                                ins = e.tensor_single_scalar(
                                    acc[:, asl], psD4[i], ds[:, ot:ot + 1], ALU.mult)
                            else:
                                ins = e.tensor_copy(acc[:, asl], psD4[i])
                        else:
                            if ds is not None:
                                ins = e.scalar_tensor_tensor(
                                    acc[:, asl], psD4[i], ds[:, ot:ot + 1],
                                    acc[:, asl], ALU.mult, ALU.add)
                            else:
                                ins = e.tensor_add(acc[:, asl], acc[:, asl], psD4[i])
                    return ins
                acc_g[g % 2] = P.add("vector", mkacc, inc=("p_vector", 1))
            t_done[s] = P.val("p_tensor")
    P.wait("tensor", "p_vector", P.val("p_vector"))
    P.wait("tensor", "p_scalar", P.val("p_scalar"))


def emit_router(P, B, aps):
    nc = P.nc
    xln, stage = B["xln"], B["stage"]
    rwb, comb, crow, crow_bf = B["rwb"], B["comb"], B["crow"], B["crow_bf"]
    combN, t8 = B["combN"], B["t8"]
    ident_f, ones_bf, cpool = B["ident_f"], B["ones_bf"], B["cpool"]
    pspool = B["pspool"]
    psL = pspool[:, 0:512]
    psT = pspool[0:1, 512:640]
    psBC = pspool[:, 1024:1536]

    P.wait("sync", "p_vector", P.val("p_vector"))
    ld = _load(P, stage[:, 0:128].rearrange("p (c e) -> p c e", e=8),
               aps["rw_T"].rearrange("(c p) e -> p c e", p=128))
    P.wait("vector", "ld", ld)
    v_rw = P.add("vector", lambda e: e.tensor_copy(rwb[:], stage[:, 0:128]),
                 inc=("p_vector", 1))
    P.wait("tensor", "p_vector", v_rw)
    v_done = 0
    for tt in range(16):
        P.wait("tensor", "p_vector", v_done)
        def mml(e, tt=tt):
            for dc in range(NDC):
                ins = e.matmul(psL[:, 0:8],
                               xln[:, dc * T + tt * 128:dc * T + tt * 128 + 128],
                               rwb[:, dc * 8:(dc + 1) * 8],
                               start=(dc == 0), stop=(dc == NDC - 1))
            return ins
        t_l = P.add("tensor", mml, inc=("p_tensor", 1))
        P.wait("vector", "p_tensor", t_l)
        P.addw("vector", lambda e: e.reduce_max(t8[:, 0:1], psL[:, 0:8],
                                                axis=AX.X, negate=True))
        P.addw("vector", lambda e: e.tensor_scalar_mul(t8[:, 1:2], t8[:, 0:1], -1.0))
        P.addw("vector", lambda e: e.tensor_single_scalar(
            t8[:, 8:16], psL[:, 0:8], t8[:, 1:2], ALU.is_ge))
        P.addw("vector", lambda e: e.scalar_tensor_tensor(
            t8[:, 16:24], t8[:, 8:16], -30000.0, psL[:, 0:8], ALU.mult, ALU.add))
        v_m2 = P.addw("vector", lambda e: e.reduce_max(
            t8[:, 2:3], t8[:, 16:24], axis=AX.X, negate=True))
        P.wait("scalar", "p_vector", v_m2)
        s_e = P.add("scalar", lambda e: e.activation(
            t8[:, 3:4], t8[:, 2:3], AF.Exp, scale=cpool[:, 4:5], bias=t8[:, 0:1]),
            inc=("p_scalar", 1))
        P.wait("vector", "p_scalar", s_e)
        P.addw("vector", lambda e: e.tensor_scalar_add(t8[:, 4:5], t8[:, 3:4], 1.0))
        P.addw("vector", lambda e: e.reciprocal(t8[:, 5:6], t8[:, 4:5]))
        v_done = P.add("vector", lambda e, tt=tt: e.tensor_mul(
            combN[:, tt:tt + 1], t8[:, 4:5], t8[:, 5:6]), inc=("p_vector", 1))
    s_ev = 0
    for tt in range(16):
        P.wait("tensor", "p_vector", v_done)
        P.wait("tensor", "p_scalar", s_ev)
        t_tr = P.add("tensor", lambda e, tt=tt: e.transpose(
            psT[0:1, 0:128], combN[:, tt:tt + 1], ident_f[:]),
            inc=("p_tensor", 1))
        P.wait("scalar", "p_tensor", t_tr)
        s_ev = P.add("scalar", lambda e, tt=tt: e.copy(
            crow[0:1, tt * 128:(tt + 1) * 128], psT[0:1, 0:128]),
            inc=("p_scalar", 1))
    P.wait("vector", "p_scalar", s_ev)
    v_cb = P.add("vector", lambda e: e.tensor_copy(crow_bf[:], crow[:]),
                 inc=("p_vector", 1))
    P.wait("tensor", "p_vector", v_cb)
    s_ev = 0
    for tc in range(4):
        P.wait("tensor", "p_scalar", s_ev)
        t_bc = P.add("tensor", lambda e, tc=tc: e.matmul(
            psBC, ones_bf[0:1, :], crow_bf[0:1, tc * 512:(tc + 1) * 512],
            start=True, stop=True), inc=("p_tensor", 1))
        P.wait("scalar", "p_tensor", t_bc)
        s_ev = P.add("scalar", lambda e, tc=tc: e.copy(
            comb[:, tc * 512:(tc + 1) * 512], psBC), inc=("p_scalar", 1))


def emit_l0f(P, aps):
    nc = P.nc
    x_in, x_out = aps["x_in"], aps["x_out"]
    with Scope(nc) as S:
        cpool = S.sb("cpool", [128, 8], F32)
        ones_f = S.sb("onesf", [128, 128], F32)
        ones_bf = S.sb("onesbf", [128, 128], BF16)
        ident_f = S.sb("identf", [128, 128], F32)
        st_mean = S.sb("stmean", [128, 512], F32)
        st_msq = S.sb("stmsq", [128, 512], F32)
        st_var = S.sb("stvar", [128, 512], F32)
        big_bf = S.sb("bigbf", [128, NDC * T], BF16)
        acc = S.sb("acc", [128, NDC * 1024], F32)
        comb = S.sb("comb", [128, 2048], BF16)
        pspool = S.ps("pspool", [128, 4096], F32)
        stats = {"mean": st_mean, "msq": st_msq, "var": st_var, "rstd": st_var}
        xln = big_bf

        _load(P, cpool[:], aps["consts"])
        ldc = _load(P, ones_f[:], aps["ones128"])
        ldi = _load(P, ident_f[:], aps["identity"])
        P.wait("vector", "ld", ldi)
        P.add("vector", lambda e: e.tensor_copy(ones_bf[:], ones_f[:]),
              inc=("p_vector", 1))

        with Scope(nc) as S1:
            xf = S1.sb("xf", [128, NDC * 512], F32)
            xb = S1.sb("xb", [128, NDC * 512], BF16)
            xsq = S1.sb("xsq", [128, NDC * 512], BF16)
            emit_ln_in(P, x_in, xln, xf, xb, xsq, ones_bf, stats, cpool,
                       pspool[:, 0:512], pspool[:, 512:1024])
            P.flush()

        with Scope(nc) as S2:
            stage = S2.sb("stage", [128, 4096], F32)
            rwb = S2.sb("rwb", [128, 128], BF16)
            crow = S2.sb("crow", [1, 2048], F32)
            crow_bf = S2.sb("crowbf", [1, 2048], BF16)
            combN = S2.sb("combN", [128, 16], F32)
            t8 = S2.sb("t8", [128, 24], F32)
            emit_router(P, dict(xln=xln, stage=stage, rwb=rwb, comb=comb,
                                crow=crow, crow_bf=crow_bf, combN=combN, t8=t8,
                                ident_f=ident_f, ones_bf=ones_bf, cpool=cpool,
                                pspool=pspool), aps)
            if "dbg_comb" in aps:
                P.wait("gpsimd", "p_scalar", P.val("p_scalar"))
                _store(P, aps["dbg_comb"], comb[:])
                P.wait("sync", "gd", P.val("gd"))
            P.flush()

        for half in range(2):
            soft_barrier(P)
            with Scope(nc) as SA:
                sync_fence(P)
                stA = SA.sb("stA%d" % half, [128, 2048], F32)
                stB = SA.sb("stB%d" % half, [128, 2048], F32)
                wA = SA.sb("wA%d" % half, [128, 8192], BF16)
                wB = SA.sb("wB%d" % half, [128, 8192], BF16)
                hT = SA.sb("hT%d" % half, [128, 4096], BF16)
                hsil = SA.sb("hsil%d" % half, [128, 4096], BF16)
                Bf = dict(xln=xln, acc=acc, stages=[stA, stB], wbufs=[wA, wB],
                          hT=hT, hsil=hsil, pspool=pspool)
                emit_ffn_pass(P, Bf, half, aps["sh_gate_T"], aps["sh_up_T"],
                              aps["sh_down_T"], 16, first_pass=True)
                emit_ffn_pass(P, Bf, half, aps["r_gate_T"], aps["r_up_T"],
                              aps["r_down_T"], 2, first_pass=False, comb=comb)
                if half == 1 and "dbg_acc" in aps:
                    P.wait("gpsimd", "p_vector", P.val("p_vector"))
                    _store(P, aps["dbg_acc"], acc[:])
                    P.wait("sync", "gd", P.val("gd"))
                P.flush()
            soft_barrier(P)
            with Scope(nc) as SB:
                sync_fence(P)
                y_tc = SB.sb("y_tc%d" % half, [128, NDC * 512], BF16)
                xf2 = SB.sb("xf2_%d" % half, [128, 4096], F32)
                xsq_dc = SB.sb("xsqdc%d" % half, [128, 4096], BF16)
                acc3 = acc[:].rearrange("p (o t) -> p o t", t=1024)
                for tcl in range(2):
                    tc = half * 2 + tcl
                    s_cast = P.add("scalar", lambda e, tcl=tcl: e.copy(
                        _v3(y_tc), acc3[:, :, tcl * 512:(tcl + 1) * 512]),
                        inc=("p_scalar", 1))
                    P.wait("tensor", "p_scalar", s_cast)
                    P.wait("sync", "gd", P.val("gd"))
                    st, v_res = emit_out_ln_residual(
                        P, tc, x_in, x_out, y_tc, xf2, xsq_dc, ones_bf,
                        stats, cpool, pspool[:, 3072:3584], pspool[:, 3584:4096],
                        dbg_stats=(2, aps["dbg_mean"], aps["dbg_rstd"])
                        if "dbg_mean" in aps else None)
                    P.wait("scalar", "p_vector", v_res)
                P.flush()
    P.wait("gpsimd", "gd", P.val("gd"))
    P.flush()


L0F_KEYS = ("consts", "ones128", "identity", "rw_T", "sh_gate_T", "sh_up_T",
            "sh_down_T", "r_gate_T", "r_up_T", "r_down_T")


def build_l0f(H):
    nc = bass.Bass("TRN2", target_bir_lowering=False)
    aps = {}
    for name in L0F_KEYS:
        aps[name] = nc.dram_tensor(name, list(H[name].shape), F32,
                                   kind="ExternalInput").ap()
    aps["x_in"] = nc.dram_tensor("x_in", [D, T], F32, kind="ExternalInput").ap()
    aps["x_out"] = nc.dram_tensor("x_out", [D, T], F32, kind="ExternalOutput").ap()
    with nc.Block() as block:
        P = Prog(nc, block)
        emit_l0f(P, aps)
    return nc


def emit_l1f(P, aps):
    nc = P.nc
    x_in, x_out = aps["x_in"], aps["x_out"]
    with Scope(nc) as S:
        cpool = S.sb("cpool", [128, 8], F32)
        ones_f = S.sb("onesf", [128, 128], F32)
        ones_bf = S.sb("onesbf", [128, 128], BF16)
        st_mean = S.sb("stmean", [128, 512], F32)
        st_msq = S.sb("stmsq", [128, 512], F32)
        st_var = S.sb("stvar", [128, 512], F32)
        big_bf = S.sb("bigbf", [128, NDC * T], BF16)
        acc = S.sb("acc", [128, NDC * 1024], F32)
        gsT = S.sb("gsT", [128, 44], F32)
        usT = S.sb("usT", [128, 44], F32)
        dsT = S.sb("dsT", [128, 16], F32)
        pspool = S.ps("pspool", [128, 4096], F32)
        stats = {"mean": st_mean, "msq": st_msq, "var": st_var, "rstd": st_var}
        xln = big_bf

        _load(P, cpool[:], aps["consts"])
        _load(P, ones_f[:], aps["ones128"])
        _load(P, gsT[:], aps["gate_s"])
        _load(P, usT[:], aps["up_s"])
        ldc = _load(P, dsT[:], aps["down_s"])
        P.wait("vector", "ld", ldc)
        P.add("vector", lambda e: e.tensor_copy(ones_bf[:], ones_f[:]),
              inc=("p_vector", 1))

        with Scope(nc) as S1:
            xf = S1.sb("xf", [128, NDC * 512], F32)
            xb = S1.sb("xb", [128, NDC * 512], BF16)
            xsq = S1.sb("xsq", [128, NDC * 512], BF16)
            emit_ln_in(P, x_in, xln, xf, xb, xsq, ones_bf, stats, cpool,
                       pspool[:, 0:512], pspool[:, 512:1024])
            P.flush()

        for half in range(2):
            soft_barrier(P)
            with Scope(nc) as SA:
                sync_fence(P)
                stA = SA.sb("stA%d" % half, [128, 2048], F32)
                stB = SA.sb("stB%d" % half, [128, 2048], F32)
                wA = SA.sb("wA%d" % half, [128, 8192], BF16)
                wB = SA.sb("wB%d" % half, [128, 8192], BF16)
                hT = SA.sb("hT%d" % half, [128, 4096], BF16)
                hsil = SA.sb("hsil%d" % half, [128, 4096], BF16)
                Bf = dict(xln=xln, acc=acc, stages=[stA, stB], wbufs=[wA, wB],
                          hT=hT, hsil=hsil, pspool=pspool)
                emit_ffn_pass(P, Bf, half, aps["f_gate_T"], aps["f_up_T"],
                              aps["f_down_T"], 11, first_pass=True,
                              gs=gsT, us=usT, ds=dsT)
                P.flush()
            soft_barrier(P)
            with Scope(nc) as SB:
                sync_fence(P)
                y_tc = SB.sb("y_tc%d" % half, [128, NDC * 512], BF16)
                xf2 = SB.sb("xf2_%d" % half, [128, 4096], F32)
                xsq_dc = SB.sb("xsqdc%d" % half, [128, 4096], BF16)
                acc3 = acc[:].rearrange("p (o t) -> p o t", t=1024)
                for tcl in range(2):
                    tc = half * 2 + tcl
                    s_cast = P.add("scalar", lambda e, tcl=tcl: e.copy(
                        _v3(y_tc), acc3[:, :, tcl * 512:(tcl + 1) * 512]),
                        inc=("p_scalar", 1))
                    P.wait("tensor", "p_scalar", s_cast)
                    st, v_res = emit_out_ln_residual(
                        P, tc, x_in, x_out, y_tc, xf2, xsq_dc, ones_bf,
                        stats, cpool, pspool[:, 3072:3584], pspool[:, 3584:4096])
                    P.wait("scalar", "p_vector", v_res)
                P.flush()
    P.wait("gpsimd", "gd", P.val("gd"))
    P.flush()


L1F_KEYS = ("consts", "ones128", "f_gate_T", "f_up_T", "f_down_T",
            "gate_s", "up_s", "down_s")


def build_l1f(H):
    nc = bass.Bass("TRN2", target_bir_lowering=False)
    aps = {}
    for name in L1F_KEYS:
        aps[name] = nc.dram_tensor(name, list(H[name].shape), F32,
                                   kind="ExternalInput").ap()
    aps["x_in"] = nc.dram_tensor("x_in", [D, T], F32, kind="ExternalInput").ap()
    aps["x_out"] = nc.dram_tensor("x_out", [D, T], F32, kind="ExternalOutput").ap()
    with nc.Block() as block:
        P = Prog(nc, block)
        emit_l1f(P, aps)
    return nc



def _l1a_projections(P, aps, B):
    nc = P.nc
    xln, tmpT, BC, wdt, seln_bf, stage, pspool = (
        B["xln"], B["tmpT"], B["BC"], B["wdt"], B["seln_bf"], B["stage"], B["pspool"])
    wbc, wdd = B["wbc"], B["wdd"]
    sync_fence(P)
    ld = _load(P, stage[:, 0:512].rearrange("p (c i) -> p c i", i=32),
               aps["WBC"].rearrange("(c p) i -> p c i", p=128))
    P.wait("vector", "ld", ld)
    P.add("vector", lambda e: e.tensor_copy(wbc[:], stage[:, 0:512]),
          inc=("p_vector", 1))
    ld = _load(P, stage[:, 1024:3072].rearrange("p (c i) -> p c i", i=128),
               aps["dt_down"].rearrange("(c p) i -> p c i", p=128))
    P.wait("vector", "ld", ld)
    P.add("vector", lambda e: e.tensor_copy(wdd[:], stage[:, 1024:3072]),
          inc=("p_vector", 1))
    P.wait("sync", "p_vector", P.val("p_vector"))
    ld = _load(P, stage[:, 0:1024], aps["dtw_T"][:, 0:1024])
    ld = _load(P, stage[:, 1024:2048], aps["dtw_T"][:, 1024:2048])
    P.wait("vector", "ld", ld)
    P.add("vector", lambda e: e.tensor_copy(wdt[:], stage[:, 0:2048]),
          inc=("p_vector", 1))
    P.wait("sync", "p_vector", P.val("p_vector"))
    ld = _load(P, stage[0:32, 0:4096], aps["seln"])
    P.wait("vector", "ld", ld)
    v_all = P.add("vector", lambda e: e.tensor_copy(seln_bf[:], stage[0:32, 0:4096]),
                  inc=("p_vector", 1))
    P.wait("tensor", "p_vector", v_all)
    ev = 0
    for tc in range(NTC):
        P.wait("tensor", "p_scalar", ev)
        def mmbc(e, tc=tc):
            for dc in range(NDC):
                ins = e.matmul(pspool[0:32, 0:512],
                               wbc[:, dc * 32:(dc + 1) * 32],
                               xln[:, dc * T + tc * 512:dc * T + (tc + 1) * 512],
                               start=(dc == 0), stop=(dc == NDC - 1))
            return ins
        t_bc = P.add("tensor", mmbc, inc=("p_tensor", 1))
        P.wait("scalar", "p_tensor", t_bc)
        ev = P.add("scalar", lambda e, tc=tc: e.copy(
            BC[:, tc * 512:(tc + 1) * 512], pspool[0:32, 0:512]),
            inc=("p_scalar", 1))
        P.wait("tensor", "p_scalar", ev)
        def mmtmp(e, tc=tc):
            for dc in range(NDC):
                ins = e.matmul(pspool[:, 512:1024],
                               wdd[:, dc * 128:(dc + 1) * 128],
                               xln[:, dc * T + tc * 512:dc * T + (tc + 1) * 512],
                               start=(dc == 0), stop=(dc == NDC - 1))
            return ins
        t_tm = P.add("tensor", mmtmp, inc=("p_tensor", 1))
        P.wait("scalar", "p_tensor", t_tm)
        ev = P.add("scalar", lambda e, tc=tc: e.copy(
            tmpT[:, tc * 512:(tc + 1) * 512], pspool[:, 512:1024]),
            inc=("p_scalar", 1))
    P.wait("tensor", "p_scalar", ev)


def _l1a_scan_tc(P, B, tc):
    nc = P.nc
    xln, tmpT, BC, wdt, seln_bf = B["xln"], B["tmpT"], B["BC"], B["wdt"], B["seln_bf"]
    A_all, hstate, dtb, cpool = B["A_all"], B["hstate"], B["dtb"], B["cpool"]
    y_acc, pspool = B["y_acc"], B["pspool"]
    Bball, Cball = B["Bball"], B["Cball"]
    spa, spb, delta, du = B["spa"], B["spb"], B["delta"], B["du"]
    dAt, ut, ht, yt = B["dAt"], B["ut"], B["ht"], B["yt"]
    psB2 = pspool[:, 0:512]
    psC2 = pspool[:, 512:1024]
    psDt = pspool[:, 1024:1536]
    tsl = slice(tc * 512, (tc + 1) * 512)

    ev = P.val("p_scalar")
    for n in range(16):
        P.wait("tensor", "p_scalar", ev)
        def mmb(e, n=n):
            e.matmul(psB2, seln_bf[:, n * 128:(n + 1) * 128], BC[:, tsl],
                     start=True, stop=True)
            return e.matmul(psC2, seln_bf[:, (16 + n) * 128:(17 + n) * 128],
                            BC[:, tsl], start=True, stop=True)
        t_b = P.add("tensor", mmb, inc=("p_tensor", 1))
        P.wait("scalar", "p_tensor", t_b)
        def evb(e, n=n):
            e.copy(Bball[:, n * 512:(n + 1) * 512], psB2)
            return e.copy(Cball[:, n * 512:(n + 1) * 512], psC2)
        ev = P.add("scalar", evb, inc=("p_scalar", 1))

    for dc in range(16):
        P.wait("tensor", "p_scalar", P.val("p_scalar"))
        P.wait("tensor", "p_vector", P.val("p_vector"))
        P.wait("tensor", "p_gpsimd", P.val("p_gpsimd"))
        t_dt = P.add("tensor", lambda e, dc=dc: e.matmul(
            psDt, wdt[:, dc * 128:(dc + 1) * 128], tmpT[:, tsl],
            start=True, stop=True), inc=("p_tensor", 1))
        P.wait("scalar", "p_tensor", t_dt)
        P.addw("scalar", lambda e, dc=dc: e.activation(
            spa[:], psDt, AF.Abs, bias=dtb[:, dc:dc + 1]))
        P.addw("scalar", lambda e: e.activation(spa[:], spa[:], AF.Exp,
                                                scale=cpool[:, 4:5]))
        P.addw("scalar", lambda e: e.activation(spa[:], spa[:], AF.Ln,
                                                bias=cpool[:, 2:3]))
        s_sp = P.add("scalar", lambda e, dc=dc: e.activation(
            spb[:], psDt, AF.Relu, bias=dtb[:, dc:dc + 1]), inc=("p_scalar", 1))
        P.wait("vector", "p_scalar", s_sp)
        P.addw("vector", lambda e: e.tensor_add(delta[:], spa[:], spb[:]))
        v_du = P.add("vector", lambda e, dc=dc: e.tensor_mul(
            du[:], delta[:], xln[:, dc * T + tc * 512:dc * T + (tc + 1) * 512]),
            inc=("p_vector", 1))
        P.wait("gpsimd", "p_vector", v_du)
        P.wait("scalar", "p_vector", v_du)

        scan_done = [0, 0]
        hcopy_done = [0, 0]
        yC_done = [0, 0]
        yadd_done = [0, 0]
        for ng in range(8):
            base = ng * 2
            pp = ng % 2
            sl = [2 * pp, 2 * pp + 1]
            P.wait("scalar", "p_vector", scan_done[pp])
            def mkdA(e, base=base, sl=sl, dc=dc):
                for i in range(2):
                    col = dc * 16 + base + i
                    ins = e.activation(dAt[:, sl[i] * 512:(sl[i] + 1) * 512],
                                       delta[:], AF.Exp,
                                       scale=A_all[:, col:col + 1])
                return ins
            s_dA = P.add("scalar", mkdA, inc=("p_scalar", 1))
            P.wait("gpsimd", "p_vector", scan_done[pp])
            def mku(e, base=base, sl=sl):
                for i in range(2):
                    n = base + i
                    ins = e.tensor_mul(ut[:, sl[i] * 512:(sl[i] + 1) * 512], du[:],
                                       Bball[:, n * 512:(n + 1) * 512])
                return ins
            g_u = P.add("gpsimd", mku, inc=("p_gpsimd", 1))
            P.wait("vector", "p_scalar", s_dA)
            P.wait("vector", "p_gpsimd", g_u)
            P.wait("vector", "p_scalar", hcopy_done[pp])
            def mkscan(e, base=base, sl=sl, dc=dc):
                for i in range(2):
                    col = dc * 16 + base + i
                    ins = e.tensor_tensor_scan(
                        ht[:, sl[i] * 512:(sl[i] + 1) * 512],
                        dAt[:, sl[i] * 512:(sl[i] + 1) * 512],
                        ut[:, sl[i] * 512:(sl[i] + 1) * 512],
                        hstate[:, col:col + 1], ALU.mult, ALU.add)
                return ins
            scan_done[pp] = P.add("vector", mkscan, inc=("p_vector", 1))
            P.wait("scalar", "p_vector", scan_done[pp])
            def mkhc(e, base=base, sl=sl, dc=dc):
                for i in range(2):
                    col = dc * 16 + base + i
                    ins = e.copy(hstate[:, col:col + 1],
                                 ht[:, sl[i] * 512 + 511:sl[i] * 512 + 512])
                return ins
            hcopy_done[pp] = P.add("scalar", mkhc, inc=("p_scalar", 1))
            P.wait("gpsimd", "p_vector", scan_done[pp])
            P.wait("gpsimd", "p_vector", yadd_done[pp])
            def mkyc(e, base=base, sl=sl):
                ins = None
                for i in range(2):
                    n = base + i
                    ysc = slice(sl[i] * 512, (sl[i] + 1) * 512)
                    ins = e.tensor_mul(yt[:, ysc],
                                       ht[:, sl[i] * 512:(sl[i] + 1) * 512],
                                       Cball[:, n * 512:(n + 1) * 512])
                return ins
            g_yc = P.add("gpsimd", mkyc, inc=("p_gpsimd", 1))
            P.wait("vector", "p_gpsimd", g_yc)
            def mky(e, base=base, sl=sl, dc=dc):
                ins = None
                ysl = slice(dc * 512, (dc + 1) * 512)
                for i in range(2):
                    n = base + i
                    ysc = slice(sl[i] * 512, (sl[i] + 1) * 512)
                    if n == 0:
                        ins = e.tensor_copy(y_acc[:, ysl], yt[:, ysc])
                    else:
                        ins = e.tensor_add(y_acc[:, ysl], y_acc[:, ysl],
                                           yt[:, ysc])
                return ins
            yC_done[pp] = P.add("vector", mky, inc=("p_vector", 1))
            yadd_done[pp] = yC_done[pp]


def _l1a_outproj(P, aps, B, tc, x_in, x_out):
    nc = P.nc
    xln, y_acc, pspool = B["xln"], B["y_acc"], B["pspool"]
    stages, wops, y_out = B["stages"], B["wops"], B["y_out"]
    opb, Dv, cpool, ones_bf, stats = B["opb"], B["Dv"], B["cpool"], B["ones_bf"], B["stats"]
    xf2, xsq_dc = B["xf2"], B["xsq_dc"]
    psO = [pspool[:, 2048:2560], pspool[:, 2560:3072]]
    ybf = B["ybf"]
    s_yb = P.add("vector", lambda e: e.tensor_copy(ybf[:], y_acc[:]),
                 inc=("p_vector", 1))
    P.wait("scalar", "p_vector", s_yb)

    slabctx = B.setdefault("oslab", {"i": 0, "guards": [0, 0]})
    mm_done = {-1: P.val("p_tensor"), -2: P.val("p_tensor")}
    cast_done = {}

    def emit_load(p):
        if p >= 8:
            return
        for j in range(4):
            sb = stages[slabctx["i"] % 2]
            P.wait("sync", "p_scalar", slabctx["guards"][slabctx["i"] % 2])
            if j == 0:
                P.wait("sync", "p_tensor", mm_done[p - 2])
            ld = _load(P, sb[:].rearrange("p (c i) -> p c i", i=256),
                       aps["op_T"][512 * j:512 * (j + 1), p * 256:(p + 1) * 256]
                       .rearrange("(c p2) i -> p2 c i", p2=128))
            P.wait("scalar", "ld", ld)
            if j == 0:
                P.wait("scalar", "p_tensor", mm_done[p - 2])
            tgt = P.add("scalar", lambda e, j=j, sb=sb, p=p: e.copy(
                wops[p % 2][:, j * 1024:(j + 1) * 1024], sb[:]),
                inc=("p_scalar", 1))
            slabctx["guards"][slabctx["i"] % 2] = tgt
            slabctx["i"] += 1
        cast_done[p] = tgt

    emit_load(0)
    relu_guard = 0
    ev_prev = [0, 0]
    P.wait("tensor", "p_vector", s_yb)
    for p in range(8):
        emit_load(p + 1)
        P.wait("tensor", "p_scalar", cast_done[p])
        wop = wops[p % 2]
        for ot in range(2):
            oc = 2 * p + ot
            pp = oc % 2
            P.wait("tensor", "p_vector", ev_prev[pp])
            def mko(e, ot=ot, pp=pp, wop=wop):
                for dc in range(NDC):
                    ins = e.matmul(
                        psO[pp],
                        wop[:, dc * 256 + ot * 128:dc * 256 + (ot + 1) * 128],
                        ybf[:, dc * 512:(dc + 1) * 512],
                        start=(dc == 0), stop=(dc == NDC - 1))
                return ins
            t_o = P.add("tensor", mko, inc=("p_tensor", 1))
            P.wait("scalar", "p_tensor", t_o)
            P.wait("scalar", "p_vector", relu_guard)
            s_r = P.add("scalar", lambda e, pp=pp, oc=oc: e.activation(
                B["relu_t"][:, pp * 512:(pp + 1) * 512], psO[pp], AF.Relu,
                bias=opb[:, oc:oc + 1]), inc=("p_scalar", 1))
            P.wait("vector", "p_scalar", s_r)
            ev_prev[pp] = P.add("vector", lambda e, oc=oc, pp=pp:
                                e.scalar_tensor_tensor(
                y_out[:, oc * 512:(oc + 1) * 512],
                xln[:, oc * T + tc * 512:oc * T + (tc + 1) * 512],
                Dv[:, oc:oc + 1], B["relu_t"][:, pp * 512:(pp + 1) * 512],
                ALU.mult, ALU.add), inc=("p_vector", 1))
        mm_done[p] = P.val("p_tensor")
    P.wait("scalar", "p_vector", P.val("p_vector"))
    st, v_y = emit_out_ln_residual(
        P, tc, x_in, x_out, y_out, xf2, xsq_dc, ones_bf,
        stats, cpool, pspool[:, 3072:3584], pspool[:, 3584:4096])
    return v_y


def emit_l1a(P, aps):
    nc = P.nc
    x_in, x_out = aps["x_in"], aps["x_out"]
    with Scope(nc) as S:
        cpool = S.sb("cpool", [128, 8], F32)
        ones_f = S.sb("onesf", [128, 128], F32)
        ones_bf = S.sb("onesbf", [128, 128], BF16)
        st_mean = S.sb("stmean", [128, 512], F32)
        st_msq = S.sb("stmsq", [128, 512], F32)
        st_var = S.sb("stvar", [128, 512], F32)
        big_bf = S.sb("bigbf", [128, NDC * T], BF16)
        y_acc = S.sb("y_acc", [128, 16 * 512], F32)
        tmpT = S.sb("tmpT", [128, T], BF16)
        BC = S.sb("BC", [32, T], BF16)
        wdt = S.sb("wdt", [128, T], BF16)
        seln_bf = S.sb("selnbf", [32, 4096], BF16)
        A_all = S.sb("A_all", [128, 256], F32)
        hstate = S.sb("hstate", [128, 256], F32)
        dtb = S.sb("dtb", [128, 16], F32)
        opb = S.sb("opb", [128, 16], F32)
        Dv = S.sb("Dv", [128, 16], F32)
        pspool = S.ps("pspool", [128, 4096], F32)
        stats = {"mean": st_mean, "msq": st_msq, "var": st_var, "rstd": st_var}
        xln = big_bf

        _load(P, cpool[:], aps["consts"])
        _load(P, ones_f[:], aps["ones128"])
        _load(P, dtb[:], aps["dt_bias"])
        _load(P, opb[:], aps["op_bias"])
        _load(P, Dv[:], aps["Dvec"])
        ldc = _load(P, A_all[:], aps["A_log"])
        P.wait("vector", "ld", ldc)
        P.add("vector", lambda e: e.tensor_copy(ones_bf[:], ones_f[:]),
              inc=("p_vector", 1))
        P.add("vector", lambda e: e.memset(hstate[:], 0.0), inc=("p_vector", 1))
        v_al = P.val("p_vector")
        P.wait("scalar", "p_vector", v_al)
        s_a = P.add("scalar", lambda e: e.activation(A_all[:], A_all[:], AF.Exp),
                    inc=("p_scalar", 1))
        P.wait("vector", "p_scalar", s_a)
        P.addw("vector", lambda e: e.tensor_scalar_mul(A_all[:], A_all[:], -1.0))

        with Scope(nc) as S1:
            xf = S1.sb("xf", [128, NDC * 512], F32)
            xb = S1.sb("xb", [128, NDC * 512], BF16)
            xsq = S1.sb("xsq", [128, NDC * 512], BF16)
            emit_ln_in(P, x_in, xln, xf, xb, xsq, ones_bf, stats, cpool,
                       pspool[:, 0:512], pspool[:, 512:1024])
            P.flush()

        with Scope(nc) as S2:
            stage = S2.sb("stageP", [128, 4096], F32)
            wbc = S2.sb("wbc", [128, 512], BF16)
            wdd = S2.sb("wdd", [128, 2048], BF16)
            _l1a_projections(P, aps, dict(
                xln=xln, tmpT=tmpT, BC=BC, wdt=wdt, seln_bf=seln_bf,
                stage=stage, wbc=wbc, wdd=wdd, pspool=pspool))
            P.flush()

        for tc in range(NTC):
            soft_barrier(P)
            with Scope(nc) as SS:
                Bball = SS.sb("Bball%d" % tc, [128, 16 * 512], BF16)
                Cball = SS.sb("Cball%d" % tc, [128, 16 * 512], BF16)
                spa = SS.sb("spa%d" % tc, [128, 512], F32)
                spb = SS.sb("spb%d" % tc, [128, 512], F32)
                delta = SS.sb("delta%d" % tc, [128, 512], F32)
                du = SS.sb("du%d" % tc, [128, 512], F32)
                dAt = SS.sb("dAt%d" % tc, [128, 2048], F32)
                ut = SS.sb("ut%d" % tc, [128, 2048], F32)
                ht = SS.sb("ht%d" % tc, [128, 2048], BF16)
                yt = SS.sb("yt%d" % tc, [128, 2048], BF16)
                _l1a_scan_tc(P, dict(
                    xln=xln, tmpT=tmpT, BC=BC, wdt=wdt, seln_bf=seln_bf,
                    A_all=A_all, hstate=hstate, dtb=dtb, cpool=cpool,
                    y_acc=y_acc, pspool=pspool, Bball=Bball, Cball=Cball,
                    spa=spa, spb=spb, delta=delta, du=du, dAt=dAt, ut=ut,
                    ht=ht, yt=yt), tc)
                P.flush()
            soft_barrier(P)
            with Scope(nc) as SO:
                sync_fence(P)
                stO = [SO.sb("stOA%d" % tc, [128, 1024], F32),
                       SO.sb("stOB%d" % tc, [128, 1024], F32)]
                wops = [SO.sb("wopA%d" % tc, [128, NDC * 256], BF16),
                        SO.sb("wopB%d" % tc, [128, NDC * 256], BF16)]
                ybf = SO.sb("ybf%d" % tc, [128, 16 * 512], BF16)
                y_out = SO.sb("yout%d" % tc, [128, NDC * 512], BF16)
                relu_t = SO.sb("relu%d" % tc, [128, 1024], BF16)
                xf2 = SO.sb("xf2_%d" % tc, [128, 4096], F32)
                xsq_dc = ybf[:, 0:4096]
                _l1a_outproj(P, aps, dict(
                    xln=xln, y_acc=y_acc, pspool=pspool, stages=stO,
                    wops=wops, ybf=ybf, y_out=y_out, relu_t=relu_t,
                    opb=opb, Dv=Dv, cpool=cpool, ones_bf=ones_bf, stats=stats,
                    xf2=xf2, xsq_dc=xsq_dc), tc, x_in, x_out)
                P.flush()
    P.wait("gpsimd", "gd", P.val("gd"))
    P.flush()


L1A_KEYS = ("consts", "ones128", "WBC", "dt_down", "dtw_T", "dt_bias",
            "A_log", "Dvec", "op_T", "op_bias", "seln")


def build_l1a(H):
    nc = bass.Bass("TRN2", target_bir_lowering=False)
    aps = {}
    for name in L1A_KEYS:
        aps[name] = nc.dram_tensor(name, list(H[name].shape), F32,
                                   kind="ExternalInput").ap()
    aps["x_in"] = nc.dram_tensor("x_in", [D, T], F32, kind="ExternalInput").ap()
    aps["x_out"] = nc.dram_tensor("x_out", [D, T], F32, kind="ExternalOutput").ap()
    with nc.Block() as block:
        P = Prog(nc, block)
        emit_l1a(P, aps)
    return nc


ALL_KEYS = tuple(dict.fromkeys(L0A_KEYS + L0F_KEYS + L1A_KEYS + L1F_KEYS))


def build_fused(H):
    nc = bass.Bass("TRN2", target_bir_lowering=False)
    aps = {}
    for name in ALL_KEYS:
        aps[name] = nc.dram_tensor(name, list(H[name].shape), F32,
                                   kind="ExternalInput").ap()
    x_in = nc.dram_tensor("x_in", [D, T], F32, kind="ExternalInput").ap()
    x_out = nc.dram_tensor("x_out", [D, T], F32, kind="ExternalOutput").ap()
    xs_a = nc.dram_tensor("xs_a", [D, T], F32, kind="Internal").ap()
    xs_b = nc.dram_tensor("xs_b", [D, T], F32, kind="Internal").ap()
    with nc.Block() as block:
        P = Prog(nc, block)
        for emit, xin, xout in ((emit_l0a, x_in, xs_a),
                                (emit_l0f, xs_a, xs_b),
                                (emit_l1a, xs_b, xs_a),
                                (emit_l1f, xs_a, x_out)):
            a = dict(aps)
            a["x_in"], a["x_out"] = xin, xout
            emit(P, a)
            soft_barrier(P)
            sync_fence(P)
        P.wait("gpsimd", "gd", P.val("gd"))
        P.flush()
    return nc


FUSED = True


def kernel(**inputs):
    if FUSED:
        return kernel_fused(**inputs)
    return kernel_unfused(**inputs)


def kernel_fused(**inputs):
    H = _prep_host(inputs)
    x = np.asarray(inputs["L0_attn_x"], np.float32)
    ncores = 8
    nc = build_fused(H)
    common = {k: H[k] for k in ALL_KEYS}
    in_maps = [dict(common, x_in=np.ascontiguousarray(x[b].T))
               for b in range(ncores)]
    res = run_bass_kernel_spmd(nc, in_maps, core_ids=list(range(ncores)))
    out = np.stack([np.asarray(r["x_out"]).T for r in res.results])
    return out.astype(np.float32)


# version 08
# speedup vs baseline: 1.6396x; 1.0746x over previous
import numpy as np
import concourse.bass as bass
import concourse.mybir as mybir
from concourse.bass_utils import run_bass_kernel_spmd

F32 = mybir.dt.float32
BF16 = mybir.dt.bfloat16
AF = mybir.ActivationFunctionType
ALU = mybir.AluOpType
AX = mybir.AxisListType

D = 2048
T = 2048
NDC = D // 128
NTC = T // 512
EPS = 1e-5


def _prep_host(inputs):
    H = {}
    f32 = np.float32

    H["identity"] = np.eye(128, dtype=f32)
    H["ones128"] = np.ones((128, 128), dtype=f32)
    half = 32
    inv_freq = 1.0 / (10000.0 ** (np.arange(half, dtype=np.float64) * 2.0 / 64.0))
    pos = np.arange(T, dtype=np.float64)
    ang = pos[None, :] * inv_freq[:, None]
    cos = np.cos(ang).astype(f32)
    sin = np.sin(ang).astype(f32)
    H["cos"] = cos
    H["sin"] = sin
    am = np.zeros((128, 4, 512), dtype=f32)
    for p in range(4):
        for b in range(4):
            blk = am[:, p, 128 * b:128 * (b + 1)]
            if b > p:
                blk[:] = -30000.0
            elif b == p:
                r = np.arange(128)
                blk[:] = np.where(r[None, :] <= r[:, None], 0.0, -30000.0)
    H["smask"] = am.reshape(128, 4 * 512)
    pm = np.zeros((128, 4, 512), dtype=f32)
    for p in range(4):
        for b in range(4):
            blk = pm[:, p, 128 * b:128 * (b + 1)]
            if b < p:
                blk[:] = 0.0
            elif b > p:
                blk[:] = 1.0
            else:
                r = np.arange(128)
                blk[:] = np.where(r[:, None] <= r[None, :], 1.0, 0.0)
    H["ptmask"] = pm.reshape(128, 4 * 512)
    cst = np.zeros((128, 8), dtype=f32)
    cst[:, 0] = EPS
    cst[:, 1] = 1.0 / D
    cst[:, 2] = 1.0
    cst[:, 3] = 1.0 / np.sqrt(128.0)
    cst[:, 4] = -1.0
    cst[:, 5] = -0.5
    H["consts"] = cst
    seln = np.zeros((32, 32 * 128), dtype=f32)
    for j in range(32):
        seln[j, j * 128:(j + 1) * 128] = 1.0
    H["seln"] = seln

    W_uq = np.asarray(inputs["L0_attn_W_uq"], f32).reshape(512, 16, 64)
    W_qr = np.asarray(inputs["L0_attn_W_qr"], f32).reshape(512, 16, 64)
    W_uk = np.asarray(inputs["L0_attn_W_uk"], f32).reshape(512, 16, 64)
    W_kr = np.asarray(inputs["L0_attn_W_kr"], f32).reshape(512, 16, 64)
    q_comb = np.concatenate([W_qr[:, :, :32], W_qr[:, :, 32:], W_uq], axis=2)
    k_comb = np.concatenate([W_kr[:, :, :32], W_kr[:, :, 32:], W_uk], axis=2)
    H["Wq_comb"] = np.ascontiguousarray(q_comb.reshape(512, 2048))
    H["Wk_comb"] = np.ascontiguousarray(k_comb.reshape(512, 2048))
    H["W_dkv"] = np.asarray(inputs["L0_attn_W_dkv"], f32)
    H["W_dq"] = np.asarray(inputs["L0_attn_W_dq"], f32)
    H["W_uv"] = np.asarray(inputs["L0_attn_W_uv"], f32)
    H["W_out"] = np.asarray(inputs["L0_attn_W_out"], f32)
    H["b_out"] = np.ascontiguousarray(
        np.asarray(inputs["L0_attn_b_out"], f32).reshape(16, 128).T)

    H["rw_T"] = np.ascontiguousarray(np.asarray(inputs["L0_ffn_router_weight"], f32).T)
    H["sh_gate_T"] = np.ascontiguousarray(np.asarray(inputs["L0_ffn_sh_gate"], f32).T)
    H["sh_up_T"] = np.ascontiguousarray(np.asarray(inputs["L0_ffn_sh_up"], f32).T)
    H["sh_down_T"] = np.ascontiguousarray(np.asarray(inputs["L0_ffn_sh_down"], f32).T)
    H["r_gate_T"] = np.ascontiguousarray(np.asarray(inputs["L0_ffn_r_gate"], f32).T)
    H["r_up_T"] = np.ascontiguousarray(np.asarray(inputs["L0_ffn_r_up"], f32).T)
    H["r_down_T"] = np.ascontiguousarray(np.asarray(inputs["L0_ffn_r_down"], f32).T)

    W = np.asarray(inputs["L1_attn_x_proj_weight"], f32).T
    H["dt_down"] = np.ascontiguousarray(W[:, :128])
    H["WBC"] = np.ascontiguousarray(W[:, 128:160])
    H["dtw_T"] = np.ascontiguousarray(np.asarray(inputs["L1_attn_dt_proj_weight"], f32).T)
    H["dt_bias"] = np.ascontiguousarray(
        np.asarray(inputs["L1_attn_dt_proj_bias"], f32).reshape(16, 128).T)
    H["A_log"] = np.ascontiguousarray(
        np.asarray(inputs["L1_attn_A_log"], f32).reshape(16, 128, 16).transpose(1, 0, 2).reshape(128, 16 * 16))
    H["Dvec"] = np.ascontiguousarray(
        np.asarray(inputs["L1_attn_D"], f32).reshape(16, 128).T)
    H["op_T"] = np.ascontiguousarray(np.asarray(inputs["L1_attn_out_proj_weight"], f32).T)
    H["op_bias"] = np.ascontiguousarray(
        np.asarray(inputs["L1_attn_out_proj_bias"], f32).reshape(16, 128).T)

    H["f_gate_T"] = np.ascontiguousarray(np.asarray(inputs["L1_ffn_gate_w"], f32).T)
    H["f_up_T"] = np.ascontiguousarray(np.asarray(inputs["L1_ffn_up_w"], f32).T)
    H["f_down_T"] = np.ascontiguousarray(np.asarray(inputs["L1_ffn_down_w"], f32).T)
    H["gate_s"] = np.ascontiguousarray(
        np.asarray(inputs["L1_ffn_gate_s"], f32).reshape(44, 128).T)
    H["up_s"] = np.ascontiguousarray(
        np.asarray(inputs["L1_ffn_up_s"], f32).reshape(44, 128).T)
    H["down_s"] = np.ascontiguousarray(
        np.asarray(inputs["L1_ffn_down_s"], f32).reshape(16, 128).T)
    return H



ENGINES = ("sync", "tensor", "scalar", "vector", "gpsimd")


class Prog:

    def __init__(self, nc, block):
        self.nc = nc
        self.block = block
        self.ops = {e: [] for e in ENGINES}
        self.semh = {}
        self.cnt = {}
        self._guards = []

    def sem(self, name):
        if name not in self.semh:
            cm = self.nc.semaphore(name)
            self.semh[name] = cm.__enter__()
            self._guards.append(cm)
            self.cnt[name] = 0
        return self.semh[name]

    def bump(self, name, amt):
        self.sem(name)
        self.cnt[name] += amt
        return self.cnt[name]

    def emit(self, e, fn):
        self.ops[e].append(fn)

    def add(self, e, builder, inc=None):
        if inc is not None:
            name, amt = inc
            tgt = self.bump(name, amt)
            h = self.semh[name]

            def f(eng, builder=builder, h=h, amt=amt):
                builder(eng).then_inc(h, amt)
            self.emit(e, f)
            return tgt
        self.emit(e, lambda eng, builder=builder: builder(eng))
        return None

    def addw(self, e, builder):
        tgt = self.add(e, builder, inc=("p_" + e, 1))
        self.wait(e, "p_" + e, tgt)
        return tgt

    def wait(self, e, name, tgt):
        h = self.sem(name)
        self.emit(e, lambda eng, h=h, tgt=tgt: eng.wait_ge(h, tgt))

    def val(self, name):
        return self.cnt.get(name, 0)

    def flush(self):
        for e in ENGINES:
            fns = self.ops[e]
            if not fns:
                continue
            self.ops[e] = []

            def chain(eng, fns=fns):
                for fn in fns:
                    fn(eng)
            getattr(self.block, e)(chain)


class Scope:

    _n = 0

    def __init__(self, nc):
        self.nc = nc
        self.cms = []
        Scope._n += 1
        self.uid = Scope._n

    def __enter__(self):
        return self

    def sb(self, name, shape, dtype):
        cm = self.nc.sbuf_tensor("%s_u%d" % (name, self.uid), shape, dtype)
        t = cm.__enter__()
        self.cms.append(cm)
        return t

    def ps(self, name, shape, dtype=F32):
        cm = self.nc.psum_tensor("%s_u%d" % (name, self.uid), shape, dtype)
        t = cm.__enter__()
        self.cms.append(cm)
        return t

    def __exit__(self, *a):
        for cm in reversed(self.cms):
            cm.__exit__(None, None, None)
        return False


def _load(P, dst, src, sem="ld"):
    tgt = P.bump(sem, 16)
    h = P.semh[sem]
    P.emit("sync", lambda eng, dst=dst, src=src, h=h: eng.dma_start(out=dst, in_=src).then_inc(h, 16))
    return tgt


def _store(P, dst, src, sem="gd"):
    tgt = P.bump(sem, 16)
    h = P.semh[sem]
    P.emit("gpsimd", lambda eng, dst=dst, src=src, h=h: eng.dma_start(out=dst, in_=src).then_inc(h, 16))
    return tgt


def chunk_dram(ap, tc, width=512):
    return ap[:, tc * width:(tc + 1) * width].rearrange("(c p) t -> p c t", p=128)


def _v3(tile, w=512):
    return tile[:].rearrange("p (c t) -> p c t", t=w)



SC = 1.0 / np.sqrt(128.0)


def emit_ln_in(P, x_in_ap, xln, xf, xb, xsq, ones_bf, stats, cpool, psA, psB):
    nc = P.nc
    mean, msq, var, rstd = stats["mean"], stats["msq"], stats["var"], stats["rstd"]
    v_norm_prev = P.val("p_vector")
    v_var_prev = 0
    g_prev = P.val("gd")
    for tc in range(NTC):
        P.wait("sync", "p_vector", v_norm_prev)
        ld = _load(P, _v3(xf), chunk_dram(x_in_ap, tc))
        P.wait("vector", "ld", ld)
        v_cast = P.add("vector", lambda e, xb=xb, xf=xf: e.tensor_copy(xb[:], xf[:]),
                       inc=("p_vector", 1))
        P.wait("scalar", "p_vector", v_cast)
        s_sq = P.add("scalar", lambda e, xsq=xsq, xb=xb: e.square(xsq[:], xb[:]),
                     inc=("p_scalar", 1))
        P.wait("tensor", "p_vector", v_var_prev)
        P.wait("tensor", "p_scalar", s_sq)
        def mk_stats(e, xb=xb, xsq=xsq, psA=psA, psB=psB, ones_bf=ones_bf):
            xb3 = _v3(xb)
            xq3 = _v3(xsq)
            for dc in range(NDC):
                e.matmul(psA[:], ones_bf[:], xb3[:, dc, :], start=(dc == 0), stop=(dc == NDC - 1))
            for dc in range(NDC):
                ins = e.matmul(psB[:], ones_bf[:], xq3[:, dc, :], start=(dc == 0), stop=(dc == NDC - 1))
            return ins
        t_stats = P.add("tensor", mk_stats, inc=("p_tensor", 1))
        P.wait("vector", "p_tensor", t_stats)
        P.wait("vector", "p_vector", v_norm_prev)
        P.addw("vector", lambda e: e.tensor_scalar_mul(mean[:], psA[:], 1.0 / D))
        P.addw("vector", lambda e: e.tensor_mul(msq[:], mean[:], mean[:]))
        v_var = P.add("vector",
                      lambda e: e.scalar_tensor_tensor(var[:], psB[:], 1.0 / D,
                                                       msq[:], ALU.mult, ALU.subtract),
                      inc=("p_vector", 1))
        v_var_prev = v_var
        P.wait("scalar", "p_vector", v_var)
        P.addw("scalar", lambda e: e.activation(msq[:], var[:], AF.Ln,
                                                bias=cpool[:, 0:1]))
        s_rstd = P.add("scalar",
                       lambda e: e.activation(rstd[:], msq[:], AF.Exp,
                                              scale=cpool[:, 5:6]),
                       inc=("p_scalar", 1))
        P.wait("vector", "p_scalar", s_rstd)
        xf3 = _v3(xf)
        xl3 = xln[:].rearrange("p (c t) -> p c t", t=T)
        for dc in range(NDC):
            P.addw("vector", lambda e, dc=dc: e.tensor_sub(msq[:], xf3[:, dc, :], mean[:]))
            last = P.add("vector",
                         lambda e, dc=dc, tc=tc: e.tensor_mul(
                             xl3[:, dc, tc * 512:(tc + 1) * 512], msq[:], rstd[:]),
                         inc=("p_vector", 1) if dc == NDC - 1 else None)
            if dc < NDC - 1:
                P.wait("vector", "p_vector", P.val("p_vector"))
        v_norm_prev = last
    return v_norm_prev


def emit_out_ln_residual(P, tc, x_in_ap, x_out_ap, y_tc, xf, xsq_dc, ones_bf,
                         stats, cpool, psA, psB, dbg_stats=None):
    mean, msq, var, rstd = stats["mean"], stats["msq"], stats["var"], stats["rstd"]
    y3 = _v3(y_tc)
    xq3 = xsq_dc[:].rearrange("p (c t) -> p c t", t=512)
    t_used = [0, 0]
    sq_done = [0, 0]
    for g in range(2):
        P.wait("scalar", "p_tensor", t_used[g - 1] if g else 0)
        def sq(e, g=g):
            return e.square(xsq_dc[:], y3[:, g * 8:(g + 1) * 8, :])
        sq_done[g] = P.add("scalar", sq, inc=("p_scalar", 1))
        P.wait("tensor", "p_scalar", sq_done[g])
        def mm(e, g=g):
            for j in range(8):
                dc = g * 8 + j
                e.matmul(psA[:], ones_bf[:], y3[:, dc, :],
                         start=(dc == 0), stop=(dc == NDC - 1))
            for j in range(8):
                dc = g * 8 + j
                ins = e.matmul(psB[:], ones_bf[:], xq3[:, j, :],
                               start=(dc == 0), stop=(dc == NDC - 1))
            return ins
        t_used[g] = P.add("tensor", mm, inc=("p_tensor", 1))
    t_stats = t_used[1]
    P.wait("vector", "p_tensor", t_stats)
    P.wait("vector", "p_vector", P.val("p_vector"))
    P.addw("vector", lambda e: e.tensor_scalar_mul(mean[:], psA[:], 1.0 / D))
    P.addw("vector", lambda e: e.tensor_mul(msq[:], mean[:], mean[:]))
    v_var = P.add("vector", lambda e: e.scalar_tensor_tensor(
        var[:], psB[:], 1.0 / D, msq[:], ALU.mult, ALU.subtract),
        inc=("p_vector", 1))
    P.wait("scalar", "p_vector", v_var)
    P.addw("scalar", lambda e: e.activation(msq[:], var[:], AF.Ln, bias=cpool[:, 0:1]))
    s_rstd = P.add("scalar", lambda e: e.activation(rstd[:], msq[:], AF.Exp,
                                                    scale=cpool[:, 5:6]),
                   inc=("p_scalar", 1))
    if dbg_stats is not None and tc == dbg_stats[0]:
        P.wait("gpsimd", "p_scalar", s_rstd)
        _store(P, dbg_stats[1], mean[:])
        _store(P, dbg_stats[2], rstd[:])
        P.wait("sync", "gd", P.val("gd"))
    sync_fence(P)
    xf3 = xf[:].rearrange("p (c t) -> p c t", t=512)
    st = 0
    for g in range(2):
        P.wait("sync", "gd", st if isinstance(st, int) else 0)
        if g == 1:
            P.wait("sync", "gd", st)
        ld = _load(P, xf3, chunk_dram(x_in_ap, tc)[:, g * 8:(g + 1) * 8, :])
        P.wait("vector", "ld", ld)
        if g == 0:
            P.wait("vector", "p_scalar", s_rstd)
        for j in range(8):
            dc = g * 8 + j
            P.addw("vector", lambda e, dc=dc: e.tensor_sub(msq[:], y3[:, dc, :], mean[:]))
            P.addw("vector", lambda e: e.tensor_mul(msq[:], msq[:], rstd[:]))
            P.add("vector", lambda e, j=j: e.tensor_add(
                xf3[:, j, :], xf3[:, j, :], msq[:]),
                inc=("p_vector", 1) if j == 7 else None)
        v_res = P.val("p_vector")
        P.wait("gpsimd", "p_vector", v_res)
        st = _store(P, chunk_dram(x_out_ap, tc)[:, g * 8:(g + 1) * 8, :], xf3)
    return st, v_res




def _l0a_phaseC(P, aps, xln, ckvT, cqT, stage, wbuf, pspool):
    sync_fence(P)
    psC = [pspool[:, 0:512], pspool[:, 512:1024]]
    ev_prev = [0, 0]
    for wkey, dstT in (("W_dkv", ckvT), ("W_dq", cqT)):
        P.wait("sync", "p_vector", P.val("p_vector"))
        ldw = _load(P, stage[:].rearrange("p (c i) -> p c i", i=512),
                    aps[wkey].rearrange("(c p) i -> p c i", p=128))
        P.wait("vector", "ld", ldw)
        v_w = P.add("vector", lambda e, wbuf=wbuf, stage=stage:
                    e.tensor_copy(wbuf[:], stage[:]), inc=("p_vector", 1))
        if wkey == "W_dq" and "dbg_wbuf" in aps:
            P.wait("gpsimd", "p_vector", v_w)
            _store(P, aps["dbg_wbuf"], wbuf[:])
            P.wait("tensor", "gd", P.val("gd"))
        P.wait("tensor", "p_vector", v_w)
        for ct in range(4):
            for tc in range(4):
                g = ct * 4 + tc
                pp = g % 2
                P.wait("tensor", "p_scalar", ev_prev[pp])
                def mk(e, pp=pp, ct=ct, tc=tc, wbuf=wbuf, xln=xln):
                    for dc in range(NDC):
                        ins = e.matmul(
                            psC[pp],
                            wbuf[:, dc * 512 + ct * 128: dc * 512 + (ct + 1) * 128],
                            xln[:, dc * T + tc * 512: dc * T + (tc + 1) * 512],
                            start=(dc == 0), stop=(dc == NDC - 1))
                    return ins
                t_g = P.add("tensor", mk, inc=("p_tensor", 1))
                P.wait("scalar", "p_tensor", t_g)
                def ev(e, pp=pp, ct=ct, tc=tc, dstT=dstT):
                    return e.copy(
                        dstT[:, ct * T + tc * 512: ct * T + (tc + 1) * 512],
                        psC[pp])
                ev_prev[pp] = P.add("scalar", ev, inc=("p_scalar", 1))
        P.wait("vector", "p_tensor", P.val("p_tensor"))
    P.wait("tensor", "p_scalar", P.val("p_scalar"))


def _l0a_loadD(P, aps, cosT, sinT, smask, ptmask, wq, wk, wuv, stage2):
    sync_fence(P)
    ld1 = _load(P, stage2[0:32, 0:2048], aps["cos"])
    ld1 = _load(P, stage2[0:32, 2048:4096], aps["sin"])
    P.wait("vector", "ld", ld1)
    P.add("vector", lambda e: e.tensor_copy(cosT[:], stage2[0:32, 0:2048]))
    v_t = P.add("vector", lambda e: e.tensor_copy(sinT[:], stage2[0:32, 2048:4096]),
                inc=("p_vector", 1))
    _load(P, smask[:], aps["smask"])
    P.wait("sync", "p_vector", v_t)
    ld2 = _load(P, stage2[:, 0:2048], aps["ptmask"])
    P.wait("vector", "ld", ld2)
    v_t = P.add("vector", lambda e: e.tensor_copy(ptmask[:], stage2[:, 0:2048]),
                inc=("p_vector", 1))
    for wkey, dst in (("Wq_comb", wq), ("Wk_comb", wk), ("W_uv", wuv)):
        for ch in range(2):
            P.wait("sync", "p_vector", P.val("p_vector"))
            ldw = _load(P, stage2[:].rearrange("p (c i) -> p c i", i=2048),
                        aps[wkey][256 * ch:256 * (ch + 1), :]
                        .rearrange("(c p) i -> p c i", p=128))
            P.wait("vector", "ld", ldw)
            P.add("vector", lambda e, dst=dst, ch=ch, stage2=stage2:
                  e.tensor_copy(dst[:, ch * 2 * T:(ch + 1) * 2 * T], stage2[:]),
                  inc=("p_vector", 1))


def _l0a_phaseE(P, aps, ctxT, stage3, woutp, y_tc, xf2, xsq_dc, ones_bf, bout,
                stats, cpool, pspool, x_in, x_out):
    sync_fence(P)
    psO = [pspool[:, 0:512], pspool[:, 512:1024]]
    ev_prev = [0, 0]
    v_res_prev = 0
    for tc in range(NTC):
        for p in range(4):
            for ch in range(2):
                P.wait("sync", "p_vector", P.val("p_vector"))
                ldw = _load(P, stage3[:].rearrange("p (c i) -> p c i", i=512),
                            aps["W_out"][1024 * ch:1024 * (ch + 1),
                                         p * 512:(p + 1) * 512]
                            .rearrange("(c p2) i -> p2 c i", p2=128))
                P.wait("vector", "ld", ldw)
                P.wait("vector", "p_tensor", P.val("p_tensor"))
                P.add("vector", lambda e, ch=ch, woutp=woutp, stage3=stage3:
                      e.tensor_copy(
                          woutp[:, ch * 8 * 512:(ch + 1) * 8 * 512], stage3[:]),
                      inc=("p_vector", 1))
            v_wp = P.val("p_vector")
            P.wait("tensor", "p_vector", v_wp)
            for ot in range(4):
                oc = 4 * p + ot
                pp = oc % 2
                P.wait("tensor", "p_scalar", ev_prev[pp])
                def mko(e, ot=ot, pp=pp, tc=tc, woutp=woutp, ctxT=ctxT):
                    for dc in range(NDC):
                        ins = e.matmul(
                            psO[pp],
                            woutp[:, dc * 512 + ot * 128:dc * 512 + (ot + 1) * 128],
                            ctxT[:, dc * T + tc * 512:dc * T + (tc + 1) * 512],
                            start=(dc == 0), stop=(dc == NDC - 1))
                    return ins
                t_o = P.add("tensor", mko, inc=("p_tensor", 1))
                P.wait("scalar", "p_tensor", t_o)
                P.wait("scalar", "p_vector", v_res_prev)
                def evo(e, pp=pp, oc=oc, y_tc=y_tc, bout=bout):
                    return e.activation(
                        y_tc[:, oc * 512:(oc + 1) * 512], psO[pp],
                        AF.Identity, bias=bout[:, oc:oc + 1])
                ev_prev[pp] = P.add("scalar", evo, inc=("p_scalar", 1))
        P.wait("sync", "gd", P.val("gd"))
        P.wait("tensor", "p_scalar", P.val("p_scalar"))
        st, v_res_prev = emit_out_ln_residual(
            P, tc, x_in, x_out, y_tc, xf2, xsq_dc, ones_bf,
            stats, cpool, pspool[:, 1024:1536], pspool[:, 1536:2048])


def emit_l0a(P, aps):
    nc = P.nc
    x_in, x_out = aps["x_in"], aps["x_out"]

    with Scope(nc) as S:
        cpool = S.sb("cpool", [128, 8], F32)
        ones_f = S.sb("onesf", [128, 128], F32)
        ones_bf = S.sb("onesbf", [128, 128], BF16)
        ident_f = S.sb("identf", [128, 128], F32)
        ident_bf = S.sb("identbf", [128, 128], BF16)
        ones2_bf = S.sb("ones2bf", [2, 128], BF16)
        bout = S.sb("bout", [128, 16], F32)
        st_mean = S.sb("stmean", [128, 512], F32)
        st_msq = S.sb("stmsq", [128, 512], F32)
        st_var = S.sb("stvar", [128, 512], F32)
        big_bf = S.sb("bigbf", [128, NDC * T], BF16)
        ckvT = S.sb("ckvT", [128, 4 * T], BF16)
        cqT = S.sb("cqT", [128, 4 * T], BF16)
        pspool = S.ps("pspool", [128, 4096], F32)

        stats = {"mean": st_mean, "msq": st_msq, "var": st_var, "rstd": st_var}
        xln = big_bf
        ctxT = big_bf

        _load(P, cpool[:], aps["consts"])
        _load(P, ones_f[:], aps["ones128"])
        _load(P, bout[:], aps["b_out"])
        ldi = _load(P, ident_f[:], aps["identity"])
        P.wait("vector", "ld", ldi)
        P.add("vector", lambda e: e.tensor_copy(ones_bf[:], ones_f[:]))
        P.add("vector", lambda e: e.tensor_copy(ones2_bf[:], ones_f[0:2, :]))
        P.add("vector", lambda e: e.tensor_copy(ident_bf[:], ident_f[:]),
              inc=("p_vector", 1))

        with Scope(nc) as S1:
            xf = S1.sb("xf", [128, NDC * 512], F32)
            xb = S1.sb("xb", [128, NDC * 512], BF16)
            xsq = S1.sb("xsq", [128, NDC * 512], BF16)
            emit_ln_in(P, x_in, xln, xf, xb, xsq, ones_bf, stats, cpool,
                       pspool[:, 0:512], pspool[:, 512:1024])
            P.flush()

        with Scope(nc) as S2:
            stage = S2.sb("stage", [128, 8192], F32)
            wbuf = S2.sb("wbuf", [128, NDC * 512], BF16)
            _l0a_phaseC(P, aps, xln, ckvT, cqT, stage, wbuf, pspool)
            if "dbg_ckvT" in aps:
                P.wait("gpsimd", "p_scalar", P.val("p_scalar"))
                _store(P, aps["dbg_ckvT"], ckvT[:])
                _store(P, aps["dbg_cqT"], cqT[:])
            P.flush()

        with Scope(nc) as S3:
            cosT = S3.sb("cosT", [32, T], BF16)
            sinT = S3.sb("sinT", [32, T], BF16)
            smask = S3.sb("sb_smask", [128, 2048], F32)
            ptmask = S3.sb("sb_ptmask", [128, 2048], BF16)
            wq = S3.sb("wq", [128, 4 * T], BF16)
            wk = S3.sb("wk", [128, 4 * T], BF16)
            wuv = S3.sb("wuv", [128, 4 * T], BF16)
            with Scope(nc) as S3a:
                stage2 = S3a.sb("stage2", [128, 4096], F32)
                _l0a_loadD(P, aps, cosT, sinT, smask, ptmask, wq, wk, wuv, stage2)
                P.flush()
            P.wait("tensor", "p_vector", P.val("p_vector"))

            with Scope(nc) as S3b:
                qh = S3b.sb("qh", [128, T], BF16)
                kh = S3b.sb("kh", [128, T], BF16)
                vh = S3b.sb("vh", [128, 16 * 128], BF16)
                ta = S3b.sb("ta", [32, 1024], F32)
                tb = S3b.sb("tb", [32, 1024], F32)
                junk = S3b.sb("junk", [128, 512], F32)
                racc = S3b.sb("racc", [128, 64], F32)
                nmx = S3b.sb("nmx", [128, 8], F32)
                nms_all = S3b.sb("nmsall", [128, 16], F32)
                scl16 = S3b.sb("scl16", [128, 48], F32)
                rowsrc = S3b.sb("rowsrc", [128, 32], BF16)
                Lrow2 = S3b.sb("Lrow2", [2, T], BF16)
                lrow_s = S3b.sb("lrows", [32, 128], BF16)
                ptb = [S3b.sb("ptb%d" % i, [128, 512], BF16) for i in range(4)]
                lrow_sc = nc.dram_tensor("lrow_sc_l0a", [32, 128], BF16,
                                         kind="Internal").ap()
                emit_l0a_heads(P, dict(
                    xln=xln, ckvT=ckvT, cqT=cqT, wq=wq, wk=wk, wuv=wuv,
                    cosT=cosT, sinT=sinT, smask=smask, ptmask=ptmask, ctxT=ctxT,
                    qh=qh, kh=kh, vh=vh, ta=ta, tb=tb, junk=junk, racc=racc,
                    nmx=nmx, nms_all=nms_all, scl16=scl16, rowsrc=rowsrc,
                    Lrow2=Lrow2, lrow_s=lrow_s, lrow_sc=lrow_sc, ptb=ptb,
                    psS=pspool[:, 0:2048],
                    psFull=pspool[:, 0:4096],
                    psCtx=pspool[:, 3072:3584],
                    psRow32=pspool[0:32, 3584:3648].bitcast(BF16),
                    ones2_bf=ones2_bf, ident_bf=ident_bf, cpool=cpool))
                P.flush()

        soft_barrier(P)
        with Scope(nc) as S4:
            stage3 = S4.sb("stage3", [128, 4096], F32)
            woutp = S4.sb("woutp", [128, NDC * 512], BF16)
            y_tc = S4.sb("y_tc", [128, NDC * 512], BF16)
            xf2 = S4.sb("xf2", [128, 4096], F32)
            xsq_dc = S4.sb("xsqdc", [128, 4096], BF16)
            _l0a_phaseE(P, aps, ctxT, stage3, woutp, y_tc, xf2, xsq_dc,
                        ones_bf, bout, stats, cpool, pspool, x_in, x_out)
            P.flush()
    P.wait("gpsimd", "gd", P.val("gd"))
    P.flush()



def sync_fence(P):
    for e in ("tensor", "scalar", "vector"):
        P.wait("sync", "p_" + e, P.val("p_" + e))
    P.wait("sync", "gd", P.val("gd"))


def soft_barrier(P, engines=("tensor", "scalar", "vector")):
    vals = {e: P.val("p_" + e) for e in engines}
    for e in engines:
        for o in engines:
            if o != e:
                P.wait(e, "p_" + o, vals[o])


def emit_l0a_heads(P, B):
    xln, ckvT, cqT = B["xln"], B["ckvT"], B["cqT"]
    wq, wk, wuv = B["wq"], B["wk"], B["wuv"]
    cosT, sinT, smask, ptmask = B["cosT"], B["sinT"], B["smask"], B["ptmask"]
    ctxT, qh, kh, vh = B["ctxT"], B["qh"], B["kh"], B["vh"]
    ta, tb, junk, racc, nmx = B["ta"], B["tb"], B["junk"], B["racc"], B["nmx"]
    nms_all, scl16, rowsrc = B["nms_all"], B["scl16"], B["rowsrc"]
    Lrow2, ptb = B["Lrow2"], B["ptb"]
    psQ = B["psS"]
    ps4 = [psQ[:, i * 512:(i + 1) * 512] for i in range(4)]
    psCtx, psRow32 = B["psCtx"], B["psRow32"]
    lrow_s, lrow_sc = B["lrow_s"], B["lrow_sc"]
    ones2_bf, ident_bf, cpool = B["ones2_bf"], B["ident_bf"], B["cpool"]
    SQ = float(np.sqrt(128.0))
    hi_view = rowsrc[:].rearrange("p (q two) -> p q two", two=2)[:, :, 0]
    lo_view = rowsrc[:].rearrange("p (q two) -> p q two", two=2)[:, :, 1]

    for h in range(16):
        soft_barrier(P)
        rope_v = 0
        base_s = 0
        for side in ("q", "k"):
            wqk = wq if side == "q" else wk
            srcT = cqT if side == "q" else ckvT
            dsth = qh if side == "q" else kh
            P.wait("tensor", "p_vector", rope_v)
            P.wait("tensor", "p_scalar", base_s)
            def mkp(e, wqk=wqk, srcT=srcT, h=h):
                for tc in range(4):
                    for ct in range(4):
                        ins = e.matmul(
                            ps4[tc],
                            wqk[:, ct * T + h * 128:ct * T + (h + 1) * 128],
                            srcT[:, ct * T + tc * 512:ct * T + (tc + 1) * 512],
                            start=(ct == 0), stop=(ct == 3))
                return ins
            t_p = P.add("tensor", mkp, inc=("p_tensor", 1))
            P.wait("vector", "p_tensor", t_p)
            for hh in range(2):
                hsl = slice(hh * 1024, (hh + 1) * 1024)
                def r1(e, hsl=hsl):
                    e.tensor_mul(ta[:], psQ[0:32, hsl], cosT[:, hsl])
                    return e.tensor_mul(tb[:], psQ[32:64, hsl], sinT[:, hsl])
                P.addw("vector", r1)
                P.add("vector", lambda e, dsth=dsth, hsl=hsl: e.tensor_sub(
                    dsth[0:32, hsl], ta[:], tb[:]))
                def r2(e, hsl=hsl):
                    e.tensor_mul(ta[:], psQ[32:64, hsl], cosT[:, hsl])
                    return e.tensor_mul(tb[:], psQ[0:32, hsl], sinT[:, hsl])
                P.addw("vector", r2)
                rope_v = P.add("vector", lambda e, dsth=dsth, hsl=hsl: e.tensor_add(
                    dsth[32:64, hsl], ta[:], tb[:]), inc=("p_vector", 1))
            P.wait("scalar", "p_tensor", t_p)
            base_s = P.add("scalar", lambda e, dsth=dsth: e.copy(
                dsth[64:128, :], psQ[64:128, :]), inc=("p_scalar", 1))
        ev_v = 0
        P.wait("tensor", "p_vector", rope_v)
        for gv in range(4):
            P.wait("tensor", "p_scalar", ev_v)
            def mkv(e, gv=gv, h=h):
                for i in range(4):
                    tt = gv * 4 + i
                    for ct in range(4):
                        ins = e.matmul(
                            ps4[i][:, 0:128],
                            ckvT[:, ct * T + tt * 128:ct * T + (tt + 1) * 128],
                            wuv[:, ct * T + h * 128:ct * T + (h + 1) * 128],
                            start=(ct == 0), stop=(ct == 3))
                return ins
            t_v = P.add("tensor", mkv, inc=("p_tensor", 1))
            P.wait("scalar", "p_tensor", t_v)
            def evv(e, gv=gv):
                for i in range(4):
                    tt = gv * 4 + i
                    ins = e.copy(vh[:, tt * 128:(tt + 1) * 128], ps4[i][:, 0:128])
                return ins
            ev_v = P.add("scalar", evv, inc=("p_scalar", 1))

        v_ms = P.add("vector", lambda e: e.memset(racc[:], 0.0),
                     inc=("p_vector", 1))
        s_exp_par = [ev_v, ev_v]
        psQfull = B["psFull"]
        for qi in range(16):
            nch = qi // 4 + 1
            par = qi % 2
            psS = psQfull[:, par * 2048:par * 2048 + 2048]
            P.wait("tensor", "p_scalar", s_exp_par[par])
            def mks(e, qi=qi, nch=nch, psS=psS):
                for kc in range(nch):
                    ins = e.matmul(
                        psS[:, kc * 512:(kc + 1) * 512],
                        qh[:, qi * 128:(qi + 1) * 128],
                        kh[:, kc * 512:(kc + 1) * 512],
                        start=True, stop=True)
                return ins
            t_s = P.add("tensor", mks, inc=("p_tensor", 1))
            P.wait("vector", "p_tensor", t_s)
            if qi == 0:
                P.wait("vector", "p_vector", v_ms)
            dg = (nch - 1) * 512
            P.addw("vector", lambda e, dg=dg, qi=qi, psS=psS: e.tensor_add(
                psS[:, dg:dg + 512], psS[:, dg:dg + 512],
                smask[:, (qi % 4) * 512:(qi % 4 + 1) * 512]))
            nxq = nmx[:, par * 4:par * 4 + 4]
            def mkmaxes(e, nch=nch, psS=psS, nxq=nxq):
                for kc in range(nch):
                    ins = e.reduce_max(nxq[:, kc:kc + 1],
                                       psS[:, kc * 512:(kc + 1) * 512],
                                       axis=AX.X, negate=True)
                return ins
            P.addw("vector", mkmaxes)
            P.addw("vector", lambda e, nch=nch, nxq=nxq, par=par: e.tensor_reduce(
                scl16[:, 46 + par:47 + par], nxq[:, 0:nch], axis=AX.X, op=ALU.min))
            v_m = P.add("vector", lambda e, qi=qi, par=par: e.tensor_scalar_mul(
                nms_all[:, qi:qi + 1], scl16[:, 46 + par:47 + par], SC),
                inc=("p_vector", 1))
            P.wait("scalar", "p_vector", v_m)
            jk = junk[:, par * 512:(par + 1) * 512] if False else junk[:]
            def mkexp(e, qi=qi, nch=nch, psS=psS):
                for kc in range(nch):
                    ins = e.activation(junk[:], psS[:, kc * 512:(kc + 1) * 512],
                                       AF.Exp, bias=nms_all[:, qi:qi + 1],
                                       scale=cpool[:, 3:4],
                                       accum_out=racc[:, 4 * qi + kc:4 * qi + kc + 1])
                return ins
            s_exp_par[par] = P.add("scalar", mkexp, inc=("p_scalar", 1))
        s_exp = max(s_exp_par)

        P.wait("vector", "p_scalar", s_exp)
        r3 = racc[:].rearrange("p (q k) -> p q k", k=4)
        def mksum(e):
            for qi in range(16):
                ins = e.tensor_reduce(scl16[:, qi:qi + 1], r3[:, qi, :],
                                      axis=AX.X, op=ALU.add)
            return ins
        v_rs = P.add("vector", mksum, inc=("p_vector", 1))
        P.wait("scalar", "p_vector", v_rs)
        s_ln = P.add("scalar", lambda e: e.activation(
            scl16[:, 16:32], scl16[:, 0:16], AF.Ln), inc=("p_scalar", 1))
        P.wait("vector", "p_scalar", s_ln)
        P.addw("vector", lambda e: e.tensor_sub(scl16[:, 32:48], nms_all[:],
                                                scl16[:, 16:32]))
        P.addw("vector", lambda e: e.tensor_scalar_mul(scl16[:, 32:48],
                                                       scl16[:, 32:48], SQ))
        P.addw("vector", lambda e: e.tensor_copy(hi_view, scl16[:, 32:48]))
        P.addw("vector", lambda e: e.tensor_copy(scl16[:, 0:16], hi_view))
        P.addw("vector", lambda e: e.tensor_sub(scl16[:, 16:32], scl16[:, 32:48],
                                                scl16[:, 0:16]))
        v_row = P.add("vector", lambda e: e.tensor_copy(lo_view, scl16[:, 16:32]),
                      inc=("p_vector", 1))
        P.wait("tensor", "p_vector", v_row)
        t_tr = P.add("tensor", lambda e: e.transpose(psRow32, rowsrc[:, 0:32],
                                                     ident_bf[:]),
                     inc=("p_tensor", 1))
        P.wait("scalar", "p_tensor", t_tr)
        s_lr = P.add("scalar", lambda e: e.copy(lrow_s[:], psRow32),
                     inc=("p_scalar", 1))
        P.wait("gpsimd", "p_scalar", s_lr)
        P.wait("gpsimd", "ld", P.val("ld"))
        st_lr = _store(P, lrow_sc, lrow_s[:])
        P.wait("sync", "gd", st_lr)
        ld_lr = _load(P, Lrow2[:].rearrange("two (q j) -> two q j", j=128),
                      lrow_sc.rearrange("(q two) j -> two q j", two=2))
        P.wait("tensor", "ld", ld_lr)

        exp_g = s_exp
        mask_g = 0
        ctx_mark = 0
        s_ctx_prev = 0
        for tqc in range(4):
            ntt = 4 * tqc + 4
            groups = [list(range(g, min(g + 4, ntt))) for g in range(0, ntt, 4)]
            for gi, tiles in enumerate(groups):
                last_grp = (gi == len(groups) - 1)
                P.wait("tensor", "p_scalar", exp_g)
                def mkpt(e, tiles=tiles, tqc=tqc):
                    for i, tt in enumerate(tiles):
                        e.matmul(ps4[i],
                                 kh[:, tt * 128:(tt + 1) * 128],
                                 qh[:, tqc * 512:(tqc + 1) * 512],
                                 start=True, stop=False)
                        ins = e.matmul(ps4[i], ones2_bf[:],
                                       Lrow2[0:2, tqc * 512:(tqc + 1) * 512],
                                       start=False, stop=True)
                    return ins
                t_pt = P.add("tensor", mkpt, inc=("p_tensor", 1))
                P.wait("scalar", "p_tensor", t_pt)
                P.wait("scalar", "p_tensor", ctx_mark)
                def mkexp2(e, tiles=tiles):
                    for i in range(len(tiles)):
                        ins = e.activation(ptb[i][:], ps4[i], AF.Exp,
                                           scale=cpool[:, 3:4])
                    return ins
                exp_g = P.add("scalar", mkexp2, inc=("p_scalar", 1))
                if last_grp:
                    P.wait("vector", "p_scalar", exp_g)
                    def mkm2(e, tiles=tiles, tqc=tqc):
                        for i, tt in enumerate(tiles):
                            pat = tt - 4 * tqc
                            ins = e.tensor_mul(ptb[i][:], ptb[i][:],
                                               ptmask[:, pat * 512:(pat + 1) * 512])
                        return ins
                    mask_g = P.add("vector", mkm2, inc=("p_vector", 1))
                    P.wait("tensor", "p_vector", mask_g)
                else:
                    P.wait("tensor", "p_scalar", exp_g)
                if gi == 0:
                    P.wait("tensor", "p_scalar", s_ctx_prev)
                def mkctx(e, tiles=tiles, ntt=ntt):
                    for i, tt in enumerate(tiles):
                        ins = e.matmul(psCtx, vh[:, tt * 128:(tt + 1) * 128],
                                       ptb[i][:], start=(tt == 0),
                                       stop=(tt == ntt - 1))
                    return ins
                ctx_mark = P.add("tensor", mkctx, inc=("p_tensor", 1))
            P.wait("scalar", "p_tensor", ctx_mark)
            def evctx(e, h=h, tqc=tqc):
                return e.copy(ctxT[:, h * T + tqc * 512:h * T + (tqc + 1) * 512],
                              psCtx)
            s_ctx_prev = P.add("scalar", evctx, inc=("p_scalar", 1))





L0A_KEYS = ("consts", "ones128", "identity", "cos", "sin", "smask", "ptmask",
            "b_out", "W_dkv", "W_dq", "Wq_comb", "Wk_comb", "W_uv", "W_out")


def build_l0a(H):
    nc = bass.Bass("TRN2", target_bir_lowering=False)
    aps = {}
    for name in L0A_KEYS:
        aps[name] = nc.dram_tensor(name, list(H[name].shape), F32,
                                   kind="ExternalInput").ap()
    aps["x_in"] = nc.dram_tensor("x_in", [D, T], F32, kind="ExternalInput").ap()
    aps["x_out"] = nc.dram_tensor("x_out", [D, T], F32, kind="ExternalOutput").ap()
    with nc.Block() as block:
        P = Prog(nc, block)
        emit_l0a(P, aps)
    return nc


def kernel_unfused(**inputs):
    H = _prep_host(inputs)
    x = np.asarray(inputs["L0_attn_x"], np.float32)
    ncores = 8
    cores = list(range(ncores))
    xT = [np.ascontiguousarray(x[b].T) for b in range(ncores)]

    for build, keys in ((build_l0a, L0A_KEYS), (build_l0f, L0F_KEYS),
                        (build_l1a, L1A_KEYS), (build_l1f, L1F_KEYS)):
        nc = build(H)
        common = {k: H[k] for k in keys}
        in_maps = [dict(common, x_in=xT[b]) for b in range(ncores)]
        res = run_bass_kernel_spmd(nc, in_maps, core_ids=cores)
        xT = [np.asarray(r["x_out"]) for r in res.results]

    out = np.stack([t.T for t in xT]).astype(np.float32)
    return out



def _ffn_emit_load(P, st, stage_bufs, slabctx, wdst, src_ap, mode, I, guard_t):
    for j in range(4):
        sb = stage_bufs[slabctx["i"] % 2]
        P.wait("sync", "p_scalar", slabctx["guards"][slabctx["i"] % 2])
        if j == 0:
            P.wait("sync", "p_tensor", guard_t)
        if mode == "cols":
            sl = src_ap[j * 512:(j + 1) * 512, I * 512:(I + 1) * 512]
            nsub = 4
        else:
            sl = src_ap[I * 512 + j * 128:I * 512 + (j + 1) * 128, :]
            nsub = 1
        ld = _load(P, sb[:].rearrange("p (c i) -> p c i", c=nsub),
                   sl.rearrange("(c p) i -> p c i", p=128))
        P.wait("scalar", "ld", ld)
        if j == 0:
            P.wait("scalar", "p_tensor", guard_t)
        tgt = P.add("scalar", lambda e, j=j, sb=sb, wdst=wdst: e.copy(
            wdst[:, j * 2048:(j + 1) * 2048], sb[:]), inc=("p_scalar", 1))
        slabctx["guards"][slabctx["i"] % 2] = tgt
        slabctx["i"] += 1
    return tgt


def emit_ffn_pass(P, B, half, wg_ap, wu_ap, wd_ap, nchunks, first_pass,
                  gs=None, us=None, ds=None, comb=None, sc_base=0):
    nc = P.nc
    xln, acc = B["xln"], B["acc"]
    stages_b, wbufs = B["stages"], B["wbufs"]
    hT, hsil = B["hT"], B["hsil"]
    pspool = B["pspool"]
    bank = lambda i: pspool[:, i * 512:(i + 1) * 512]
    psGsets = [[bank(i) for i in range(4)], [bank(4 + i) for i in range(4)]]
    psDsets = [[bank(4 + i) for i in range(4)], [bank(i) for i in range(4)]]
    tok0 = half * 1024
    units = [(isub, tch) for isub in range(4) for tch in range(2)]

    stages = []
    for I in range(nchunks):
        stages.append(("g", I, wg_ap, "cols"))
        stages.append(("u", I, wu_ap, "cols"))
        stages.append(("d", I, wd_ap, "rows"))
    slabctx = B.setdefault("slabctx", {"i": 0, "guards": [0, 0]})
    t_done = {-1: P.val("p_tensor"), -2: P.val("p_tensor")}
    cast_done = {}

    def emit_load_for(s):
        if s >= len(stages):
            return
        kind, I, ap, mode = stages[s]
        cast_done[s] = _ffn_emit_load(P, None, stages_b, slabctx,
                                      wbufs[s % 2], ap, mode, I,
                                      t_done[s - 2])

    emit_load_for(0)
    sil_g = [0, 0]
    h_g = [0, 0]
    acc_g = [0, 0]
    for s, (kind, I, ap, mode) in enumerate(stages):
        emit_load_for(s + 1)
        wbuf = wbufs[s % 2]
        P.wait("tensor", "p_scalar", cast_done[s])
        if kind == "g":
            for g in range(2):
                P.wait("tensor", "p_scalar", sil_g[g])
                P.wait("tensor", "p_vector", h_g[g])
                P.wait("tensor", "p_vector", acc_g[1 - g])
                def mmg(e, g=g, wbuf=wbuf):
                    psA4 = psGsets[g]
                    for i in range(4):
                        isub, tch = units[g * 4 + i]
                        for dc in range(NDC):
                            ins = e.matmul(psA4[i],
                                           wbuf[:, dc * 512 + isub * 128:dc * 512 + (isub + 1) * 128],
                                           xln[:, dc * T + tok0 + tch * 512:dc * T + tok0 + (tch + 1) * 512],
                                           start=(dc == 0), stop=(dc == NDC - 1))
                    return ins
                t_g = P.add("tensor", mmg, inc=("p_tensor", 1))
                P.wait("scalar", "p_tensor", t_g)
                def mksil(e, g=g, I=I):
                    psA4 = psGsets[g]
                    for i in range(4):
                        isub, tch = units[g * 4 + i]
                        hs = slice(isub * 1024 + tch * 512, isub * 1024 + (tch + 1) * 512)
                        if gs is not None:
                            gi = sc_base + I * 4 + isub
                            ins = e.activation(hsil[:, hs], psA4[i], AF.Silu,
                                               scale=gs[:, gi:gi + 1])
                        else:
                            ins = e.activation(hsil[:, hs], psA4[i], AF.Silu)
                    return ins
                sil_g[g] = P.add("scalar", mksil, inc=("p_scalar", 1))
            t_done[s] = P.val("p_tensor")
        elif kind == "u":
            P.wait("tensor", "p_scalar", max(sil_g))
            for g in range(2):
                P.wait("tensor", "p_vector", h_g[g])
                P.wait("tensor", "p_vector", acc_g[1 - g])
                def mmu(e, g=g, wbuf=wbuf):
                    psA4 = psGsets[g]
                    for i in range(4):
                        isub, tch = units[g * 4 + i]
                        for dc in range(NDC):
                            ins = e.matmul(psA4[i],
                                           wbuf[:, dc * 512 + isub * 128:dc * 512 + (isub + 1) * 128],
                                           xln[:, dc * T + tok0 + tch * 512:dc * T + tok0 + (tch + 1) * 512],
                                           start=(dc == 0), stop=(dc == NDC - 1))
                    return ins
                t_u = P.add("tensor", mmu, inc=("p_tensor", 1))
                P.wait("vector", "p_tensor", t_u)
                def mkh(e, g=g, I=I):
                    psA4 = psGsets[g]
                    ins = None
                    for i in range(4):
                        isub, tch = units[g * 4 + i]
                        hs = slice(isub * 1024 + tch * 512, isub * 1024 + (tch + 1) * 512)
                        if us is not None:
                            gi = sc_base + I * 4 + isub
                            ins = e.scalar_tensor_tensor(
                                hT[:, hs], psA4[i], us[:, gi:gi + 1],
                                hsil[:, hs], ALU.mult, ALU.mult)
                        else:
                            ins = e.tensor_mul(hT[:, hs], psA4[i], hsil[:, hs])
                        if comb is not None:
                            ins = e.tensor_mul(hT[:, hs], hT[:, hs],
                                               comb[:, tok0 + tch * 512:tok0 + (tch + 1) * 512])
                    return ins
                h_g[g] = P.add("vector", mkh, inc=("p_vector", 1))
            t_done[s] = P.val("p_tensor")
        else:
            P.wait("tensor", "p_vector", max(h_g))
            dunits = [(ot, tch) for ot in range(16) for tch in range(2)]
            for g in range(8):
                P.wait("tensor", "p_vector", acc_g[g % 2])
                def mmd(e, g=g, wbuf=wbuf):
                    psD4 = psDsets[g % 2]
                    for i in range(4):
                        ot, tch = dunits[g * 4 + i]
                        for isub in range(4):
                            ins = e.matmul(
                                psD4[i],
                                wbuf[:, isub * 2048 + ot * 128:isub * 2048 + (ot + 1) * 128],
                                hT[:, isub * 1024 + tch * 512:isub * 1024 + (tch + 1) * 512],
                                start=(isub == 0), stop=(isub == 3))
                    return ins
                t_d = P.add("tensor", mmd, inc=("p_tensor", 1))
                P.wait("vector", "p_tensor", t_d)
                def mkacc(e, g=g, I=I):
                    psD4 = psDsets[g % 2]
                    ins = None
                    for i in range(4):
                        ot, tch = dunits[g * 4 + i]
                        asl = slice(ot * 1024 + tch * 512, ot * 1024 + (tch + 1) * 512)
                        if first_pass and I == 0:
                            if ds is not None:
                                ins = e.tensor_single_scalar(
                                    acc[:, asl], psD4[i], ds[:, ot:ot + 1], ALU.mult)
                            else:
                                ins = e.tensor_copy(acc[:, asl], psD4[i])
                        else:
                            if ds is not None:
                                ins = e.scalar_tensor_tensor(
                                    acc[:, asl], psD4[i], ds[:, ot:ot + 1],
                                    acc[:, asl], ALU.mult, ALU.add)
                            else:
                                ins = e.tensor_add(acc[:, asl], acc[:, asl], psD4[i])
                    return ins
                acc_g[g % 2] = P.add("vector", mkacc, inc=("p_vector", 1))
            t_done[s] = P.val("p_tensor")
    P.wait("tensor", "p_vector", P.val("p_vector"))
    P.wait("tensor", "p_scalar", P.val("p_scalar"))


def emit_router(P, B, aps):
    nc = P.nc
    xln, stage = B["xln"], B["stage"]
    rwb, comb, crow, crow_bf = B["rwb"], B["comb"], B["crow"], B["crow_bf"]
    combN, t8 = B["combN"], B["t8"]
    ident_f, ones_bf, cpool = B["ident_f"], B["ones_bf"], B["cpool"]
    pspool = B["pspool"]
    psL = pspool[:, 0:512]
    psT = pspool[0:1, 512:640]
    psBC = pspool[:, 1024:1536]

    P.wait("sync", "p_vector", P.val("p_vector"))
    ld = _load(P, stage[:, 0:128].rearrange("p (c e) -> p c e", e=8),
               aps["rw_T"].rearrange("(c p) e -> p c e", p=128))
    P.wait("vector", "ld", ld)
    v_rw = P.add("vector", lambda e: e.tensor_copy(rwb[:], stage[:, 0:128]),
                 inc=("p_vector", 1))
    P.wait("tensor", "p_vector", v_rw)
    v_done = 0
    for tt in range(16):
        P.wait("tensor", "p_vector", v_done)
        def mml(e, tt=tt):
            for dc in range(NDC):
                ins = e.matmul(psL[:, 0:8],
                               xln[:, dc * T + tt * 128:dc * T + tt * 128 + 128],
                               rwb[:, dc * 8:(dc + 1) * 8],
                               start=(dc == 0), stop=(dc == NDC - 1))
            return ins
        t_l = P.add("tensor", mml, inc=("p_tensor", 1))
        P.wait("vector", "p_tensor", t_l)
        P.addw("vector", lambda e: e.reduce_max(t8[:, 0:1], psL[:, 0:8],
                                                axis=AX.X, negate=True))
        P.addw("vector", lambda e: e.tensor_scalar_mul(t8[:, 1:2], t8[:, 0:1], -1.0))
        P.addw("vector", lambda e: e.tensor_single_scalar(
            t8[:, 8:16], psL[:, 0:8], t8[:, 1:2], ALU.is_ge))
        P.addw("vector", lambda e: e.scalar_tensor_tensor(
            t8[:, 16:24], t8[:, 8:16], -30000.0, psL[:, 0:8], ALU.mult, ALU.add))
        v_m2 = P.addw("vector", lambda e: e.reduce_max(
            t8[:, 2:3], t8[:, 16:24], axis=AX.X, negate=True))
        P.wait("scalar", "p_vector", v_m2)
        s_e = P.add("scalar", lambda e: e.activation(
            t8[:, 3:4], t8[:, 2:3], AF.Exp, scale=cpool[:, 4:5], bias=t8[:, 0:1]),
            inc=("p_scalar", 1))
        P.wait("vector", "p_scalar", s_e)
        P.addw("vector", lambda e: e.tensor_scalar_add(t8[:, 4:5], t8[:, 3:4], 1.0))
        P.addw("vector", lambda e: e.reciprocal(t8[:, 5:6], t8[:, 4:5]))
        v_done = P.add("vector", lambda e, tt=tt: e.tensor_mul(
            combN[:, tt:tt + 1], t8[:, 4:5], t8[:, 5:6]), inc=("p_vector", 1))
    s_ev = 0
    for tt in range(16):
        P.wait("tensor", "p_vector", v_done)
        P.wait("tensor", "p_scalar", s_ev)
        t_tr = P.add("tensor", lambda e, tt=tt: e.transpose(
            psT[0:1, 0:128], combN[:, tt:tt + 1], ident_f[:]),
            inc=("p_tensor", 1))
        P.wait("scalar", "p_tensor", t_tr)
        s_ev = P.add("scalar", lambda e, tt=tt: e.copy(
            crow[0:1, tt * 128:(tt + 1) * 128], psT[0:1, 0:128]),
            inc=("p_scalar", 1))
    P.wait("vector", "p_scalar", s_ev)
    v_cb = P.add("vector", lambda e: e.tensor_copy(crow_bf[:], crow[:]),
                 inc=("p_vector", 1))
    P.wait("tensor", "p_vector", v_cb)
    s_ev = 0
    for tc in range(4):
        P.wait("tensor", "p_scalar", s_ev)
        t_bc = P.add("tensor", lambda e, tc=tc: e.matmul(
            psBC, ones_bf[0:1, :], crow_bf[0:1, tc * 512:(tc + 1) * 512],
            start=True, stop=True), inc=("p_tensor", 1))
        P.wait("scalar", "p_tensor", t_bc)
        s_ev = P.add("scalar", lambda e, tc=tc: e.copy(
            comb[:, tc * 512:(tc + 1) * 512], psBC), inc=("p_scalar", 1))


def emit_l0f(P, aps):
    nc = P.nc
    x_in, x_out = aps["x_in"], aps["x_out"]
    with Scope(nc) as S:
        cpool = S.sb("cpool", [128, 8], F32)
        ones_f = S.sb("onesf", [128, 128], F32)
        ones_bf = S.sb("onesbf", [128, 128], BF16)
        ident_f = S.sb("identf", [128, 128], F32)
        st_mean = S.sb("stmean", [128, 512], F32)
        st_msq = S.sb("stmsq", [128, 512], F32)
        st_var = S.sb("stvar", [128, 512], F32)
        big_bf = S.sb("bigbf", [128, NDC * T], BF16)
        acc = S.sb("acc", [128, NDC * 1024], F32)
        comb = S.sb("comb", [128, 2048], BF16)
        pspool = S.ps("pspool", [128, 4096], F32)
        stats = {"mean": st_mean, "msq": st_msq, "var": st_var, "rstd": st_var}
        xln = big_bf

        _load(P, cpool[:], aps["consts"])
        ldc = _load(P, ones_f[:], aps["ones128"])
        ldi = _load(P, ident_f[:], aps["identity"])
        P.wait("vector", "ld", ldi)
        P.add("vector", lambda e: e.tensor_copy(ones_bf[:], ones_f[:]),
              inc=("p_vector", 1))

        with Scope(nc) as S1:
            xf = S1.sb("xf", [128, NDC * 512], F32)
            xb = S1.sb("xb", [128, NDC * 512], BF16)
            xsq = S1.sb("xsq", [128, NDC * 512], BF16)
            emit_ln_in(P, x_in, xln, xf, xb, xsq, ones_bf, stats, cpool,
                       pspool[:, 0:512], pspool[:, 512:1024])
            P.flush()

        with Scope(nc) as S2:
            stage = S2.sb("stage", [128, 4096], F32)
            rwb = S2.sb("rwb", [128, 128], BF16)
            crow = S2.sb("crow", [1, 2048], F32)
            crow_bf = S2.sb("crowbf", [1, 2048], BF16)
            combN = S2.sb("combN", [128, 16], F32)
            t8 = S2.sb("t8", [128, 24], F32)
            emit_router(P, dict(xln=xln, stage=stage, rwb=rwb, comb=comb,
                                crow=crow, crow_bf=crow_bf, combN=combN, t8=t8,
                                ident_f=ident_f, ones_bf=ones_bf, cpool=cpool,
                                pspool=pspool), aps)
            if "dbg_comb" in aps:
                P.wait("gpsimd", "p_scalar", P.val("p_scalar"))
                _store(P, aps["dbg_comb"], comb[:])
                P.wait("sync", "gd", P.val("gd"))
            P.flush()

        for half in range(2):
            soft_barrier(P)
            with Scope(nc) as SA:
                sync_fence(P)
                stA = SA.sb("stA%d" % half, [128, 2048], F32)
                stB = SA.sb("stB%d" % half, [128, 2048], F32)
                wA = SA.sb("wA%d" % half, [128, 8192], BF16)
                wB = SA.sb("wB%d" % half, [128, 8192], BF16)
                hT = SA.sb("hT%d" % half, [128, 4096], BF16)
                hsil = SA.sb("hsil%d" % half, [128, 4096], BF16)
                Bf = dict(xln=xln, acc=acc, stages=[stA, stB], wbufs=[wA, wB],
                          hT=hT, hsil=hsil, pspool=pspool)
                emit_ffn_pass(P, Bf, half, aps["sh_gate_T"], aps["sh_up_T"],
                              aps["sh_down_T"], 16, first_pass=True)
                emit_ffn_pass(P, Bf, half, aps["r_gate_T"], aps["r_up_T"],
                              aps["r_down_T"], 2, first_pass=False, comb=comb)
                if half == 1 and "dbg_acc" in aps:
                    P.wait("gpsimd", "p_vector", P.val("p_vector"))
                    _store(P, aps["dbg_acc"], acc[:])
                    P.wait("sync", "gd", P.val("gd"))
                P.flush()
            soft_barrier(P)
            with Scope(nc) as SB:
                sync_fence(P)
                y_tc = SB.sb("y_tc%d" % half, [128, NDC * 512], BF16)
                xf2 = SB.sb("xf2_%d" % half, [128, 4096], F32)
                xsq_dc = SB.sb("xsqdc%d" % half, [128, 4096], BF16)
                acc3 = acc[:].rearrange("p (o t) -> p o t", t=1024)
                for tcl in range(2):
                    tc = half * 2 + tcl
                    s_cast = P.add("scalar", lambda e, tcl=tcl: e.copy(
                        _v3(y_tc), acc3[:, :, tcl * 512:(tcl + 1) * 512]),
                        inc=("p_scalar", 1))
                    P.wait("tensor", "p_scalar", s_cast)
                    P.wait("sync", "gd", P.val("gd"))
                    st, v_res = emit_out_ln_residual(
                        P, tc, x_in, x_out, y_tc, xf2, xsq_dc, ones_bf,
                        stats, cpool, pspool[:, 3072:3584], pspool[:, 3584:4096],
                        dbg_stats=(2, aps["dbg_mean"], aps["dbg_rstd"])
                        if "dbg_mean" in aps else None)
                    P.wait("scalar", "p_vector", v_res)
                P.flush()
    P.wait("gpsimd", "gd", P.val("gd"))
    P.flush()


L0F_KEYS = ("consts", "ones128", "identity", "rw_T", "sh_gate_T", "sh_up_T",
            "sh_down_T", "r_gate_T", "r_up_T", "r_down_T")


def build_l0f(H):
    nc = bass.Bass("TRN2", target_bir_lowering=False)
    aps = {}
    for name in L0F_KEYS:
        aps[name] = nc.dram_tensor(name, list(H[name].shape), F32,
                                   kind="ExternalInput").ap()
    aps["x_in"] = nc.dram_tensor("x_in", [D, T], F32, kind="ExternalInput").ap()
    aps["x_out"] = nc.dram_tensor("x_out", [D, T], F32, kind="ExternalOutput").ap()
    with nc.Block() as block:
        P = Prog(nc, block)
        emit_l0f(P, aps)
    return nc


def emit_l1f(P, aps):
    nc = P.nc
    x_in, x_out = aps["x_in"], aps["x_out"]
    with Scope(nc) as S:
        cpool = S.sb("cpool", [128, 8], F32)
        ones_f = S.sb("onesf", [128, 128], F32)
        ones_bf = S.sb("onesbf", [128, 128], BF16)
        st_mean = S.sb("stmean", [128, 512], F32)
        st_msq = S.sb("stmsq", [128, 512], F32)
        st_var = S.sb("stvar", [128, 512], F32)
        big_bf = S.sb("bigbf", [128, NDC * T], BF16)
        acc = S.sb("acc", [128, NDC * 1024], F32)
        gsT = S.sb("gsT", [128, 44], F32)
        usT = S.sb("usT", [128, 44], F32)
        dsT = S.sb("dsT", [128, 16], F32)
        pspool = S.ps("pspool", [128, 4096], F32)
        stats = {"mean": st_mean, "msq": st_msq, "var": st_var, "rstd": st_var}
        xln = big_bf

        _load(P, cpool[:], aps["consts"])
        _load(P, ones_f[:], aps["ones128"])
        _load(P, gsT[:], aps["gate_s"])
        _load(P, usT[:], aps["up_s"])
        ldc = _load(P, dsT[:], aps["down_s"])
        P.wait("vector", "ld", ldc)
        P.add("vector", lambda e: e.tensor_copy(ones_bf[:], ones_f[:]),
              inc=("p_vector", 1))

        with Scope(nc) as S1:
            xf = S1.sb("xf", [128, NDC * 512], F32)
            xb = S1.sb("xb", [128, NDC * 512], BF16)
            xsq = S1.sb("xsq", [128, NDC * 512], BF16)
            emit_ln_in(P, x_in, xln, xf, xb, xsq, ones_bf, stats, cpool,
                       pspool[:, 0:512], pspool[:, 512:1024])
            P.flush()

        for half in range(2):
            soft_barrier(P)
            with Scope(nc) as SA:
                sync_fence(P)
                stA = SA.sb("stA%d" % half, [128, 2048], F32)
                stB = SA.sb("stB%d" % half, [128, 2048], F32)
                wA = SA.sb("wA%d" % half, [128, 8192], BF16)
                wB = SA.sb("wB%d" % half, [128, 8192], BF16)
                hT = SA.sb("hT%d" % half, [128, 4096], BF16)
                hsil = SA.sb("hsil%d" % half, [128, 4096], BF16)
                Bf = dict(xln=xln, acc=acc, stages=[stA, stB], wbufs=[wA, wB],
                          hT=hT, hsil=hsil, pspool=pspool)
                emit_ffn_pass(P, Bf, half, aps["f_gate_T"], aps["f_up_T"],
                              aps["f_down_T"], 11, first_pass=True,
                              gs=gsT, us=usT, ds=dsT)
                P.flush()
            soft_barrier(P)
            with Scope(nc) as SB:
                sync_fence(P)
                y_tc = SB.sb("y_tc%d" % half, [128, NDC * 512], BF16)
                xf2 = SB.sb("xf2_%d" % half, [128, 4096], F32)
                xsq_dc = SB.sb("xsqdc%d" % half, [128, 4096], BF16)
                acc3 = acc[:].rearrange("p (o t) -> p o t", t=1024)
                for tcl in range(2):
                    tc = half * 2 + tcl
                    s_cast = P.add("scalar", lambda e, tcl=tcl: e.copy(
                        _v3(y_tc), acc3[:, :, tcl * 512:(tcl + 1) * 512]),
                        inc=("p_scalar", 1))
                    P.wait("tensor", "p_scalar", s_cast)
                    st, v_res = emit_out_ln_residual(
                        P, tc, x_in, x_out, y_tc, xf2, xsq_dc, ones_bf,
                        stats, cpool, pspool[:, 3072:3584], pspool[:, 3584:4096])
                    P.wait("scalar", "p_vector", v_res)
                P.flush()
    P.wait("gpsimd", "gd", P.val("gd"))
    P.flush()


L1F_KEYS = ("consts", "ones128", "f_gate_T", "f_up_T", "f_down_T",
            "gate_s", "up_s", "down_s")


def build_l1f(H):
    nc = bass.Bass("TRN2", target_bir_lowering=False)
    aps = {}
    for name in L1F_KEYS:
        aps[name] = nc.dram_tensor(name, list(H[name].shape), F32,
                                   kind="ExternalInput").ap()
    aps["x_in"] = nc.dram_tensor("x_in", [D, T], F32, kind="ExternalInput").ap()
    aps["x_out"] = nc.dram_tensor("x_out", [D, T], F32, kind="ExternalOutput").ap()
    with nc.Block() as block:
        P = Prog(nc, block)
        emit_l1f(P, aps)
    return nc



def _l1a_projections(P, aps, B):
    nc = P.nc
    xln, tmpT, BC, wdt, seln_bf, stage, pspool = (
        B["xln"], B["tmpT"], B["BC"], B["wdt"], B["seln_bf"], B["stage"], B["pspool"])
    wbc, wdd = B["wbc"], B["wdd"]
    sync_fence(P)
    ld = _load(P, stage[:, 0:512].rearrange("p (c i) -> p c i", i=32),
               aps["WBC"].rearrange("(c p) i -> p c i", p=128))
    P.wait("vector", "ld", ld)
    P.add("vector", lambda e: e.tensor_copy(wbc[:], stage[:, 0:512]),
          inc=("p_vector", 1))
    ld = _load(P, stage[:, 1024:3072].rearrange("p (c i) -> p c i", i=128),
               aps["dt_down"].rearrange("(c p) i -> p c i", p=128))
    P.wait("vector", "ld", ld)
    P.add("vector", lambda e: e.tensor_copy(wdd[:], stage[:, 1024:3072]),
          inc=("p_vector", 1))
    P.wait("sync", "p_vector", P.val("p_vector"))
    ld = _load(P, stage[:, 0:1024], aps["dtw_T"][:, 0:1024])
    ld = _load(P, stage[:, 1024:2048], aps["dtw_T"][:, 1024:2048])
    P.wait("vector", "ld", ld)
    P.add("vector", lambda e: e.tensor_copy(wdt[:], stage[:, 0:2048]),
          inc=("p_vector", 1))
    P.wait("sync", "p_vector", P.val("p_vector"))
    ld = _load(P, stage[0:32, 0:4096], aps["seln"])
    P.wait("vector", "ld", ld)
    v_all = P.add("vector", lambda e: e.tensor_copy(seln_bf[:], stage[0:32, 0:4096]),
                  inc=("p_vector", 1))
    P.wait("tensor", "p_vector", v_all)
    ev = 0
    for tc in range(NTC):
        P.wait("tensor", "p_scalar", ev)
        def mmbc(e, tc=tc):
            for dc in range(NDC):
                ins = e.matmul(pspool[0:32, 0:512],
                               wbc[:, dc * 32:(dc + 1) * 32],
                               xln[:, dc * T + tc * 512:dc * T + (tc + 1) * 512],
                               start=(dc == 0), stop=(dc == NDC - 1))
            return ins
        t_bc = P.add("tensor", mmbc, inc=("p_tensor", 1))
        P.wait("scalar", "p_tensor", t_bc)
        ev = P.add("scalar", lambda e, tc=tc: e.copy(
            BC[:, tc * 512:(tc + 1) * 512], pspool[0:32, 0:512]),
            inc=("p_scalar", 1))
        P.wait("tensor", "p_scalar", ev)
        def mmtmp(e, tc=tc):
            for dc in range(NDC):
                ins = e.matmul(pspool[:, 512:1024],
                               wdd[:, dc * 128:(dc + 1) * 128],
                               xln[:, dc * T + tc * 512:dc * T + (tc + 1) * 512],
                               start=(dc == 0), stop=(dc == NDC - 1))
            return ins
        t_tm = P.add("tensor", mmtmp, inc=("p_tensor", 1))
        P.wait("scalar", "p_tensor", t_tm)
        ev = P.add("scalar", lambda e, tc=tc: e.copy(
            tmpT[:, tc * 512:(tc + 1) * 512], pspool[:, 512:1024]),
            inc=("p_scalar", 1))
    P.wait("tensor", "p_scalar", ev)


def _l1a_scan_tc(P, B, tc):
    nc = P.nc
    xln, tmpT, BC, wdt, seln_bf = B["xln"], B["tmpT"], B["BC"], B["wdt"], B["seln_bf"]
    A_all, hstate, dtb, cpool = B["A_all"], B["hstate"], B["dtb"], B["cpool"]
    y_acc, pspool = B["y_acc"], B["pspool"]
    Bball, Cball = B["Bball"], B["Cball"]
    spa, spb, delta, du = B["spa"], B["spb"], B["delta"], B["du"]
    dAt, ut, ht, yt = B["dAt"], B["ut"], B["ht"], B["yt"]
    psB2 = pspool[:, 0:512]
    psC2 = pspool[:, 512:1024]
    psDt = pspool[:, 1024:1536]
    tsl = slice(tc * 512, (tc + 1) * 512)

    ev = P.val("p_scalar")
    for n in range(16):
        P.wait("tensor", "p_scalar", ev)
        def mmb(e, n=n):
            e.matmul(psB2, seln_bf[:, n * 128:(n + 1) * 128], BC[:, tsl],
                     start=True, stop=True)
            return e.matmul(psC2, seln_bf[:, (16 + n) * 128:(17 + n) * 128],
                            BC[:, tsl], start=True, stop=True)
        t_b = P.add("tensor", mmb, inc=("p_tensor", 1))
        P.wait("scalar", "p_tensor", t_b)
        def evb(e, n=n):
            e.copy(Bball[:, n * 512:(n + 1) * 512], psB2)
            return e.copy(Cball[:, n * 512:(n + 1) * 512], psC2)
        ev = P.add("scalar", evb, inc=("p_scalar", 1))

    for dc in range(16):
        P.wait("tensor", "p_scalar", P.val("p_scalar"))
        P.wait("tensor", "p_vector", P.val("p_vector"))
        P.wait("tensor", "p_gpsimd", P.val("p_gpsimd"))
        t_dt = P.add("tensor", lambda e, dc=dc: e.matmul(
            psDt, wdt[:, dc * 128:(dc + 1) * 128], tmpT[:, tsl],
            start=True, stop=True), inc=("p_tensor", 1))
        P.wait("scalar", "p_tensor", t_dt)
        P.addw("scalar", lambda e, dc=dc: e.activation(
            spa[:], psDt, AF.Abs, bias=dtb[:, dc:dc + 1]))
        P.addw("scalar", lambda e: e.activation(spa[:], spa[:], AF.Exp,
                                                scale=cpool[:, 4:5]))
        P.addw("scalar", lambda e: e.activation(spa[:], spa[:], AF.Ln,
                                                bias=cpool[:, 2:3]))
        s_sp = P.add("scalar", lambda e, dc=dc: e.activation(
            spb[:], psDt, AF.Relu, bias=dtb[:, dc:dc + 1]), inc=("p_scalar", 1))
        P.wait("vector", "p_scalar", s_sp)
        P.addw("vector", lambda e: e.tensor_add(delta[:], spa[:], spb[:]))
        v_du = P.add("vector", lambda e, dc=dc: e.tensor_mul(
            du[:], delta[:], xln[:, dc * T + tc * 512:dc * T + (tc + 1) * 512]),
            inc=("p_vector", 1))
        P.wait("gpsimd", "p_vector", v_du)
        P.wait("scalar", "p_vector", v_du)

        scan_done = [0, 0]
        hcopy_done = [0, 0]
        yC_done = [0, 0]
        for ng in range(8):
            base = ng * 2
            pp = ng % 2
            sl = [2 * pp, 2 * pp + 1]
            P.wait("scalar", "p_vector", scan_done[pp])
            def mkdA(e, base=base, sl=sl, dc=dc):
                for i in range(2):
                    col = dc * 16 + base + i
                    ins = e.activation(dAt[:, sl[i] * 512:(sl[i] + 1) * 512],
                                       delta[:], AF.Exp,
                                       scale=A_all[:, col:col + 1])
                return ins
            s_dA = P.add("scalar", mkdA, inc=("p_scalar", 1))
            P.wait("gpsimd", "p_vector", scan_done[pp])
            def mku(e, base=base, sl=sl):
                for i in range(2):
                    n = base + i
                    ins = e.tensor_mul(ut[:, sl[i] * 512:(sl[i] + 1) * 512], du[:],
                                       Bball[:, n * 512:(n + 1) * 512])
                return ins
            g_u = P.add("gpsimd", mku, inc=("p_gpsimd", 1))
            P.wait("vector", "p_scalar", s_dA)
            P.wait("vector", "p_gpsimd", g_u)
            P.wait("vector", "p_scalar", hcopy_done[pp])
            def mkscan(e, base=base, sl=sl, dc=dc):
                for i in range(2):
                    col = dc * 16 + base + i
                    ins = e.tensor_tensor_scan(
                        ht[:, sl[i] * 512:(sl[i] + 1) * 512],
                        dAt[:, sl[i] * 512:(sl[i] + 1) * 512],
                        ut[:, sl[i] * 512:(sl[i] + 1) * 512],
                        hstate[:, col:col + 1], ALU.mult, ALU.add)
                return ins
            scan_done[pp] = P.add("vector", mkscan, inc=("p_vector", 1))
            P.wait("scalar", "p_vector", scan_done[pp])
            def mkhc(e, base=base, sl=sl, dc=dc):
                for i in range(2):
                    col = dc * 16 + base + i
                    ins = e.copy(hstate[:, col:col + 1],
                                 ht[:, sl[i] * 512 + 511:sl[i] * 512 + 512])
                return ins
            hcopy_done[pp] = P.add("scalar", mkhc, inc=("p_scalar", 1))
            def mky(e, base=base, sl=sl, dc=dc):
                ins = None
                ysl = slice(dc * 512, (dc + 1) * 512)
                for i in range(2):
                    n = base + i
                    if n == 0:
                        ins = e.tensor_mul(y_acc[:, ysl],
                                           ht[:, sl[i] * 512:(sl[i] + 1) * 512],
                                           Cball[:, 0:512])
                    else:
                        ysc = slice(sl[i] * 512, (sl[i] + 1) * 512)
                        e.tensor_mul(yt[:, ysc],
                                     ht[:, sl[i] * 512:(sl[i] + 1) * 512],
                                     Cball[:, n * 512:(n + 1) * 512])
                        ins = e.tensor_add(y_acc[:, ysl], y_acc[:, ysl],
                                           yt[:, ysc])
                return ins
            yC_done[pp] = P.add("vector", mky, inc=("p_vector", 1))


def _l1a_outproj(P, aps, B, tc, x_in, x_out):
    nc = P.nc
    xln, y_acc, pspool = B["xln"], B["y_acc"], B["pspool"]
    stages, wops, y_out = B["stages"], B["wops"], B["y_out"]
    opb, Dv, cpool, ones_bf, stats = B["opb"], B["Dv"], B["cpool"], B["ones_bf"], B["stats"]
    xf2, xsq_dc = B["xf2"], B["xsq_dc"]
    psO = [pspool[:, 2048:2560], pspool[:, 2560:3072]]
    ybf = B["ybf"]
    s_yb = P.add("vector", lambda e: e.tensor_copy(ybf[:], y_acc[:]),
                 inc=("p_vector", 1))
    P.wait("scalar", "p_vector", s_yb)

    slabctx = B.setdefault("oslab", {"i": 0, "guards": [0, 0]})
    mm_done = {-1: P.val("p_tensor"), -2: P.val("p_tensor")}
    cast_done = {}

    def emit_load(p):
        if p >= 8:
            return
        for j in range(4):
            sb = stages[slabctx["i"] % 2]
            P.wait("sync", "p_scalar", slabctx["guards"][slabctx["i"] % 2])
            if j == 0:
                P.wait("sync", "p_tensor", mm_done[p - 2])
            ld = _load(P, sb[:].rearrange("p (c i) -> p c i", i=256),
                       aps["op_T"][512 * j:512 * (j + 1), p * 256:(p + 1) * 256]
                       .rearrange("(c p2) i -> p2 c i", p2=128))
            P.wait("scalar", "ld", ld)
            if j == 0:
                P.wait("scalar", "p_tensor", mm_done[p - 2])
            tgt = P.add("scalar", lambda e, j=j, sb=sb, p=p: e.copy(
                wops[p % 2][:, j * 1024:(j + 1) * 1024], sb[:]),
                inc=("p_scalar", 1))
            slabctx["guards"][slabctx["i"] % 2] = tgt
            slabctx["i"] += 1
        cast_done[p] = tgt

    emit_load(0)
    relu_guard = 0
    ev_prev = [0, 0]
    P.wait("tensor", "p_vector", s_yb)
    for p in range(8):
        emit_load(p + 1)
        P.wait("tensor", "p_scalar", cast_done[p])
        wop = wops[p % 2]
        for ot in range(2):
            oc = 2 * p + ot
            pp = oc % 2
            P.wait("tensor", "p_vector", ev_prev[pp])
            def mko(e, ot=ot, pp=pp, wop=wop):
                for dc in range(NDC):
                    ins = e.matmul(
                        psO[pp],
                        wop[:, dc * 256 + ot * 128:dc * 256 + (ot + 1) * 128],
                        ybf[:, dc * 512:(dc + 1) * 512],
                        start=(dc == 0), stop=(dc == NDC - 1))
                return ins
            t_o = P.add("tensor", mko, inc=("p_tensor", 1))
            P.wait("scalar", "p_tensor", t_o)
            P.wait("scalar", "p_vector", relu_guard)
            s_r = P.add("scalar", lambda e, pp=pp, oc=oc: e.activation(
                B["relu_t"][:, pp * 512:(pp + 1) * 512], psO[pp], AF.Relu,
                bias=opb[:, oc:oc + 1]), inc=("p_scalar", 1))
            P.wait("vector", "p_scalar", s_r)
            ev_prev[pp] = P.add("vector", lambda e, oc=oc, pp=pp:
                                e.scalar_tensor_tensor(
                y_out[:, oc * 512:(oc + 1) * 512],
                xln[:, oc * T + tc * 512:oc * T + (tc + 1) * 512],
                Dv[:, oc:oc + 1], B["relu_t"][:, pp * 512:(pp + 1) * 512],
                ALU.mult, ALU.add), inc=("p_vector", 1))
        mm_done[p] = P.val("p_tensor")
    P.wait("scalar", "p_vector", P.val("p_vector"))
    st, v_y = emit_out_ln_residual(
        P, tc, x_in, x_out, y_out, xf2, xsq_dc, ones_bf,
        stats, cpool, pspool[:, 3072:3584], pspool[:, 3584:4096])
    return v_y


def emit_l1a(P, aps):
    nc = P.nc
    x_in, x_out = aps["x_in"], aps["x_out"]
    with Scope(nc) as S:
        cpool = S.sb("cpool", [128, 8], F32)
        ones_f = S.sb("onesf", [128, 128], F32)
        ones_bf = S.sb("onesbf", [128, 128], BF16)
        st_mean = S.sb("stmean", [128, 512], F32)
        st_msq = S.sb("stmsq", [128, 512], F32)
        st_var = S.sb("stvar", [128, 512], F32)
        big_bf = S.sb("bigbf", [128, NDC * T], BF16)
        y_acc = S.sb("y_acc", [128, 16 * 512], F32)
        tmpT = S.sb("tmpT", [128, T], BF16)
        BC = S.sb("BC", [32, T], BF16)
        wdt = S.sb("wdt", [128, T], BF16)
        seln_bf = S.sb("selnbf", [32, 4096], BF16)
        A_all = S.sb("A_all", [128, 256], F32)
        hstate = S.sb("hstate", [128, 256], F32)
        dtb = S.sb("dtb", [128, 16], F32)
        opb = S.sb("opb", [128, 16], F32)
        Dv = S.sb("Dv", [128, 16], F32)
        pspool = S.ps("pspool", [128, 4096], F32)
        stats = {"mean": st_mean, "msq": st_msq, "var": st_var, "rstd": st_var}
        xln = big_bf

        _load(P, cpool[:], aps["consts"])
        _load(P, ones_f[:], aps["ones128"])
        _load(P, dtb[:], aps["dt_bias"])
        _load(P, opb[:], aps["op_bias"])
        _load(P, Dv[:], aps["Dvec"])
        ldc = _load(P, A_all[:], aps["A_log"])
        P.wait("vector", "ld", ldc)
        P.add("vector", lambda e: e.tensor_copy(ones_bf[:], ones_f[:]),
              inc=("p_vector", 1))
        P.add("vector", lambda e: e.memset(hstate[:], 0.0), inc=("p_vector", 1))
        v_al = P.val("p_vector")
        P.wait("scalar", "p_vector", v_al)
        s_a = P.add("scalar", lambda e: e.activation(A_all[:], A_all[:], AF.Exp),
                    inc=("p_scalar", 1))
        P.wait("vector", "p_scalar", s_a)
        P.addw("vector", lambda e: e.tensor_scalar_mul(A_all[:], A_all[:], -1.0))

        with Scope(nc) as S1:
            xf = S1.sb("xf", [128, NDC * 512], F32)
            xb = S1.sb("xb", [128, NDC * 512], BF16)
            xsq = S1.sb("xsq", [128, NDC * 512], BF16)
            emit_ln_in(P, x_in, xln, xf, xb, xsq, ones_bf, stats, cpool,
                       pspool[:, 0:512], pspool[:, 512:1024])
            P.flush()

        with Scope(nc) as S2:
            stage = S2.sb("stageP", [128, 4096], F32)
            wbc = S2.sb("wbc", [128, 512], BF16)
            wdd = S2.sb("wdd", [128, 2048], BF16)
            _l1a_projections(P, aps, dict(
                xln=xln, tmpT=tmpT, BC=BC, wdt=wdt, seln_bf=seln_bf,
                stage=stage, wbc=wbc, wdd=wdd, pspool=pspool))
            P.flush()

        for tc in range(NTC):
            soft_barrier(P)
            with Scope(nc) as SS:
                Bball = SS.sb("Bball%d" % tc, [128, 16 * 512], BF16)
                Cball = SS.sb("Cball%d" % tc, [128, 16 * 512], BF16)
                spa = SS.sb("spa%d" % tc, [128, 512], F32)
                spb = SS.sb("spb%d" % tc, [128, 512], F32)
                delta = SS.sb("delta%d" % tc, [128, 512], F32)
                du = SS.sb("du%d" % tc, [128, 512], F32)
                dAt = SS.sb("dAt%d" % tc, [128, 2048], F32)
                ut = SS.sb("ut%d" % tc, [128, 2048], F32)
                ht = SS.sb("ht%d" % tc, [128, 2048], BF16)
                yt = SS.sb("yt%d" % tc, [128, 2048], BF16)
                _l1a_scan_tc(P, dict(
                    xln=xln, tmpT=tmpT, BC=BC, wdt=wdt, seln_bf=seln_bf,
                    A_all=A_all, hstate=hstate, dtb=dtb, cpool=cpool,
                    y_acc=y_acc, pspool=pspool, Bball=Bball, Cball=Cball,
                    spa=spa, spb=spb, delta=delta, du=du, dAt=dAt, ut=ut,
                    ht=ht, yt=yt), tc)
                P.flush()
            soft_barrier(P)
            with Scope(nc) as SO:
                sync_fence(P)
                stO = [SO.sb("stOA%d" % tc, [128, 1024], F32),
                       SO.sb("stOB%d" % tc, [128, 1024], F32)]
                wops = [SO.sb("wopA%d" % tc, [128, NDC * 256], BF16),
                        SO.sb("wopB%d" % tc, [128, NDC * 256], BF16)]
                ybf = SO.sb("ybf%d" % tc, [128, 16 * 512], BF16)
                y_out = SO.sb("yout%d" % tc, [128, NDC * 512], BF16)
                relu_t = SO.sb("relu%d" % tc, [128, 1024], BF16)
                xf2 = SO.sb("xf2_%d" % tc, [128, 4096], F32)
                xsq_dc = ybf[:, 0:4096]
                _l1a_outproj(P, aps, dict(
                    xln=xln, y_acc=y_acc, pspool=pspool, stages=stO,
                    wops=wops, ybf=ybf, y_out=y_out, relu_t=relu_t,
                    opb=opb, Dv=Dv, cpool=cpool, ones_bf=ones_bf, stats=stats,
                    xf2=xf2, xsq_dc=xsq_dc), tc, x_in, x_out)
                P.flush()
    P.wait("gpsimd", "gd", P.val("gd"))
    P.flush()


L1A_KEYS = ("consts", "ones128", "WBC", "dt_down", "dtw_T", "dt_bias",
            "A_log", "Dvec", "op_T", "op_bias", "seln")


def build_l1a(H):
    nc = bass.Bass("TRN2", target_bir_lowering=False)
    aps = {}
    for name in L1A_KEYS:
        aps[name] = nc.dram_tensor(name, list(H[name].shape), F32,
                                   kind="ExternalInput").ap()
    aps["x_in"] = nc.dram_tensor("x_in", [D, T], F32, kind="ExternalInput").ap()
    aps["x_out"] = nc.dram_tensor("x_out", [D, T], F32, kind="ExternalOutput").ap()
    with nc.Block() as block:
        P = Prog(nc, block)
        emit_l1a(P, aps)
    return nc


ALL_KEYS = tuple(dict.fromkeys(L0A_KEYS + L0F_KEYS + L1A_KEYS + L1F_KEYS))


def build_fused(H):
    nc = bass.Bass("TRN2", target_bir_lowering=False)
    aps = {}
    for name in ALL_KEYS:
        aps[name] = nc.dram_tensor(name, list(H[name].shape), F32,
                                   kind="ExternalInput").ap()
    x_in = nc.dram_tensor("x_in", [D, T], F32, kind="ExternalInput").ap()
    x_out = nc.dram_tensor("x_out", [D, T], F32, kind="ExternalOutput").ap()
    xs_a = nc.dram_tensor("xs_a", [D, T], F32, kind="Internal").ap()
    xs_b = nc.dram_tensor("xs_b", [D, T], F32, kind="Internal").ap()
    with nc.Block() as block:
        P = Prog(nc, block)
        for emit, xin, xout in ((emit_l0a, x_in, xs_a),
                                (emit_l0f, xs_a, xs_b),
                                (emit_l1a, xs_b, xs_a),
                                (emit_l1f, xs_a, x_out)):
            a = dict(aps)
            a["x_in"], a["x_out"] = xin, xout
            emit(P, a)
            soft_barrier(P)
            sync_fence(P)
        P.wait("gpsimd", "gd", P.val("gd"))
        P.flush()
    return nc


FUSED = True


def kernel(**inputs):
    if FUSED:
        return kernel_fused(**inputs)
    return kernel_unfused(**inputs)


def kernel_fused(**inputs):
    H = _prep_host(inputs)
    x = np.asarray(inputs["L0_attn_x"], np.float32)
    ncores = 8
    nc = build_fused(H)
    common = {k: H[k] for k in ALL_KEYS}
    in_maps = [dict(common, x_in=np.ascontiguousarray(x[b].T))
               for b in range(ncores)]
    res = run_bass_kernel_spmd(nc, in_maps, core_ids=list(range(ncores)))
    out = np.stack([np.asarray(r["x_out"]).T for r in res.results])
    return out.astype(np.float32)
